# version 10
# baseline (speedup 1.0000x reference)
import numpy as np
import ml_dtypes
import concourse.bass as bass
import concourse.mybir as mybir
from concourse.bass_utils import run_bass_kernel_spmd

F32 = mybir.dt.float32
BF16 = mybir.dt.bfloat16
AF = mybir.ActivationFunctionType
ALU = mybir.AluOpType
AX = mybir.AxisListType

NCORE = 8
NTOK = 2048
NT = 16
D = 1024
T = 8192
RS32 = 1.0 / np.sqrt(32.0)
RG = [[0, 1, 2, 3], [4, 5, 6, 7]]


class SP:

    ENGS = ["sync", "scalar", "vector", "tensor", "gpsimd"]

    def __init__(self, nc, sems):
        self.nc = nc
        self.sems = sems
        self.plan = []
        self.cnt = {e: 0 for e in self.ENGS}
        self.ctxc = None

    def now(self):
        return dict(self.cnt)

    def ctx_open(self, n):
        self.ctxc = [dict(self.cnt) for _ in range(n)]

    def ctx_close(self):
        self.ctxc = None

    def op(self, eng, fn, inc=1, deps=None, ctx=None):
        if ctx is not None:
            base = list(self.ctxc[ctx].items())
            deps = base if deps is None else list(deps) + base
        if deps is None:
            deps = [(F, self.cnt[F]) for F in self.ENGS
                    if self.cnt[F] > 0 and not (F == eng == "tensor")]
        else:
            dd = {}
            for (F, c) in deps:
                if c > 0 and not (F == eng == "tensor"):
                    dd[F] = max(dd.get(F, 0), c)
            deps = list(dd.items())
        self.plan.append((eng, fn, deps, inc))
        self.cnt[eng] += inc
        if ctx is not None:
            self.ctxc[ctx][eng] = self.cnt[eng]
        return (eng, self.cnt[eng])

    def dma(self, out, in_, eng="sync", deps=None, ctx=None, **kw):
        return self.op(eng, lambda e: e.dma_start(out, in_, **kw), inc=16, deps=deps, ctx=ctx)

    def emit(self, block):
        for eng_name in self.ENGS:
            items = [(f, d, i) for (e, f, d, i) in self.plan if e == eng_name]
            if not items:
                continue

            def make(items, eng_name):
                def body(eng):
                    g = {}
                    for fn, deps, inc in items:
                        for (F, c) in deps:
                            if g.get(F, 0) < c:
                                eng.wait_ge(self.sems[F], c)
                                g[F] = c
                        fn(eng).then_inc(self.sems[eng_name], inc)
                return body

            getattr(block, eng_name)(make(items, eng_name))
        fin_counts = dict(self.cnt)

        def fin(eng):
            for F in self.ENGS:
                if fin_counts[F] > 0 and F != "scalar":
                    eng.wait_ge(self.sems[F], fin_counts[F])

        block.scalar(fin)


class Arena:

    def __init__(self, nc, n_bf16):
        self.h = nc.alloc_sbuf_tensor("arena", [128, n_bf16], BF16)
        self.n = n_bf16
        self.off = 0

    def reset(self):
        self.off = 0

    def _carve(self, units):
        o = self.off
        assert o + units <= self.n, f"arena overflow: {o}+{units} > {self.n}"
        self.off = o + units
        return o

    @staticmethod
    def _shape(ap, shape):
        if len(shape) == 3:
            ap = ap.rearrange("p (a b) -> p a b", b=shape[2])
        elif len(shape) == 4:
            ap = ap.rearrange("p (a b c) -> p a b c", b=shape[2], c=shape[3])
        if shape[0] != 128:
            ap = ap[0:shape[0]]
        return ap

    def bf16(self, shape):
        free = int(np.prod(shape[1:]))
        o = self._carve((free + 15) // 16 * 16)
        return self._shape(self.h.ap()[:, o:o + free], shape)

    def f32(self, shape):
        free = int(np.prod(shape[1:]))
        o = self._carve((2 * free + 15) // 16 * 16)
        return self._shape(self.h.ap()[:, o:o + 2 * free].bitcast(F32), shape)


def emit_ln(sp, x_ap, xc, sq, stat, Dd, eps=1e-5, c0=0, ctx=None):
    sp.op("vector", lambda e: e.reduce_sum(stat[:, c0:c0 + 1], x_ap, axis=AX.X), ctx=ctx)
    sp.op("vector", lambda e: e.tensor_scalar_mul(stat[:, c0 + 1:c0 + 2], stat[:, c0:c0 + 1], 1.0 / Dd), ctx=ctx)
    sp.op("vector", lambda e: e.tensor_scalar(xc, x_ap, stat[:, c0 + 1:c0 + 2], None, ALU.subtract), ctx=ctx)
    sp.op("scalar", lambda e: e.activation(sq, xc, AF.Square, accum_out=stat[:, c0 + 2:c0 + 3]), ctx=ctx)
    sp.op("vector", lambda e: e.tensor_scalar(stat[:, c0 + 3:c0 + 4], stat[:, c0 + 2:c0 + 3], 1.0 / Dd, eps, ALU.mult, ALU.add), ctx=ctx)
    sp.op("scalar", lambda e: e.activation(stat[:, c0 + 3:c0 + 4], stat[:, c0 + 3:c0 + 4], AF.Sqrt), ctx=ctx)
    sp.op("vector", lambda e: e.reciprocal(stat[:, c0 + 3:c0 + 4], stat[:, c0 + 3:c0 + 4]), ctx=ctx)


def emit_transposes(sp, dst_bf, src_f32, idn, ps, nchunk, ctx=None):
    for c in range(nchunk):
        def tr(c=c):
            return lambda e: e.transpose(ps[:, 0:128], src_f32[:, 128 * c:128 * (c + 1)], idn)
        sp.op("tensor", tr(c), ctx=ctx)

        def cp(c=c):
            return lambda e: e.tensor_copy(dst_bf[:, c, :], ps[:, 0:128])
        sp.op("vector", cp(c), ctx=ctx)


def emit_bcast(sp, dst, src_row, ones1, ps, N):
    nslab = (N + 511) // 512
    for s in range(nslab):
        lo, hi = 512 * s, min(512 * (s + 1), N)

        def mm(lo=lo, hi=hi):
            return lambda e: e.matmul(ps[:, lo:hi], ones1[:], src_row[0:1, lo:hi], start=True, stop=True)
        sp.op("tensor", mm(lo, hi))
    sp.op("vector", lambda e: e.tensor_copy(dst, ps[:, 0:N]))


def emit_load_w(sp, dst_bf, w_dram, din, dout, stage, stage_cols):
    nch = din // 128
    wr = w_dram.rearrange("(c p) n -> p c n", p=128)
    per = max(1, stage_cols // dout)
    c = 0
    while c < nch:
        cc = min(per, nch - c)

        def d(c=c, cc=cc):
            return lambda e: e.dma_start(stage[:, 0:cc * dout], wr[:, c:c + cc, :])
        sp.op("sync", d(c, cc), inc=16)

        def cp(c=c, cc=cc):
            return lambda e: e.tensor_copy(dst_bf[:, c:c + cc, :], stage[:, 0:cc * dout].rearrange("p (c n) -> p c n", n=dout))
        sp.op("vector", cp(c, cc))
        c += cc


def emit_proj_tm(sp, ps_view, lhsT_bf, rhs_w_bf, nk, dout, ctx=None, coff=0):
    nslab = (dout + 511) // 512
    for s in range(nslab):
        lo = 512 * s
        hi = min(lo + 512, dout)
        for k in range(nk):
            def mm(k=k, lo=lo, hi=hi):
                return lambda e: e.matmul(
                    ps_view[:, lo:hi], lhsT_bf[:, k, :], rhs_w_bf[:, k, coff + lo:coff + hi],
                    start=(k == 0), stop=(k == nk - 1))
            sp.op("tensor", mm(k, lo, hi), ctx=ctx)


def build_fused(debug=False):
    nc = bass.Bass("TRN2", target_bir_lowering=False)

    def inp(name, shape):
        return nc.dram_tensor(name, shape, F32, kind="ExternalInput").ap()

    x = inp("x", [NTOK, D])
    wq = inp("wq", [D, 1024]); wk = inp("wk", [D, 1024])
    wv = inp("wv", [D, 2048]); wg = inp("wg", [D, 2048])
    gkw1 = inp("gkw1", [D, 16]); gkw2 = inp("gkw2", [16, 1024]); gkb = inp("gkb", [D])
    rmsw = inp("rmsw", [512])
    wo = inp("wo", [2048, 1024])
    qwe = inp("qwe", [D, 512]); qbe = inp("qbe", [512])
    kwf = inp("kwf", [D, 512]); kbf = inp("kbf", [512])
    vwf = inp("vwf", [D, 512]); vbf = inp("vbf", [512])
    owe = inp("owe", [512, 1024]); obv = inp("obv", [1024])
    rw = inp("rw", [D, 8]); rb = inp("rb", [8])
    w1a = inp("w1a", [8, 1024, 2048]); b1fma = inp("b1fma", [8, 128, 16])
    w2a = inp("w2a", [8, 2048, 1024]); b2a = inp("b2a", [8, 1024])
    ln1w = inp("ln1w", [D]); ln1b = inp("ln1b", [D])
    wq2 = inp("wq2", [D, D]); wk2 = inp("wk2", [D, D])
    wv2 = inp("wv2", [D, D]); wg2 = inp("wg2", [D, D])
    wo2 = inp("wo2", [D, D])
    ln2w = inp("ln2w", [D]); ln2b = inp("ln2b", [D])
    wina = inp("wina", [D, 2816]); winb = inp("winb", [D, 2816])
    binaf = inp("binaf", [128, 22]); binbf = inp("binbf", [128, 22])
    wout = inp("wout", [2816, 1024]); bout = inp("bout", [1024])
    wr9 = inp("wr9", [D, 8]); br9 = inp("br9", [8])
    lng = inp("lng", [D]); lnb9 = inp("lnb9", [D])
    w1b = inp("w1b", [8, 1024, 4096]); b1fmb = inp("b1fmb", [8, 128, 32])
    w2b = inp("w2b", [8, 4096, 1024]); b2b9 = inp("b2b9", [8, 1024])
    idn = inp("idn", [128, 128])
    triu = inp("triu", [128, 128])
    i16 = inp("i16", [16, 16])
    rgq = inp("rgq", [128, 16])
    rgk2 = inp("rgk2", [128, 16])
    rgkh = inp("rgkh", [128, 16])
    rgs = inp("rgs", [1, 16])
    rgcc = inp("rgcc", [64, 256])
    rccoef = inp("rccoef", [64, 64])
    gmsel = inp("gmsel", [128, 16])
    gsel = inp("gsel", [128, 4])

    ox4 = nc.dram_tensor("ox4", [NTOK, D], F32, kind="ExternalOutput").ap()

    kind_s = "ExternalOutput" if debug else "Internal"

    def scr(name, shape, dtype=F32):
        return nc.dram_tensor(name, shape, dtype, kind=kind_s).ap()

    q_s = scr("q_s", [NTOK, 1024])
    k_s = scr("k_s", [NTOK, 1024])
    v_s = scr("v_s", [NTOK, 2048])
    g_s = scr("g_s", [NTOK, 1024])
    gate_s = scr("gate_s", [NTOK, 2048])
    oraw_s = scr("oraw_s", [NTOK, 2048])
    qgT_s = scr("qgT_s", [1024, NTOK], BF16)
    on_s = scr("on_s", [NTOK, 2048])
    x1_s = scr("x1_s", [NTOK, D])
    xn1_s = scr("xn1_s", [NTOK, D])
    Qe_s = scr("Qe_s", [NTOK, 512])
    mla_s = scr("mla_s", [NTOK, 1024])
    comb_s = scr("comb_s", [NTOK, 8])
    x2_s = scr("x2_s", [NTOK, D])
    xn2_s = scr("xn2_s", [NTOK, D])
    q2_s = scr("q2_s", [NTOK, D])
    k2_s = scr("k2_s", [NTOK, D])
    v2_s = scr("v2_s", [NTOK, D])
    gate2_s = scr("gate2_s", [NTOK, D])
    o2raw_s = scr("o2raw_s", [NTOK, D])
    q2f_s = scr("q2f_s", [1024, NTOK], BF16)
    ret_s = scr("ret_s", [NTOK, D])
    x3_s = scr("x3_s", [NTOK, D])
    xn3_s = scr("xn3_s", [NTOK, D])
    accd = nc.dram_tensor("accd", [NTOK, D], F32, kind="Internal").ap()
    xl_s = nc.dram_tensor("xl_s", [1024, NTOK], BF16, kind="Internal").ap()

    cc_gs_in = nc.dram_tensor("cc_gs_in", [1026, 512], F32, kind="Internal").ap()
    cc_gs_out = nc.dram_tensor("cc_gs_out", [4104, 512], F32, kind="Internal").ap()
    cc_kv_in = nc.dram_tensor("cc_kv_in", [NTOK, 1024], BF16, kind="Internal").ap()
    cc_kv_out = nc.dram_tensor("cc_kv_out", [T, 1024], BF16, kind="Internal").ap()
    cc_rs_in = nc.dram_tensor("cc_rs_in", [1024, 64], F32, kind="Internal").ap()
    cc_rs_out = nc.dram_tensor("cc_rs_out", [4096, 64], F32, kind="Internal").ap()

    A = Arena(nc, 105088)
    idn_sb = nc.alloc_sbuf_tensor("idn_p", [128, 128], F32).ap()
    triu_sb = nc.alloc_sbuf_tensor("triu_p", [128, 128], F32).ap()
    ones1 = nc.alloc_sbuf_tensor("ones1_p", [1, 128], F32).ap()
    ones16 = nc.alloc_sbuf_tensor("ones16_p", [16, 128], F32).ap()
    i16_sb = nc.alloc_sbuf_tensor("i16_p", [16, 16], F32).ap()
    statp = nc.alloc_sbuf_tensor("stat_p", [128, 64], F32).ap()

    with nc.psum_tensor("psA", [128, 2048], F32) as psA, \
         nc.psum_tensor("psB", [128, 2048], F32) as psB, \
         nc.semaphore("S0") as S0, nc.semaphore("S1") as S1, \
         nc.semaphore("S2") as S2, nc.semaphore("S3") as S3, \
         nc.semaphore("S4") as S4, nc.Block() as block:
        sems = {"sync": S0, "scalar": S1, "vector": S2, "tensor": S3, "gpsimd": S4}
        sp = SP(nc, sems)
        stat = statp
        sp.op("vector", lambda e: e.memset(ones1, 1.0))
        sp.op("vector", lambda e: e.memset(ones16, 1.0))
        sp.dma(idn_sb, idn)
        sp.dma(triu_sb, triu)
        sp.dma(i16_sb, i16)

        def ph1():
            A.reset()
            wq_bf = A.bf16([128, 8, 1024]); wk_bf = A.bf16([128, 8, 1024])
            wv_bf = A.bf16([128, 8, 2048]); wg_bf = A.bf16([128, 8, 2048])
            gkw1_bf = A.bf16([128, 8, 16]); gkw2_bf = A.bf16([16, 1024])
            stage = A.f32([128, 8192])
            gkb_row = A.f32([1, 1024]); gkb_b = A.f32([128, 1024])
            xt = A.f32([128, 1024]); xc = A.f32([128, 1024]); sq = A.f32([128, 1024])
            xn32 = A.f32([128, 1024]); xnT = A.bf16([128, 8, 128])
            t1_bf = A.bf16([16, 128])
            gl = A.f32([128, 1024]); gw = A.f32([128, 1024]); ob = A.f32([128, 2048])

            emit_load_w(sp, wq_bf, wq, D, 1024, stage, 8192)
            emit_load_w(sp, wk_bf, wk, D, 1024, stage, 8192)
            emit_load_w(sp, wv_bf, wv, D, 2048, stage, 8192)
            emit_load_w(sp, wg_bf, wg, D, 2048, stage, 8192)
            emit_load_w(sp, gkw1_bf, gkw1, D, 16, stage, 8192)
            sp.dma(stage[0:16, 0:1024], gkw2)
            sp.op("vector", lambda e: e.tensor_copy(gkw2_bf, stage[0:16, 0:1024]))
            sp.dma(gkb_row[0:1, :], gkb[None, :])
            emit_bcast(sp, gkb_b, gkb_row, ones1, psA, 1024)

            for i in range(NT):
                r0 = 128 * i
                sp.dma(xt, x[r0:r0 + 128, :])
                emit_ln(sp, xt, xc, sq, stat, D)
                sp.op("vector", lambda e: e.tensor_scalar(xn32, xc, stat[:, 3:4], None, ALU.mult))
                emit_transposes(sp, xnT, xn32, idn_sb, psA, 8)
                emit_proj_tm(sp, psB, xnT, wq_bf, 8, 1024)
                sp.op("scalar", lambda e: e.activation(ob[:, 0:1024], psB[:, 0:1024], AF.Copy, scale=1.0 / 16.0))
                sp.dma(q_s[r0:r0 + 128, :], ob[:, 0:1024])
                emit_proj_tm(sp, psB, xnT, wk_bf, 8, 1024)
                sp.op("scalar", lambda e: e.copy(ob[:, 0:1024], psB[:, 0:1024]))
                sp.dma(k_s[r0:r0 + 128, :], ob[:, 0:1024])
                emit_proj_tm(sp, psB, xnT, wv_bf, 8, 2048)
                sp.op("scalar", lambda e: e.copy(ob, psB[:]))
                sp.dma(v_s[r0:r0 + 128, :], ob)
                emit_proj_tm(sp, psB, xnT, wg_bf, 8, 2048)
                sp.op("scalar", lambda e: e.activation(ob, psB[:], AF.Silu))
                sp.dma(gate_s[r0:r0 + 128, :], ob)
                for k in range(8):
                    def mm(k=k):
                        return lambda e: e.matmul(psA[0:16, 0:128], gkw1_bf[:, k, :], xnT[:, k, :], start=(k == 0), stop=(k == 7))
                    sp.op("tensor", mm(k))
                sp.op("vector", lambda e: e.tensor_copy(t1_bf, psA[0:16, 0:128]))
                for s in range(2):
                    def mm2(s=s):
                        return lambda e: e.matmul(psA[:, 512 * s:512 * (s + 1)], t1_bf, gkw2_bf[:, 512 * s:512 * (s + 1)], start=True, stop=True)
                    sp.op("tensor", mm2(s))
                sp.op("vector", lambda e: e.tensor_add(gl, psA[:, 0:1024], gkb_b))
                sp.op("scalar", lambda e: e.activation(gw, gl, AF.Abs))
                sp.op("scalar", lambda e: e.activation(gw, gw, AF.Exp, scale=-1.0))
                sp.op("scalar", lambda e: e.activation(gw, gw, AF.Ln, bias=1.0))
                sp.op("vector", lambda e: e.tensor_scalar_min(xc, gl, 0.0))
                sp.op("vector", lambda e: e.tensor_sub(gw, xc, gw))
                sp.op("scalar", lambda e: e.activation(gw, gw, AF.Copy, scale=1.0 / 16.0))
                sp.dma(g_s[r0:r0 + 128, :], gw)

        ph1()

        def ph2():
            A.reset()
            qtm32 = [[A.f32([128, 256]) for _ in range(2)] for _ in range(4)]
            ktm32 = [[A.f32([128, 256]) for _ in range(2)] for _ in range(4)]
            vtm32 = [[A.f32([128, 512]) for _ in range(2)] for _ in range(4)]
            gtm32 = [[A.f32([128, 256]) for _ in range(2)] for _ in range(4)]
            ePfm = [A.f32([128, 256]) for _ in range(4)]
            eMPfm = [A.f32([128, 256]) for _ in range(4)]
            eMPtm = [A.f32([128, 256]) for _ in range(4)]
            qt_bf = [A.bf16([128, 2, 128]) for _ in range(4)]
            kt_bf = [A.bf16([128, 2, 128]) for _ in range(4)]
            ktm_bf = [A.bf16([128, 256]) for _ in range(4)]
            v_bf = [A.bf16([128, 512]) for _ in range(4)]
            at_bf = [A.bf16([128, 128]) for _ in range(4)]
            Ss = [[A.f32([128, 512]) for _ in range(2)] for _ in range(4)]
            Sbs = [[A.bf16([128, 512]) for _ in range(2)] for _ in range(4)]
            D_sb = [A.f32([128, 2]) for _ in range(4)]
            Prun = [A.f32([128, 2]) for _ in range(4)]
            pcum = A.f32([128, 4, 32])
            osb = [A.f32([128, 512]) for _ in range(4)]
            sp.ctx_open(4)
            for h in range(4):
                def z0(h=h):
                    return lambda e: e.memset(Ss[h][0], 0.0)
                sp.op("vector", z0(), ctx=h)

                def z1(h=h):
                    return lambda e: e.memset(Ss[h][1], 0.0)
                sp.op("vector", z1(), ctx=h)

                def zp(h=h):
                    return lambda e: e.memset(Prun[h], 0.0)
                sp.op("vector", zp(), ctx=h)
            def ld2(h, t):
                qm = "sync" if h % 2 == 0 else "gpsimd"
                r0_ = 128 * t
                b = t % 2

                def dm(src, dst, lo, w, r0=r0_):
                    return lambda e: e.dma_start(dst, src[r0:r0 + 128, lo:lo + w])
                sp.op(qm, dm(g_s, gtm32[h][b], 256 * h, 256), inc=16, ctx=h)
                sp.op(qm, dm(k_s, ktm32[h][b], 256 * h, 256), inc=16, ctx=h)
                sp.op(qm, dm(v_s, vtm32[h][b], 512 * h, 512), inc=16, ctx=h)
                sp.op(qm, dm(q_s, qtm32[h][b], 256 * h, 256), inc=16, ctx=h)

            for c in range(NT):
                r0 = 128 * c
                for h in range(4):
                    qs = "gpsimd" if h % 2 == 0 else "sync"
                    psa_h = psA[:, 512 * h:512 * h + 512]
                    psb_h = psB[:, 512 * h:512 * h + 512]
                    if c == 0:
                        ld2(h, 0)
                    if c < NT - 1:
                        ld2(h, c + 1)
                    b = c % 2
                    def pc(h=h, c=c):
                        return lambda e: e.tensor_copy(pcum[:, h, 2 * c:2 * c + 2], Prun[h])
                    sp.op("vector", pc(), ctx=h)
                    for c2 in range(2):
                        def mmp(c2=c2, h=h, psa_h=psa_h, b=b):
                            return lambda e: e.matmul(psa_h[:, 128 * c2:128 * (c2 + 1)], gtm32[h][b][:, 128 * c2:128 * (c2 + 1)], triu_sb, start=True, stop=True)
                        sp.op("tensor", mmp(), ctx=h)

                    def ep1(h=h, psa_h=psa_h):
                        return lambda e: e.activation(ePfm[h], psa_h[:, 0:256], AF.Exp)
                    sp.op("scalar", ep1(), ctx=h)

                    def ep2(h=h, psa_h=psa_h):
                        return lambda e: e.activation(eMPfm[h], psa_h[:, 0:256], AF.Exp, scale=-1.0)
                    sp.op("scalar", ep2(), ctx=h)

                    def ep3(h=h, psa_h=psa_h):
                        return lambda e: e.activation(D_sb[h], psa_h[:, 127:256:128], AF.Exp)
                    sp.op("scalar", ep3(), ctx=h)

                    def pra(h=h, psa_h=psa_h):
                        return lambda e: e.tensor_add(Prun[h], Prun[h], psa_h[:, 127:256:128])
                    sp.op("vector", pra(), ctx=h)
                    def mmt(h=h, psa_h=psa_h, b=b):
                        return lambda e: e.matmul(psa_h[:, 256:512], triu_sb, gtm32[h][b], start=True, stop=True)
                    sp.op("tensor", mmt(), ctx=h)

                    def ep4(h=h, psa_h=psa_h):
                        return lambda e: e.activation(eMPtm[h], psa_h[:, 256:512], AF.Exp, scale=-1.0)
                    sp.op("scalar", ep4(), ctx=h)
                    for c2 in range(2):
                        def trq(c2=c2, h=h, psb_h=psb_h, b=b):
                            return lambda e: e.transpose(psb_h[:, 128 * c2:128 * (c2 + 1)], qtm32[h][b][:, 128 * c2:128 * (c2 + 1)], idn_sb)
                        sp.op("tensor", trq(), ctx=h)

                        def mq(c2=c2, h=h, psb_h=psb_h):
                            return lambda e: e.tensor_tensor(qt_bf[h][:, c2, :], psb_h[:, 128 * c2:128 * (c2 + 1)], ePfm[h][:, 128 * c2:128 * (c2 + 1)], ALU.mult)
                        sp.op("vector", mq(), ctx=h)

                        def trk(c2=c2, h=h, psb_h=psb_h, b=b):
                            return lambda e: e.transpose(psb_h[:, 256 + 128 * c2:256 + 128 * (c2 + 1)], ktm32[h][b][:, 128 * c2:128 * (c2 + 1)], idn_sb)
                        sp.op("tensor", trk(), ctx=h)

                        def mk(c2=c2, h=h, psb_h=psb_h):
                            return lambda e: e.tensor_tensor(kt_bf[h][:, c2, :], psb_h[:, 256 + 128 * c2:256 + 128 * (c2 + 1)], eMPfm[h][:, 128 * c2:128 * (c2 + 1)], ALU.mult)
                        sp.op("vector", mk(), ctx=h)

                    def ktb(h=h, b=b):
                        return lambda e: e.tensor_mul(ktm_bf[h], ktm32[h][b], eMPtm[h])
                    sp.op("vector", ktb(), ctx=h)

                    def vcp2(h=h, b=b):
                        return lambda e: e.tensor_copy(v_bf[h], vtm32[h][b])
                    sp.op("vector", vcp2(), ctx=h)

                    def sb0(h=h):
                        return lambda e: e.tensor_copy(Sbs[h][0], Ss[h][0])
                    sp.op("vector", sb0(), ctx=h)

                    def sb1(h=h):
                        return lambda e: e.tensor_copy(Sbs[h][1], Ss[h][1])
                    sp.op("vector", sb1(), ctx=h)
                    for c2 in range(2):
                        def sq_(h=h, c2=c2, r0=r0):
                            return lambda e: e.dma_start(qgT_s[256 * h + 128 * c2:256 * h + 128 * (c2 + 1), r0:r0 + 128], qt_bf[h][:, c2, :])
                        sp.op(qs, sq_(), inc=16, ctx=h)
                    for c2 in range(2):
                        def mma(c2=c2, h=h, psb_h=psb_h):
                            return lambda e: e.matmul(psb_h[:, 0:128], kt_bf[h][:, c2, :], qt_bf[h][:, c2, :], start=(c2 == 0), stop=(c2 == 1))
                        sp.op("tensor", mma(), ctx=h)

                    def atc(h=h, psb_h=psb_h):
                        return lambda e: e.tensor_tensor(at_bf[h], psb_h[:, 0:128], triu_sb, ALU.mult)
                    sp.op("vector", atc(), ctx=h)
                    def mo1g(h=h, psa_h=psa_h):
                        return lambda e: e.matmul(psa_h[:, 0:512], at_bf[h], v_bf[h], start=True, stop=False)
                    sp.op("tensor", mo1g(), ctx=h)
                    for c2 in range(2):
                        def mmo(c2=c2, h=h, psa_h=psa_h):
                            return lambda e: e.matmul(psa_h[:, 0:512], qt_bf[h][:, c2, :], Sbs[h][c2], start=False, stop=(c2 == 1))
                        sp.op("tensor", mmo(), ctx=h)
                    def ocp2(h=h, psa_h=psa_h):
                        return lambda e: e.tensor_copy(osb[h], psa_h[:, 0:512])
                    sp.op("vector", ocp2(), ctx=h)

                    def do_(h=h, r0=r0):
                        return lambda e: e.dma_start(oraw_s[r0:r0 + 128, 512 * h:512 * (h + 1)], osb[h])
                    sp.op(qs, do_(), inc=16, ctx=h)
                    for c2 in range(2):
                        def mmd(c2=c2, h=h, psb_h=psb_h):
                            return lambda e: e.matmul(psb_h[:, 0:512], ktm_bf[h][:, 128 * c2:128 * (c2 + 1)], v_bf[h], start=True, stop=True)
                        sp.op("tensor", mmd(), ctx=h)

                        def upd2(c2=c2, h=h, psb_h=psb_h):
                            return lambda e: e.tensor_add(Ss[h][c2], Ss[h][c2], psb_h[:, 0:512])
                        sp.op("vector", upd2(), ctx=h)

                        def upd(c2=c2, h=h):
                            return lambda e: e.tensor_scalar(Ss[h][c2], Ss[h][c2], D_sb[h][:, c2:c2 + 1], None, ALU.mult)
                        sp.op("vector", upd(), ctx=h)
            for h in range(4):
                qm = "sync" if h % 2 == 0 else "gpsimd"
                for c2 in range(2):
                    def ds_(h=h, c2=c2):
                        return lambda e: e.dma_start(cc_gs_in[256 * h + 128 * c2:256 * h + 128 * (c2 + 1), :], Ss[h][c2])
                    sp.op(qm, ds_(), inc=16, ctx=h)

                    def dp_(h=h, c2=c2):
                        return lambda e: e.dma_start(cc_gs_in[1024 + h // 2, (h % 2) * 256 + 128 * c2:(h % 2) * 256 + 128 * (c2 + 1)], Prun[h][:, c2:c2 + 1])
                    sp.op(qm, dp_(), inc=16, ctx=h)
            sp.ctx_close()

            sp.op("gpsimd", lambda e: e.collective_compute(
                "AllGather", ALU.bypass, replica_groups=RG,
                ins=[cc_gs_in.opt()],
                outs=[cc_gs_out.opt()]), inc=1)

            Pl = [A.f32([128, 4, 2]) for _ in range(4)]
            coef = [A.f32([128, 4, 2]) for _ in range(4)]
            Sins = [[A.f32([128, 512]) for _ in range(2)] for _ in range(4)]
            Sib2 = [[A.bf16([128, 512]) for _ in range(2)] for _ in range(4)]
            eP2 = [A.f32([128, 2]) for _ in range(4)]
            qld = [[A.bf16([128, 2, 128]) for _ in range(2)] for _ in range(4)]
            orawt = [[A.f32([128, 512]) for _ in range(2)] for _ in range(4)]
            osb2 = [A.f32([128, 512]) for _ in range(4)]
            gat = [A.f32([128, 512]) for _ in range(4)]
            sq5 = [A.f32([128, 512]) for _ in range(4)]
            msel_sb = A.f32([128, 16])
            sel_sb = A.f32([128, 4])
            rms_row = A.f32([1, 512]); rms_b = A.f32([128, 512])
            sp.dma(msel_sb, gmsel)
            sp.dma(sel_sb, gsel)
            sp.dma(rms_row[0:1, :], rmsw[None, :])
            emit_bcast(sp, rms_b, rms_row, ones1, psA, 512)
            qgT_r = qgT_s.rearrange("(a p) t -> p a t", p=128)
            sp.ctx_open(4)
            for h in range(4):
                qm = "sync" if h % 2 == 0 else "gpsimd"
                for l in range(4):
                    for c2 in range(2):
                        def dpl(l=l, h=h, c2=c2):
                            return lambda e: e.dma_start(Pl[h][:, l, c2:c2 + 1], cc_gs_out[1026 * l + 1024 + h // 2, (h % 2) * 256 + 128 * c2:(h % 2) * 256 + 128 * (c2 + 1)])
                        sp.op(qm, dpl(), inc=16, ctx=h)
                for j in range(4):
                    def z_(j=j, h=h):
                        return lambda e: e.memset(coef[h][:, j, :], 0.0)
                    sp.op("vector", z_(), ctx=h)
                    for l in range(4):
                        def ac(j=j, l=l, h=h):
                            return lambda e: e.scalar_tensor_tensor(coef[h][:, j, :], Pl[h][:, l, :], msel_sb[:, 4 * j + l:4 * j + l + 1], coef[h][:, j, :], ALU.mult, ALU.add)
                        sp.op("vector", ac(), ctx=h)

                    def ex_(j=j, h=h):
                        return lambda e: e.activation(coef[h][:, j, :], coef[h][:, j, :], AF.Exp)
                    sp.op("scalar", ex_(), ctx=h)

                    def se_(j=j, h=h):
                        return lambda e: e.tensor_scalar(coef[h][:, j, :], coef[h][:, j, :], sel_sb[:, j:j + 1], None, ALU.mult)
                    sp.op("vector", se_(), ctx=h)
                def zi0(h=h):
                    return lambda e: e.memset(Sins[h][0], 0.0)
                sp.op("vector", zi0(), ctx=h)

                def zi1(h=h):
                    return lambda e: e.memset(Sins[h][1], 0.0)
                sp.op("vector", zi1(), ctx=h)
                for j in range(4):
                    for c2 in range(2):
                        def dsj(j=j, h=h, c2=c2):
                            return lambda e: e.dma_start(orawt[h][0], cc_gs_out[1026 * j + 256 * h + 128 * c2:1026 * j + 256 * h + 128 * (c2 + 1), :])
                        sp.op(qm, dsj(), inc=16, ctx=h)

                        def acs(j=j, c2=c2, h=h):
                            return lambda e: e.scalar_tensor_tensor(Sins[h][c2], orawt[h][0], coef[h][:, j, c2:c2 + 1], Sins[h][c2], ALU.mult, ALU.add)
                        sp.op("vector", acs(), ctx=h)
            def ld2c(h, t):
                qm = "sync" if h % 2 == 0 else "gpsimd"
                r0_ = 128 * t
                b = t % 2

                def dq(h=h, r0=r0_, b=b):
                    return lambda e: e.dma_start(qld[h][b], qgT_r[:, 2 * h:2 * h + 2, r0:r0 + 128])
                sp.op(qm, dq(), inc=16, ctx=h)

                def dor(h=h, r0=r0_, b=b):
                    return lambda e: e.dma_start(orawt[h][b], oraw_s[r0:r0 + 128, 512 * h:512 * (h + 1)])
                sp.op(qm, dor(), inc=16, ctx=h)

            for c in range(NT):
                r0 = 128 * c
                for h in range(4):
                    qs = "gpsimd" if h % 2 == 0 else "sync"
                    psa_h = psA[:, 512 * h:512 * h + 512]
                    st = stat[:, 8 * h:8 * h + 8]
                    if c == 0:
                        ld2c(h, 0)
                    if c < NT - 1:
                        ld2c(h, c + 1)
                    b = c % 2

                    def epc(h=h, c=c):
                        return lambda e: e.activation(eP2[h], pcum[:, h, 2 * c:2 * c + 2], AF.Exp)
                    sp.op("scalar", epc(), ctx=h)
                    for c2 in range(2):
                        def sc_(c2=c2, h=h):
                            return lambda e: e.tensor_scalar(Sib2[h][c2], Sins[h][c2], eP2[h][:, c2:c2 + 1], None, ALU.mult)
                        sp.op("vector", sc_(), ctx=h)
                    for c2 in range(2):
                        def mmc(c2=c2, h=h, psa_h=psa_h, b=b):
                            return lambda e: e.matmul(psa_h[:, 0:512], qld[h][b][:, c2, :], Sib2[h][c2], start=(c2 == 0), stop=(c2 == 1))
                        sp.op("tensor", mmc(), ctx=h)

                    def oad(h=h, psa_h=psa_h, b=b):
                        return lambda e: e.tensor_add(osb2[h], orawt[h][b], psa_h[:, 0:512])
                    sp.op("vector", oad(), ctx=h)
                    def r1(h=h, st=st):
                        return lambda e: e.activation(sq5[h], osb2[h], AF.Square, accum_out=st[:, 0:1])
                    sp.op("scalar", r1(), ctx=h)

                    def r2(h=h, st=st):
                        return lambda e: e.tensor_scalar(st[:, 1:2], st[:, 0:1], 1.0 / 512.0, 1e-5, ALU.mult, ALU.add)
                    sp.op("vector", r2(), ctx=h)

                    def r3(h=h, st=st):
                        return lambda e: e.activation(st[:, 1:2], st[:, 1:2], AF.Sqrt)
                    sp.op("scalar", r3(), ctx=h)

                    def r4(h=h, st=st):
                        return lambda e: e.reciprocal(st[:, 2:3], st[:, 1:2])
                    sp.op("vector", r4(), ctx=h)

                    def r5(h=h, st=st):
                        return lambda e: e.scalar_tensor_tensor(gat[h], osb2[h], st[:, 2:3], rms_b, ALU.mult, ALU.mult)
                    sp.op("vector", r5(), ctx=h)

                    def don(h=h, r0=r0):
                        return lambda e: e.dma_start(on_s[r0:r0 + 128, 512 * h:512 * (h + 1)], gat[h])
                    sp.op(qs, don(), inc=16, ctx=h)
            sp.ctx_close()

        ph2()

        def ph3():
            A.reset()
            wo_bf = A.bf16([128, 16, 1024])
            qw_bf = A.bf16([128, 8, 512]); kw_bf = A.bf16([128, 8, 512]); vw_bf = A.bf16([128, 8, 512])
            stage = A.f32([128, 8192])
            qb_b = A.f32([128, 512]); kb_b = A.f32([128, 512]); vb_b = A.f32([128, 512])
            brow = A.f32([1, 512])
            on_t = [A.f32([128, 2048]) for _ in range(2)]
            ga_t = [A.f32([128, 2048]) for _ in range(2)]
            og32 = [A.f32([128, 2048]) for _ in range(2)]
            ogT = [A.bf16([128, 16, 128]) for _ in range(2)]
            xt = [A.f32([128, 1024]) for _ in range(2)]
            x1t = [A.f32([128, 1024]) for _ in range(2)]
            xc = [A.f32([128, 1024]) for _ in range(2)]
            sq = [A.f32([128, 1024]) for _ in range(2)]
            xnT = [A.bf16([128, 8, 128]) for _ in range(2)]
            ob = [A.f32([128, 1024]) for _ in range(2)]
            kvb = [A.bf16([128, 1024]) for _ in range(2)]
            emit_load_w(sp, wo_bf, wo, 2048, 1024, stage, 8192)
            emit_load_w(sp, qw_bf, qwe, D, 512, stage, 8192)
            emit_load_w(sp, kw_bf, kwf, D, 512, stage, 8192)
            emit_load_w(sp, vw_bf, vwf, D, 512, stage, 8192)
            for row, dst in ((qbe, qb_b), (kbf, kb_b), (vbf, vb_b)):
                def d(row=row):
                    return lambda e: e.dma_start(brow[0:1, :], row[None, :])
                sp.op("sync", d(row), inc=16)
                emit_bcast(sp, dst, brow, ones1, psA, 512)
            sp.ctx_open(2)
            for i in range(NT):
                r0 = 128 * i
                j = i % 2
                qm = "sync" if j == 0 else "gpsimd"
                qst = "gpsimd" if j == 0 else "sync"
                psb_j = psB[:, 1024 * j:1024 * j + 1024]
                psa_j = psA[:, 512 * j:512 * j + 128]

                def d1(j=j, r0=r0):
                    return lambda e: e.dma_start(on_t[j], on_s[r0:r0 + 128, :])
                sp.op(qm, d1(), inc=16, ctx=j)

                def d2(j=j, r0=r0):
                    return lambda e: e.dma_start(ga_t[j], gate_s[r0:r0 + 128, :])
                sp.op(qm, d2(), inc=16, ctx=j)

                def d3(j=j, r0=r0):
                    return lambda e: e.dma_start(xt[j], x[r0:r0 + 128, :])
                sp.op(qm, d3(), inc=16, ctx=j)

                def m1(j=j):
                    return lambda e: e.tensor_mul(og32[j], on_t[j], ga_t[j])
                sp.op("vector", m1(), ctx=j)
                emit_transposes(sp, ogT[j], og32[j], idn_sb, psa_j, 16, ctx=j)
                emit_proj_tm(sp, psb_j, ogT[j], wo_bf, 16, 1024, ctx=j)

                def m2(j=j, psb_j=psb_j):
                    return lambda e: e.tensor_add(x1t[j], psb_j[:, 0:1024], xt[j])
                sp.op("vector", m2(), ctx=j)

                def d4(j=j, r0=r0):
                    return lambda e: e.dma_start(x1_s[r0:r0 + 128, :], x1t[j])
                sp.op(qst, d4(), inc=16, ctx=j)
                emit_ln(sp, x1t[j], xc[j], sq[j], stat, D, c0=16 * j, ctx=j)

                def m3(j=j):
                    return lambda e: e.tensor_scalar(ob[j], xc[j], stat[:, 16 * j + 3:16 * j + 4], None, ALU.mult)
                sp.op("vector", m3(), ctx=j)

                def d5(j=j, r0=r0):
                    return lambda e: e.dma_start(xn1_s[r0:r0 + 128, :], ob[j])
                sp.op(qst, d5(), inc=16, ctx=j)
                emit_transposes(sp, xnT[j], ob[j], idn_sb, psa_j, 8, ctx=j)
                emit_proj_tm(sp, psb_j, xnT[j], qw_bf, 8, 512, ctx=j)

                def m4(j=j, psb_j=psb_j):
                    return lambda e: e.tensor_add(ob[j][:, 0:512], psb_j[:, 0:512], qb_b)
                sp.op("vector", m4(), ctx=j)

                def d6(j=j, r0=r0):
                    return lambda e: e.dma_start(Qe_s[r0:r0 + 128, :], ob[j][:, 0:512])
                sp.op(qst, d6(), inc=16, ctx=j)
                emit_proj_tm(sp, psb_j, xnT[j], kw_bf, 8, 512, ctx=j)

                def m5(j=j, psb_j=psb_j):
                    return lambda e: e.tensor_add(ob[j][:, 0:512], psb_j[:, 0:512], kb_b)
                sp.op("vector", m5(), ctx=j)

                def m6(j=j):
                    return lambda e: e.tensor_copy(kvb[j][:, 0:512], ob[j][:, 0:512])
                sp.op("vector", m6(), ctx=j)
                emit_proj_tm(sp, psb_j, xnT[j], vw_bf, 8, 512, ctx=j)

                def m7(j=j, psb_j=psb_j):
                    return lambda e: e.tensor_add(ob[j][:, 0:512], psb_j[:, 0:512], vb_b)
                sp.op("vector", m7(), ctx=j)

                def m8(j=j):
                    return lambda e: e.tensor_copy(kvb[j][:, 512:1024], ob[j][:, 0:512])
                sp.op("vector", m8(), ctx=j)

                def d7(j=j, r0=r0):
                    return lambda e: e.dma_start(cc_kv_in[r0:r0 + 128, :], kvb[j])
                sp.op(qst, d7(), inc=16, ctx=j)
            sp.ctx_close()

            sp.op("gpsimd", lambda e: e.collective_compute(
                "AllGather", ALU.bypass, replica_groups=RG,
                ins=[cc_kv_in.opt()], outs=[cc_kv_out.opt()]), inc=1)

        ph3()

        def ph4():
            A.reset()
            Kfm = A.bf16([128, 4, T])
            Vp = A.bf16([128, 16, 64, 34])
            kv = A.bf16([128, 1024])
            kvf = A.f32([128, 512])
            Qfm = A.bf16([128, 4, 512])
            qstage = A.f32([128, 512])
            kn = A.f32([128, 16]); knt = A.f32([128, 16]); knT = A.f32([16, 128])
            knmax = A.f32([16, 1]); qnm = A.f32([128, 16]); qmax = A.f32([16, 1])
            Mv = A.f32([16, 1]); mdiag = A.f32([16, 16])
            mb = A.f32([128, 16]); mbt = A.f32([128, 8]); mbg = A.f32([128, 4])
            sq4 = A.f32([128, 512])
            ptil2 = [A.bf16([128, 2048]), A.bf16([128, 2048])]
            ozsb = A.f32([33, 2048])
            tz = A.f32([128, 33])
            aot = A.f32([128, 4, 512])
            aofm = A.bf16([128, 4, 512])
            ow_bf = A.bf16([128, 4, 1024])
            ob_b = A.f32([128, 1024]); rb_b = A.f32([128, 8])
            rw_bf = A.bf16([128, 8, 8])
            mla_sb = A.f32([128, 1024])
            mlaT = A.bf16([128, 8, 128])
            pr = A.f32([128, 8]); e8 = A.f32([128, 8]); w8 = A.f32([128, 8])
            brow = A.f32([1, 1024])
            stage4 = A.f32([128, 2048])
            emit_load_w(sp, ow_bf, owe, 512, 1024, stage4, 2048)
            emit_load_w(sp, rw_bf, rw, 1024, 8, stage4, 2048)
            sp.dma(brow[0:1, :], obv[None, :])
            emit_bcast(sp, ob_b, brow, ones1, psA, 1024)
            sp.dma(brow[0:1, 0:8], rb[None, :])
            emit_bcast(sp, rb_b, brow, ones1, psA, 8)
            sp.op("vector", lambda e: e.memset(kn, 0.0))
            sp.op("vector", lambda e: e.memset(Vp[:, :, :, 32:33], 1.0))
            for j in range(64):
                r0 = 128 * j
                sp.dma(kv, cc_kv_out[r0:r0 + 128, :])
                sp.op("vector", lambda e: e.tensor_copy(kvf, kv[:, 0:512]))
                sp.op("scalar", lambda e: e.activation(sq4, kvf, AF.Square))
                sp.op("vector", lambda e: e.reduce_sum(knt, sq4.rearrange("p (h k) -> p h k", k=32), axis=AX.X))
                sp.op("vector", lambda e: e.tensor_max(kn, kn, knt))
                for c in range(4):
                    def tr(c=c):
                        return lambda e: e.transpose(psA[:, 0:128], kvf[:, 128 * c:128 * (c + 1)], idn_sb)
                    sp.op("tensor", tr(c))

                    def cp(c=c, j=j):
                        return lambda e: e.tensor_copy(Kfm[:, c, 128 * j:128 * (j + 1)], psA[:, 0:128])
                    sp.op("vector", cp(c, j))

                def cpv(j=j):
                    return lambda e: e.tensor_copy(Vp[:, :, j, 0:32], kv[:, 512:1024].rearrange("p (h k) -> p h k", k=32))
                sp.op("vector", cpv(j))
            sp.op("tensor", lambda e: e.transpose(psA[0:16, 0:128], kn, idn_sb))
            sp.op("vector", lambda e: e.tensor_copy(knT, psA[0:16, 0:128]))
            sp.op("vector", lambda e: e.reduce_max(knmax, knT, axis=AX.X))
            sp.op("scalar", lambda e: e.activation(knmax, knmax, AF.Sqrt))

            for qb in range(4):
                q0 = 512 * qb
                for qt in range(4):
                    r0 = q0 + 128 * qt
                    sp.dma(qstage, Qe_s[r0:r0 + 128, :])
                    sp.op("scalar", lambda e: e.activation(sq4, qstage, AF.Square))
                    sp.op("vector", lambda e: e.reduce_sum(knt, sq4.rearrange("p (h k) -> p h k", k=32), axis=AX.X))
                    if qt == 0:
                        sp.op("vector", lambda e: e.tensor_copy(qnm, knt))
                    else:
                        sp.op("vector", lambda e: e.tensor_max(qnm, qnm, knt))
                    for c in range(4):
                        def tr(c=c):
                            return lambda e: e.transpose(psA[:, 0:128], qstage[:, 128 * c:128 * (c + 1)], idn_sb)
                        sp.op("tensor", tr(c))

                        def cp(c=c, qt=qt):
                            return lambda e: e.tensor_copy(Qfm[:, c, 128 * qt:128 * (qt + 1)], psA[:, 0:128])
                        sp.op("vector", cp(c, qt))
                sp.op("tensor", lambda e: e.transpose(psA[0:16, 0:128], qnm, idn_sb))
                sp.op("vector", lambda e: e.tensor_copy(knT, psA[0:16, 0:128]))
                sp.op("vector", lambda e: e.reduce_max(qmax, knT, axis=AX.X))
                sp.op("scalar", lambda e: e.activation(qmax, qmax, AF.Sqrt))
                sp.op("vector", lambda e: e.tensor_mul(Mv, qmax, knmax))
                sp.op("vector", lambda e: e.tensor_scalar(mdiag, i16_sb, Mv[:, 0:1], None, ALU.mult))
                sp.op("tensor", lambda e: e.matmul(psA[:, 0:16], ones16, mdiag, start=True, stop=True))
                sp.op("scalar", lambda e: e.activation(mb, psA[:, 0:16], AF.Copy, scale=-RS32))
                sp.op("vector", lambda e: e.tensor_tensor(mbt, mb[:, 0:16:2], mb[:, 1:16:2], ALU.min))
                sp.op("vector", lambda e: e.tensor_tensor(mbg, mbt[:, 0:8:2], mbt[:, 1:8:2], ALU.min))

                bq = list(sp.now().items())
                for g in range(4):
                    e_ex_prev = [None, None]
                    e_mmv_pp = [[None, None], [None, None]]
                    e_ex_j = {}

                    def emit_mms(j, pair):
                        dl = bq + ([e_ex_prev[pair]] if e_ex_prev[pair] else [])
                        ev = None
                        for hh in (2 * pair, 2 * pair + 1):
                            def mms(g=g, j=j, hh=hh):
                                return lambda e: e.matmul(
                                    psA[:, 512 * hh:512 * (hh + 1)],
                                    Kfm[32 * hh:32 * (hh + 1), g, 128 * j:128 * (j + 1)],
                                    Qfm[32 * hh:32 * (hh + 1), g, :],
                                    start=True, stop=True, tile_position=(32 * hh, 0))
                            ev = sp.op("tensor", mms(g, j, hh), deps=dl)
                        return ev

                    def emit_ex(j, pair, e_mms):
                        par = j % 2

                        def ex(g=g, par=par, pair=pair):
                            return lambda e: e.activation(
                                ptil2[par][:, 1024 * pair:1024 * (pair + 1)],
                                psA[:, 1024 * pair:1024 * (pair + 1)],
                                AF.Exp, bias=mbg[:, g:g + 1], scale=RS32)
                        dl = bq + [e_mms] + ([e_mmv_pp[par][pair]] if e_mmv_pp[par][pair] else [])
                        e_ex_prev[pair] = sp.op("scalar", ex(), deps=dl)
                        e_ex_j[(j, pair)] = e_ex_prev[pair]

                    def emit_mmv(j, pair):
                        par = j % 2
                        dl = bq + [e_ex_j.pop((j, pair))]
                        ev = None
                        for hh in (2 * pair, 2 * pair + 1):
                            h = 4 * g + hh

                            def mmv(h=h, hh=hh, j=j, par=par):
                                return lambda e: e.matmul(
                                    psB[0:33, 512 * hh:512 * hh + 512],
                                    Vp[:, h, j, 0:33],
                                    ptil2[par][:, 512 * hh:512 * (hh + 1)],
                                    start=(j == 0), stop=(j == 63), skip_group_check=True)
                            ev = sp.op("tensor", mmv(h, hh, j, par), deps=dl)
                        e_mmv_pp[par][pair] = ev

                    for j in range(64):
                        for pair in range(2):
                            e_mms = emit_mms(j, pair)
                            emit_ex(j, pair, e_mms)
                        if j > 0:
                            for pair in range(2):
                                emit_mmv(j - 1, pair)
                    for pair in range(2):
                        emit_mmv(63, pair)
                    sp.op("vector", lambda e: e.tensor_copy(ozsb, psB[0:33, :]))
                    for hh in range(4):
                        h = 4 * g + hh
                        for qt in range(4):
                            def trz(hh=hh, qt=qt):
                                return lambda e: e.transpose(psA[:, 0:33], ozsb[0:33, 512 * hh + 128 * qt:512 * hh + 128 * (qt + 1)], idn_sb[0:33, 0:33])
                            sp.op("tensor", trz(hh, qt))
                            sp.op("vector", lambda e: e.tensor_copy(tz, psA[:, 0:33]))
                            sp.op("vector", lambda e: e.tensor_scalar_add(stat[:, 0:1], tz[:, 32:33], 1e-38))
                            sp.op("vector", lambda e: e.reciprocal(stat[:, 1:2], stat[:, 0:1]))

                            def dv(h=h, qt=qt):
                                return lambda e: e.tensor_scalar(aot[:, qt, 32 * h:32 * (h + 1)], tz[:, 0:32], stat[:, 1:2], None, ALU.mult)
                            sp.op("vector", dv(h, qt))
                    bq = list(sp.now().items())
                    e_ex_prev = None
                    e_mmv_pp = [None, None]
                for qt in range(4):
                    for c in range(4):
                        def tra(qt=qt, c=c):
                            return lambda e: e.transpose(psA[:, 0:128], aot[:, qt, 128 * c:128 * (c + 1)], idn_sb)
                        sp.op("tensor", tra(qt, c))

                        def cpa(qt=qt, c=c):
                            return lambda e: e.tensor_copy(aofm[:, c, 128 * qt:128 * (qt + 1)], psA[:, 0:128])
                        sp.op("vector", cpa(qt, c))
                for qt in range(4):
                    r0 = q0 + 128 * qt
                    for c in range(4):
                        for s in range(2):
                            def mmm(c=c, s=s, qt=qt):
                                return lambda e: e.matmul(psB[:, 512 * s:512 * (s + 1)], aofm[:, c, 128 * qt:128 * (qt + 1)], ow_bf[:, c, 512 * s:512 * (s + 1)], start=(c == 0), stop=(c == 3))
                            sp.op("tensor", mmm(c, s, qt))
                    sp.op("vector", lambda e: e.tensor_add(mla_sb, psB[:, 0:1024], ob_b))

                    def dmla(r0=r0):
                        return lambda e: e.dma_start(mla_s[r0:r0 + 128, :], mla_sb)
                    sp.op("sync", dmla(r0), inc=16)
                    emit_transposes(sp, mlaT, mla_sb, idn_sb, psA, 8)
                    for c in range(8):
                        def mmr(c=c):
                            return lambda e: e.matmul(psB[:, 1024:1032], mlaT[:, c, :], rw_bf[:, c, :], start=(c == 0), stop=(c == 7))
                        sp.op("tensor", mmr(c))
                    sp.op("vector", lambda e: e.tensor_add(pr, psB[:, 1024:1032], rb_b))
                    sp.op("vector", lambda e: e.reduce_max(stat[:, 0:1], pr, axis=AX.X, negate=True))
                    sp.op("scalar", lambda e: e.activation(e8, pr, AF.Exp, bias=stat[:, 0:1], accum_out=stat[:, 1:2]))
                    sp.op("vector", lambda e: e.reciprocal(stat[:, 2:3], stat[:, 1:2]))
                    sp.op("vector", lambda e: e.tensor_scalar(pr, e8, stat[:, 2:3], None, ALU.mult))
                    sp.op("vector", lambda e: e.reduce_max(stat[:, 3:4], pr, axis=AX.X))
                    sp.op("vector", lambda e: e.tensor_scalar(e8, pr, stat[:, 3:4], None, ALU.is_ge))
                    sp.op("vector", lambda e: e.scalar_tensor_tensor(w8, e8, -1e30, pr, ALU.mult, ALU.add))
                    sp.op("vector", lambda e: e.reduce_max(stat[:, 4:5], w8, axis=AX.X))
                    sp.op("vector", lambda e: e.tensor_scalar(e8, pr, stat[:, 4:5], None, ALU.is_ge))
                    sp.op("vector", lambda e: e.tensor_add(stat[:, 5:6], stat[:, 3:4], stat[:, 4:5]))
                    sp.op("vector", lambda e: e.tensor_scalar_add(stat[:, 5:6], stat[:, 5:6], 1e-8))
                    sp.op("vector", lambda e: e.reciprocal(stat[:, 6:7], stat[:, 5:6]))
                    sp.op("vector", lambda e: e.tensor_mul(e8, e8, pr))
                    sp.op("vector", lambda e: e.tensor_scalar(e8, e8, stat[:, 6:7], None, ALU.mult))

                    def dcomb(r0=r0):
                        return lambda e: e.dma_start(comb_s[r0:r0 + 128, :], e8)
                    sp.op("sync", dcomb(r0), inc=16)

        ph4()

        def ph5():
            A.reset()
            mlaT = A.bf16([128, 8, 2048])
            comb_sb = A.f32([128, 16, 8])
            w1_bf = A.bf16([128, 8, 2048])
            w2_bf = A.bf16([128, 16, 1024])
            h1_bf = A.bf16([128, 16, 512])
            acc = A.f32([128, 16, 1024])
            stage = A.f32([128, 2048])
            b1f = A.f32([128, 16]); b2b = A.f32([128, 1024])
            brow = A.f32([1, 1024])
            mt = A.f32([128, 1024]); ysb = A.f32([128, 1024]); xc = A.f32([128, 1024])
            w1r = w1a.rearrange("e (c p) n -> e p c n", p=128)
            w2r = w2a.rearrange("e (c p) n -> e p c n", p=128)
            sp.dma(comb_sb, comb_s.rearrange("(n p) e -> p n e", p=128))
            sp.op("vector", lambda e: e.memset(acc, 0.0))
            mt2 = [mt, xc]
            sp.ctx_open(2)
            for i in range(NT):
                r0 = 128 * i
                j = i % 2
                qm = "sync" if j == 0 else "gpsimd"

                def dmt(i=i, j=j, r0=r0):
                    return lambda e: e.dma_start(mt2[j], mla_s[r0:r0 + 128, :])
                sp.op(qm, dmt(), inc=16, ctx=j)
                for c in range(8):
                    def tr(c=c, j=j):
                        return lambda e: e.transpose(psA[:, 512 * j:512 * j + 128], mt2[j][:, 128 * c:128 * (c + 1)], idn_sb)
                    sp.op("tensor", tr(c, j), ctx=j)

                    def cp(c=c, i=i, j=j):
                        return lambda e: e.tensor_copy(mlaT[:, c, 128 * i:128 * (i + 1)], psA[:, 512 * j:512 * j + 128])
                    sp.op("vector", cp(c, i, j), ctx=j)
            sp.ctx_close()
            e_mlaT = ("vector", sp.cnt["vector"])
            e_hlast = ("tensor", 0)
            e_ylast = ("tensor", 0)
            e_gel_last = ("scalar", 0)
            e_b2mm = ("tensor", 0)
            e_evac = {qt: ("vector", 0) for qt in range(4)}
            e_stg = [("vector", 0), ("vector", 0)]
            e_b2cp = ("vector", 0)
            ge = {}
            for ex in range(8):
                caster = []
                pieces = []
                for c in range(8):
                    for hf in range(2):
                        pieces.append(("w1", c, hf))
                for c in range(16):
                    pieces.append(("w2", c, 0))
                for pi, (kind, c, hf) in enumerate(pieces):
                    par = pi % 2
                    stg = stage[:, 1024 * par:1024 * (par + 1)]
                    if kind == "w1":
                        def dd(ex=ex, c=c, hf=hf, stg=stg):
                            return lambda e: e.dma_start(stg, w1r[ex, :, c, 1024 * hf:1024 * (hf + 1)])
                        war = e_hlast
                    else:
                        def dd(ex=ex, c=c, stg=stg):
                            return lambda e: e.dma_start(stg, w2r[ex, :, c, :])
                        war = e_ylast
                    e_d = sp.op("sync", dd(), inc=16, deps=[e_stg[par]])
                    if kind == "w1":
                        def cc(c=c, hf=hf, stg=stg):
                            return lambda e: e.tensor_copy(w1_bf[:, c, 1024 * hf:1024 * (hf + 1)], stg)
                    else:
                        def cc(c=c, stg=stg):
                            return lambda e: e.tensor_copy(w2_bf[:, c, :], stg)
                    e_c = sp.op("vector", cc(), deps=[e_d, war])
                    e_stg[par] = e_c
                    caster.append((kind, e_c))
                e_w1cast = max(c for (k, c) in [(k, e[1]) for (k, e) in caster if k == "w1"])
                e_w1cast = ("vector", e_w1cast)
                e_w2cast = ("vector", max(e[1] for (k, e) in caster if k == "w2"))
                e_b1 = sp.op("sync", (lambda ex=ex: (lambda e: e.dma_start(b1f, b1fma[ex])))(), inc=16, deps=[e_gel_last])
                e_b2r = sp.op("sync", (lambda ex=ex: (lambda e: e.dma_start(brow[0:1, :], b2a[ex:ex + 1, :])))(), inc=16, deps=[e_b2mm])
                e_b2mm = sp.op("tensor", lambda e: e.matmul(psA[:, 1536:2048], ones1, brow[0:1, 0:512], start=True, stop=True), deps=[e_b2r, e_evac[2], e_evac[3], e_b2cp])
                e_b2mm2 = sp.op("tensor", lambda e: e.matmul(psB[:, 1536:2048], ones1, brow[0:1, 512:1024], start=True, stop=True), deps=[e_b2r, e_evac[1], e_b2cp])
                e_c1 = sp.op("vector", lambda e: e.tensor_copy(b2b[:, 0:512], psA[:, 1536:2048]), deps=[e_b2mm, e_evac[0]])
                e_b2cp = sp.op("vector", lambda e: e.tensor_copy(b2b[:, 512:1024], psB[:, 1536:2048]), deps=[e_b2mm2])
                for qb in range(4):
                    e_y_prev_qb = e_ylast
                    for fc in range(16):
                        bank = psA[:, 512 * (fc % 2):512 * (fc % 2 + 1)]
                        if fc >= 2:
                            bank_war = [ge.get(fc - 2, ("scalar", 0))]
                        else:
                            bank_war = [ge.get(fc + 14, ("scalar", 0)), e_evac[2], e_evac[3]]
                        dl = [e_w1cast, e_mlaT, e_b2cp] + bank_war
                        for k in range(8):
                            def mh(fc=fc, k=k, qb=qb, bank=bank):
                                return lambda e: e.matmul(bank, w1_bf[:, k, 128 * fc:128 * (fc + 1)], mlaT[:, k, 512 * qb:512 * (qb + 1)], start=(k == 0), stop=(k == 7))
                            e_hlast = sp.op("tensor", mh(), deps=dl)

                        def gel(fc=fc, bank=bank):
                            return lambda e: e.activation(h1_bf[:, fc, :], bank, AF.Gelu, bias=b1f[:, fc:fc + 1])
                        ge[fc] = sp.op("scalar", gel(), deps=[e_hlast, e_y_prev_qb, e_b1])
                    e_gel_last = ge[15]
                    for qt in range(4):
                        ps = psB if qt < 2 else psA
                        off = 1024 * (qt % 2)
                        dl = [e_w2cast, e_evac[qt], ge[15], e_b2cp]
                        for fc in range(16):
                            dl2 = dl + [ge[fc]]
                            for s in range(2):
                                def my(fc=fc, s=s, qt=qt, ps=ps, off=off, qb=qb):
                                    return lambda e: e.matmul(ps[:, off + 512 * s:off + 512 * (s + 1)], h1_bf[:, fc, 128 * qt:128 * (qt + 1)], w2_bf[:, fc, 512 * s:512 * (s + 1)], start=(fc == 0), stop=(fc == 15))
                                e_ylast = sp.op("tensor", my(), deps=dl2)

                        def ev(ps=ps, off=off):
                            return lambda e: e.tensor_add(ysb, ps[:, off:off + 1024], b2b)
                        e_ev = sp.op("vector", ev(), deps=[e_ylast])
                        qg = 4 * qb + qt

                        def accum(ex=ex, qg=qg):
                            return lambda e: e.scalar_tensor_tensor(acc[:, qg, :], ysb, comb_sb[:, qg, ex:ex + 1], acc[:, qg, :], ALU.mult, ALU.add)
                        e_evac[qt] = sp.op("vector", accum(), deps=[])
            for i in range(NT):
                r0 = 128 * i
                sp.dma(mt, x1_s[r0:r0 + 128, :])
                sp.dma(stage[:, 0:1024], xn1_s[r0:r0 + 128, :])

                def fin(i=i):
                    return lambda e: e.scalar_tensor_tensor(ysb, acc[:, i, :], 0.3, mt, ALU.mult, ALU.add)
                sp.op("vector", fin(i))
                sp.op("vector", lambda e: e.tensor_add(ysb, ysb, stage[:, 0:1024]))
                sp.dma(x2_s[r0:r0 + 128, :], ysb)
                emit_ln(sp, ysb, xc, stage[:, 1024:2048], stat, D)
                sp.op("vector", lambda e: e.tensor_scalar(mt, xc, stat[:, 3:4], None, ALU.mult))
                sp.dma(xn2_s[r0:r0 + 128, :], mt)

        ph5()

        def ph6():
            A.reset()
            wqb = A.bf16([128, 8, 1024]); wkb = A.bf16([128, 8, 1024])
            wvb = A.bf16([128, 8, 1024]); wgb = A.bf16([128, 8, 1024])
            stage = A.f32([128, 8192])
            lw_b = A.f32([128, 1024]); lb_b = A.f32([128, 1024])
            brow = A.f32([1, 1024])
            xt = [A.f32([128, 1024]) for _ in range(2)]
            xc = [A.f32([128, 1024]) for _ in range(2)]
            sq = [A.f32([128, 1024]) for _ in range(2)]
            u = [A.f32([128, 1024]) for _ in range(2)]
            uT = [A.bf16([128, 8, 128]) for _ in range(2)]
            ob = [A.f32([128, 1024]) for _ in range(2)]
            emit_load_w(sp, wqb, wq2, D, 1024, stage, 8192)
            emit_load_w(sp, wkb, wk2, D, 1024, stage, 8192)
            emit_load_w(sp, wvb, wv2, D, 1024, stage, 8192)
            emit_load_w(sp, wgb, wg2, D, 1024, stage, 8192)
            for row, dst in ((ln1w, lw_b), (ln1b, lb_b)):
                def d(row=row):
                    return lambda e: e.dma_start(brow[0:1, :], row[None, :])
                sp.op("sync", d(row), inc=16)
                emit_bcast(sp, dst, brow, ones1, psA, 1024)
            sp.ctx_open(2)
            for i in range(NT):
                r0 = 128 * i
                j = i % 2
                qm = "sync" if j == 0 else "gpsimd"
                qst = "gpsimd" if j == 0 else "sync"

                def dxt6(j=j, r0=r0):
                    return lambda e: e.dma_start(xt[j], xn2_s[r0:r0 + 128, :])
                sp.op(qm, dxt6(), inc=16, ctx=j)
                emit_ln(sp, xt[j], xc[j], sq[j], stat, D, c0=16 * j, ctx=j)

                def u1(j=j):
                    return lambda e: e.tensor_scalar(u[j], xc[j], stat[:, 16 * j + 3:16 * j + 4], None, ALU.mult)
                sp.op("vector", u1(), ctx=j)

                def u2(j=j):
                    return lambda e: e.tensor_mul(u[j], u[j], lw_b)
                sp.op("vector", u2(), ctx=j)

                def u3(j=j):
                    return lambda e: e.tensor_add(u[j], u[j], lb_b)
                sp.op("vector", u3(), ctx=j)
                emit_transposes(sp, uT[j], u[j], idn_sb, psA[:, 512 * j:512 * j + 128], 8, ctx=j)
                for wbf, dst, act, scale in ((wqb, q2_s, AF.Copy, 0.125), (wkb, k2_s, AF.Copy, 1.0), (wvb, v2_s, AF.Copy, 1.0), (wgb, gate2_s, AF.Silu, 1.0)):
                    emit_proj_tm(sp, psB[:, 1024 * j:1024 * j + 1024], uT[j], wbf, 8, 1024, ctx=j)

                    def ev(act=act, scale=scale, j=j):
                        return lambda e: e.activation(ob[j], psB[:, 1024 * j:1024 * j + 1024], act, scale=scale)
                    sp.op("scalar", ev(), ctx=j)

                    def dd(dst=dst, r0=r0, j=j):
                        return lambda e: e.dma_start(dst[r0:r0 + 128, :], ob[j])
                    sp.op(qst, dd(), inc=16, ctx=j)
            sp.ctx_close()

        ph6()

        def ph7a():
            A.reset()
            NC7 = 8
            S16 = A.f32([64, 16, 64])
            gq_sb = A.f32([128, 16]); gk2_sb = A.f32([128, 16]); gkh_sb = A.f32([128, 16])
            gs_b = A.f32([64, 16])
            rg_row = A.f32([1, 16])
            qtm = [[A.f32([128, 64]) for _ in range(2)] for _ in range(NC7)]
            ktm7 = [[A.f32([128, 64]) for _ in range(2)] for _ in range(NC7)]
            vtm7 = [[A.f32([128, 64]) for _ in range(2)] for _ in range(NC7)]
            qstm = [A.f32([128, 64]) for _ in range(NC7)]
            k2tm = [A.f32([128, 64]) for _ in range(NC7)]
            khat = [A.bf16([128, 64]) for _ in range(NC7)]
            v_bf7 = [A.bf16([128, 64]) for _ in range(NC7)]
            qf_bf = [A.bf16([64, 128]) for _ in range(NC7)]
            kf_bf = [A.bf16([64, 128]) for _ in range(NC7)]
            at7 = [A.bf16([128, 128]) for _ in range(NC7)]
            Sb7 = [A.bf16([64, 64]) for _ in range(NC7)]
            osb7 = [A.f32([128, 64]) for _ in range(NC7)]
            sp.dma(gq_sb, rgq)
            sp.dma(gk2_sb, rgk2)
            sp.dma(gkh_sb, rgkh)
            sp.dma(rg_row[0:1, :], rgs)
            sp.op("tensor", lambda e: e.matmul(psA[0:64, 0:16], ones1[0:1, 0:64], rg_row[0:1, :], start=True, stop=True))
            sp.op("vector", lambda e: e.tensor_copy(gs_b, psA[0:64, 0:16]))
            sp.op("vector", lambda e: e.memset(S16, 0.0))
            def ld7(j, t):
                c_, hi_ = t // 2, t % 2
                h_ = 2 * j + hi_
                r0_ = 128 * c_
                qm = "sync" if j % 2 == 0 else "gpsimd"
                b = t % 2

                def dq7(h=h_, r0=r0_, j=j, b=b):
                    return lambda e: e.dma_start(qtm[j][b], q2_s[r0:r0 + 128, 64 * h:64 * (h + 1)])
                sp.op(qm, dq7(), inc=16, ctx=j)

                def dk7(h=h_, r0=r0_, j=j, b=b):
                    return lambda e: e.dma_start(ktm7[j][b], k2_s[r0:r0 + 128, 64 * h:64 * (h + 1)])
                sp.op(qm, dk7(), inc=16, ctx=j)

                def dv7(h=h_, r0=r0_, j=j, b=b):
                    return lambda e: e.dma_start(vtm7[j][b], v2_s[r0:r0 + 128, 64 * h:64 * (h + 1)])
                sp.op(qm, dv7(), inc=16, ctx=j)

            sp.ctx_open(NC7)
            for c in range(NT):
                r0 = 128 * c
                for j in range(NC7):
                    qs = "gpsimd" if j % 2 == 0 else "sync"
                    psm = (psA if j < 4 else psB)[:, 512 * (j % 4):512 * (j % 4) + 512]
                    for h in (2 * j, 2 * j + 1):
                        t = 2 * c + (h - 2 * j)
                        if t == 0:
                            ld7(j, 0)
                        if t < 2 * NT - 1:
                            ld7(j, t + 1)
                        b = t % 2

                        def sq_(h=h, j=j, b=b):
                            return lambda e: e.tensor_scalar(qstm[j], qtm[j][b], gq_sb[:, h:h + 1], None, ALU.mult)
                        sp.op("vector", sq_(), ctx=j)

                        def sk2(h=h, j=j, b=b):
                            return lambda e: e.tensor_scalar(k2tm[j], ktm7[j][b], gk2_sb[:, h:h + 1], None, ALU.mult)
                        sp.op("vector", sk2(), ctx=j)

                        def skh(h=h, j=j, b=b):
                            return lambda e: e.tensor_scalar(khat[j], ktm7[j][b], gkh_sb[:, h:h + 1], None, ALU.mult)
                        sp.op("vector", skh(), ctx=j)

                        def vcp(j=j, b=b):
                            return lambda e: e.tensor_copy(v_bf7[j], vtm7[j][b])
                        sp.op("vector", vcp(), ctx=j)

                        def sbc(h=h, j=j):
                            return lambda e: e.tensor_copy(Sb7[j], S16[:, h, :])
                        sp.op("vector", sbc(), ctx=j)

                        def trq7(j=j, psm=psm):
                            return lambda e: e.transpose(psm[0:64, 0:128], qstm[j], idn_sb)
                        sp.op("tensor", trq7(), ctx=j)

                        def cpq7(j=j, psm=psm):
                            return lambda e: e.tensor_copy(qf_bf[j], psm[0:64, 0:128])
                        sp.op("vector", cpq7(), ctx=j)

                        def trk7(j=j, psm=psm):
                            return lambda e: e.transpose(psm[0:64, 128:256], k2tm[j], idn_sb)
                        sp.op("tensor", trk7(), ctx=j)

                        def cpk7(j=j, psm=psm):
                            return lambda e: e.tensor_copy(kf_bf[j], psm[0:64, 128:256])
                        sp.op("vector", cpk7(), ctx=j)

                        def mkq(j=j, psm=psm):
                            return lambda e: e.matmul(psm[:, 256:384], kf_bf[j], qf_bf[j], start=True, stop=True)
                        sp.op("tensor", mkq(), ctx=j)

                        def mat7(j=j, psm=psm):
                            return lambda e: e.tensor_tensor(at7[j], psm[:, 256:384], triu_sb, ALU.mult)
                        sp.op("vector", mat7(), ctx=j)

                        def mo1(j=j, psm=psm):
                            return lambda e: e.matmul(psm[:, 384:448], at7[j], v_bf7[j], start=True, stop=False)
                        sp.op("tensor", mo1(), ctx=j)

                        def mo2(j=j, psm=psm):
                            return lambda e: e.matmul(psm[:, 384:448], qf_bf[j], Sb7[j], start=False, stop=True)
                        sp.op("tensor", mo2(), ctx=j)

                        def dqs(h=h, r0=r0, j=j):
                            return lambda e: e.dma_start(q2f_s[64 * h:64 * (h + 1), r0:r0 + 128], qf_bf[j])
                        sp.op(qs, dqs(), inc=16, ctx=j)

                        def ocp(j=j, psm=psm):
                            return lambda e: e.tensor_copy(osb7[j], psm[:, 384:448])
                        sp.op("vector", ocp(), ctx=j)

                        def mkh(j=j, psm=psm):
                            return lambda e: e.matmul(psm[0:64, 448:512], khat[j], v_bf7[j], start=True, stop=True)
                        sp.op("tensor", mkh(), ctx=j)

                        def upd(h=h, j=j, psm=psm):
                            return lambda e: e.scalar_tensor_tensor(S16[:, h, :], S16[:, h, :], gs_b[:, h:h + 1], psm[0:64, 448:512], ALU.mult, ALU.add)
                        sp.op("vector", upd(), ctx=j)

                        def dos(h=h, r0=r0, j=j):
                            return lambda e: e.dma_start(o2raw_s[r0:r0 + 128, 64 * h:64 * (h + 1)], osb7[j])
                        sp.op(qs, dos(), inc=16, ctx=j)
            for j in range(NC7):
                qs = "gpsimd" if j % 2 == 0 else "sync"
                for h in (2 * j, 2 * j + 1):
                    def dst_(h=h):
                        return lambda e: e.dma_start(cc_rs_in[64 * h:64 * (h + 1), :], S16[:, h, :])
                    sp.op(qs, dst_(), inc=16, ctx=j)
            sp.ctx_close()

            sp.op("gpsimd", lambda e: e.collective_compute(
                "AllGather", ALU.bypass, replica_groups=RG,
                ins=[cc_rs_in.opt()], outs=[cc_rs_out.opt()]), inc=1)

        ph7a()

        def ph7c():
            NC7 = 8
            ccoef_sb = A.f32([64, 64])
            gcc_sb = A.f32([64, 256])
            Sin7 = [A.f32([64, 2, 64]) for _ in range(NC7)]
            Sic = [A.bf16([64, 64]) for _ in range(NC7)]
            qld7 = [[A.bf16([64, 128]) for _ in range(2)] for _ in range(NC7)]
            ort = [[A.f32([128, 64]) for _ in range(2)] for _ in range(NC7)]
            osb7c = [A.f32([128, 64]) for _ in range(NC7)]
            sq7 = [A.f32([128, 64]) for _ in range(NC7)]
            sp.dma(ccoef_sb, rccoef)
            sp.dma(gcc_sb, rgcc)
            def ldc(j, t):
                c_, hi_ = t // 2, t % 2
                h_ = 2 * j + hi_
                r0_ = 128 * c_
                qm = "sync" if j % 2 == 0 else "gpsimd"
                b = t % 2

                def dq2(h=h_, r0=r0_, j=j, b=b):
                    return lambda e: e.dma_start(qld7[j][b], q2f_s[64 * h:64 * (h + 1), r0:r0 + 128])
                sp.op(qm, dq2(), inc=16, ctx=j)

                def dor2(h=h_, r0=r0_, j=j, b=b):
                    return lambda e: e.dma_start(ort[j][b], o2raw_s[r0:r0 + 128, 64 * h:64 * (h + 1)])
                sp.op(qm, dor2(), inc=16, ctx=j)

            sp.ctx_open(NC7)
            for j in range(NC7):
                qm = "sync" if j % 2 == 0 else "gpsimd"
                for hi in range(2):
                    h = 2 * j + hi

                    def zs(j=j, hi=hi):
                        return lambda e: e.memset(Sin7[j][:, hi, :], 0.0)
                    sp.op("vector", zs(), ctx=j)
                    for l in range(4):
                        def dsj(l=l, h=h, j=j):
                            return lambda e: e.dma_start(ort[j][0][0:64, :], cc_rs_out[1024 * l + 64 * h:1024 * l + 64 * (h + 1), :])
                        sp.op(qm, dsj(), inc=16, ctx=j)

                        def acs(l=l, h=h, j=j, hi=hi):
                            return lambda e: e.scalar_tensor_tensor(Sin7[j][:, hi, :], ort[j][0][0:64, :], ccoef_sb[:, 4 * h + l:4 * h + l + 1], Sin7[j][:, hi, :], ALU.mult, ALU.add)
                        sp.op("vector", acs(), ctx=j)
            for c in range(NT):
                r0 = 128 * c
                for j in range(NC7):
                    qs = "gpsimd" if j % 2 == 0 else "sync"
                    psm = (psA if j < 4 else psB)[:, 512 * (j % 4):512 * (j % 4) + 512]
                    st = stat[:, 8 * j:8 * j + 8]
                    for hi in range(2):
                        h = 2 * j + hi
                        t = 2 * c + hi
                        if t == 0:
                            ldc(j, 0)
                        if t < 2 * NT - 1:
                            ldc(j, t + 1)
                        b = t % 2

                        def sc7(h=h, c=c, j=j, hi=hi):
                            return lambda e: e.tensor_scalar(Sic[j], Sin7[j][:, hi, :], gcc_sb[:, 16 * h + c:16 * h + c + 1], None, ALU.mult)
                        sp.op("vector", sc7(), ctx=j)

                        def mmc7(j=j, psm=psm, b=b):
                            return lambda e: e.matmul(psm[:, 0:64], qld7[j][b], Sic[j], start=True, stop=True)
                        sp.op("tensor", mmc7(), ctx=j)

                        def add7(j=j, psm=psm, b=b):
                            return lambda e: e.tensor_add(osb7c[j], ort[j][b], psm[:, 0:64])
                        sp.op("vector", add7(), ctx=j)

                        def gn1(j=j, st=st):
                            return lambda e: e.reduce_sum(st[:, 0:1], osb7c[j], axis=AX.X)
                        sp.op("vector", gn1(), ctx=j)

                        def gn2(j=j, st=st):
                            return lambda e: e.tensor_scalar_mul(st[:, 1:2], st[:, 0:1], 1.0 / 64.0)
                        sp.op("vector", gn2(), ctx=j)

                        def gn3(j=j, st=st):
                            return lambda e: e.tensor_scalar(osb7c[j], osb7c[j], st[:, 1:2], None, ALU.subtract)
                        sp.op("vector", gn3(), ctx=j)

                        def gn4(j=j, st=st):
                            return lambda e: e.activation(sq7[j], osb7c[j], AF.Square, accum_out=st[:, 2:3])
                        sp.op("scalar", gn4(), ctx=j)

                        def gn5(j=j, st=st):
                            return lambda e: e.tensor_scalar(st[:, 3:4], st[:, 2:3], 1.0 / 64.0, 1e-6, ALU.mult, ALU.add)
                        sp.op("vector", gn5(), ctx=j)

                        def gn6(j=j, st=st):
                            return lambda e: e.activation(st[:, 3:4], st[:, 3:4], AF.Sqrt)
                        sp.op("scalar", gn6(), ctx=j)

                        def gn7(j=j, st=st):
                            return lambda e: e.reciprocal(st[:, 3:4], st[:, 3:4])
                        sp.op("vector", gn7(), ctx=j)

                        def gn8(j=j, st=st):
                            return lambda e: e.tensor_scalar(osb7c[j], osb7c[j], st[:, 3:4], None, ALU.mult)
                        sp.op("vector", gn8(), ctx=j)

                        def drt(h=h, r0=r0, j=j):
                            return lambda e: e.dma_start(ret_s[r0:r0 + 128, 64 * h:64 * (h + 1)], osb7c[j])
                        sp.op(qs, drt(), inc=16, ctx=j)
            sp.ctx_close()

        ph7c()

        def ph8():
            A.reset()
            wo2_bf = A.bf16([128, 8, 1024])
            wout_bf = A.bf16([128, 22, 1024])
            wcb_a = A.bf16([128, 8, 128]); wcb_b = A.bf16([128, 8, 128])
            stage = A.f32([128, 4096])
            binaf_sb = A.f32([128, 22]); binbf_sb = A.f32([128, 22])
            lw_b = A.f32([128, 1024]); lb_b = A.f32([128, 1024]); bo_b = A.f32([128, 1024])
            brow = A.f32([1, 1024])
            rt = A.f32([128, 1024]); g2t = A.f32([128, 1024]); mo32 = A.f32([128, 1024])
            moT = A.bf16([128, 8, 128])
            t4 = A.f32([128, 4, 1024])
            u = A.f32([128, 1024]); uT = A.bf16([128, 8, 512])
            att = A.f32([128, 512]); sbt = A.f32([128, 512])
            swfm = A.bf16([128, 22, 512])
            xc = A.f32([128, 1024]); sq = A.f32([128, 1024])
            ysb = A.f32([128, 1024]); xt = A.f32([128, 1024])
            winar = wina.rearrange("(c p) n -> p c n", p=128)
            winbr = winb.rearrange("(c p) n -> p c n", p=128)
            emit_load_w(sp, wo2_bf, wo2, D, 1024, stage, 4096)
            emit_load_w(sp, wout_bf, wout, 2816, 1024, stage, 4096)
            sp.dma(binaf_sb, binaf)
            sp.dma(binbf_sb, binbf)
            for row, dst in ((ln2w, lw_b), (ln2b, lb_b), (bout, bo_b)):
                def d(row=row):
                    return lambda e: e.dma_start(brow[0:1, :], row[None, :])
                sp.op("sync", d(row), inc=16)
                emit_bcast(sp, dst, brow, ones1, psA, 1024)
            for qb in range(4):
                q0 = 512 * qb
                for qt in range(4):
                    r0 = q0 + 128 * qt
                    sp.dma(rt, ret_s[r0:r0 + 128, :])
                    sp.dma(g2t, gate2_s[r0:r0 + 128, :])
                    sp.op("vector", lambda e: e.tensor_mul(mo32, rt, g2t))
                    emit_transposes(sp, moT, mo32, idn_sb, psA, 8)
                    emit_proj_tm(sp, psB, moT, wo2_bf, 8, 1024)
                    sp.dma(xt, xn2_s[r0:r0 + 128, :])

                    def ti(qt=qt):
                        return lambda e: e.tensor_add(t4[:, qt, :], psB[:, 0:1024], xt)
                    sp.op("vector", ti(qt))
                    emit_ln(sp, t4[:, qt, :], xc, sq, stat, D)
                    sp.op("vector", lambda e: e.tensor_scalar(u, xc, stat[:, 3:4], None, ALU.mult))
                    sp.op("vector", lambda e: e.tensor_mul(u, u, lw_b))
                    sp.op("vector", lambda e: e.tensor_add(u, u, lb_b))
                    for c in range(8):
                        def tr(c=c):
                            return lambda e: e.transpose(psA[:, 0:128], u[:, 128 * c:128 * (c + 1)], idn_sb)
                        sp.op("tensor", tr(c))

                        def cp(c=c, qt=qt):
                            return lambda e: e.tensor_copy(uT[:, c, 128 * qt:128 * (qt + 1)], psA[:, 0:128])
                        sp.op("vector", cp(c, qt))
                for fc in range(22):
                    def da(fc=fc):
                        return lambda e: e.dma_start(stage[:, 0:1024], winar[:, :, 128 * fc:128 * (fc + 1)])
                    sp.op("sync", da(fc), inc=16)
                    sp.op("vector", lambda e: e.tensor_copy(wcb_a.rearrange("p c n -> p (c n)"), stage[:, 0:1024]))

                    def db(fc=fc):
                        return lambda e: e.dma_start(stage[:, 0:1024], winbr[:, :, 128 * fc:128 * (fc + 1)])
                    sp.op("sync", db(fc), inc=16)
                    sp.op("vector", lambda e: e.tensor_copy(wcb_b.rearrange("p c n -> p (c n)"), stage[:, 0:1024]))
                    for k in range(8):
                        def ma(k=k):
                            return lambda e: e.matmul(psA[:, 0:512], wcb_a[:, k, :], uT[:, k, :], start=(k == 0), stop=(k == 7))
                        sp.op("tensor", ma(k))

                        def mb_(k=k):
                            return lambda e: e.matmul(psA[:, 512:1024], wcb_b[:, k, :], uT[:, k, :], start=(k == 0), stop=(k == 7))
                        sp.op("tensor", mb_(k))

                    def eva(fc=fc):
                        return lambda e: e.activation(att, psA[:, 0:512], AF.Identity, bias=binaf_sb[:, fc:fc + 1])
                    sp.op("scalar", eva(fc))

                    def evb(fc=fc):
                        return lambda e: e.activation(sbt, psA[:, 512:1024], AF.Silu, bias=binbf_sb[:, fc:fc + 1])
                    sp.op("scalar", evb(fc))

                    def mulab(fc=fc):
                        return lambda e: e.tensor_mul(swfm[:, fc, :], att, sbt)
                    sp.op("vector", mulab(fc))
                for qt in range(4):
                    ps = psA if qt < 2 else psB
                    off = 1024 * (qt % 2)
                    for fc in range(22):
                        for s in range(2):
                            def my(fc=fc, s=s, qt=qt, ps=ps, off=off):
                                return lambda e: e.matmul(ps[:, off + 512 * s:off + 512 * (s + 1)], swfm[:, fc, 128 * qt:128 * (qt + 1)], wout_bf[:, fc, 512 * s:512 * (s + 1)], start=(fc == 0), stop=(fc == 21))
                            sp.op("tensor", my(fc, s, qt, ps, off))
                    r0 = q0 + 128 * qt

                    def ev1(ps=ps, off=off):
                        return lambda e: e.tensor_add(ysb, ps[:, off:off + 1024], bo_b)
                    sp.op("vector", ev1(ps, off))

                    def ev2(qt=qt):
                        return lambda e: e.tensor_add(ysb, ysb, t4[:, qt, :])
                    sp.op("vector", ev2(qt))
                    sp.dma(xt, x2_s[r0:r0 + 128, :])
                    sp.op("vector", lambda e: e.tensor_add(ysb, ysb, xt))
                    sp.dma(x3_s[r0:r0 + 128, :], ysb)
                    emit_ln(sp, ysb, xc, sq, stat, D)
                    sp.op("vector", lambda e: e.tensor_scalar(xt, xc, stat[:, 3:4], None, ALU.mult))
                    sp.dma(xn3_s[r0:r0 + 128, :], xt)

        ph8()

        def ph9():
            A.reset()
            xlb = [A.bf16([128, 8, 512]), A.bf16([128, 8, 512])]
            w1_bf = A.bf16([128, 8, 4096])
            w2_bf = A.bf16([128, 32, 1024])
            h_bf = A.bf16([128, 32, 512])
            stage = A.f32([128, 1024])
            stage2 = A.f32([128, 1024])
            wr_bf = A.bf16([128, 8, 8])
            br_b = A.f32([128, 8])
            lg_b = A.bf16([128, 1024]); lnb_b = A.bf16([128, 1024])
            b1f = A.f32([128, 32]); b2bb = A.bf16([128, 1024])
            g8 = A.f32([128, 16, 8]); gates = A.f32([128, 16])
            pr = A.f32([128, 8]); e8 = A.f32([128, 8])
            ysb = A.f32([128, 1024])
            hflat = h_bf.rearrange("p a b -> p (a b)")
            hv = hflat.bitcast(F32)
            xt = hv[:, 0:1024]; xc = hv[:, 1024:2048]; sq = hv[:, 2048:3072]
            u = sq
            xnT = hflat[:, 10240:11264].rearrange("p (c t) -> p c t", t=128)
            x0flat = xlb[0].rearrange("p a b -> p (a b)")
            x0v = x0flat.bitcast(F32)
            x1flat = xlb[1].rearrange("p a b -> p (a b)")
            x1v = x1flat.bitcast(F32)
            xt2 = [xt, x0v[:, 0:1024]]
            xc2 = [xc, x0v[:, 1024:2048]]
            sq2 = [sq, x1v[:, 0:1024]]
            xnT2 = [xnT, x1flat[:, 2048:3072].rearrange("p (c t) -> p c t", t=128)]
            pr2 = [pr, A.f32([128, 8])]
            e82 = [e8, A.f32([128, 8])]
            brow = stage
            zt = ysb
            xl_r = xl_s.rearrange("(c p) t -> p c t", p=128)
            w1r = w1b.rearrange("e (c p) n -> e p c n", p=128)
            w2r = w2b.rearrange("e (c p) n -> e p c n", p=128)
            emit_load_w(sp, wr_bf, wr9, D, 8, stage, 1024)
            sp.dma(brow[0:1, 0:8], br9[None, :])
            emit_bcast(sp, br_b, brow, ones1, psA, 8)
            for row, dst in ((lng, lg_b), (lnb9, lnb_b)):
                def d(row=row):
                    return lambda e: e.dma_start(brow[0:1, :], row[None, :])
                sp.op("sync", d(row), inc=16)
                emit_bcast(sp, dst, brow, ones1, psA, 1024)
            sp.op("vector", lambda e: e.memset(zt, 0.0))
            for i in range(NT):
                def dz(i=i):
                    return lambda e: e.dma_start(accd[128 * i:128 * (i + 1), :], zt)
                sp.op("sync", dz(i), inc=16)
            sp.ctx_open(2)
            for i in range(NT):
                r0 = 128 * i
                j = i % 2
                qm = "sync" if j == 0 else "gpsimd"
                qs = "gpsimd" if j == 0 else "sync"
                psa_j = psA[:, 512 * j:512 * j + 128]
                st = stat[:, 16 * j:16 * j + 8]

                def dxt(i=i, j=j, r0=r0):
                    return lambda e: e.dma_start(xt2[j], xn3_s[r0:r0 + 128, :])
                sp.op(qm, dxt(), inc=16, ctx=j)
                for c in range(8):
                    def tr0(c=c, j=j, psa_j=psa_j):
                        return lambda e: e.transpose(psa_j, xt2[j][:, 128 * c:128 * (c + 1)], idn_sb)
                    sp.op("tensor", tr0(), ctx=j)

                    def cp0(c=c, j=j, psa_j=psa_j):
                        return lambda e: e.tensor_copy(xnT2[j][:, c, :], psa_j)
                    sp.op("vector", cp0(), ctx=j)
                for c in range(8):
                    def mr(c=c, j=j):
                        return lambda e: e.matmul(psB[:, 512 * j:512 * j + 8], xnT2[j][:, c, :], wr_bf[:, c, :], start=(c == 0), stop=(c == 7))
                    sp.op("tensor", mr(), ctx=j)

                def v1(j=j):
                    return lambda e: e.tensor_add(pr2[j], psB[:, 512 * j:512 * j + 8], br_b)
                sp.op("vector", v1(), ctx=j)

                def v2(j=j, st=st):
                    return lambda e: e.reduce_max(st[:, 0:1], pr2[j], axis=AX.X, negate=True)
                sp.op("vector", v2(), ctx=j)

                def s1(j=j, st=st):
                    return lambda e: e.activation(e82[j], pr2[j], AF.Exp, bias=st[:, 0:1], accum_out=st[:, 1:2])
                sp.op("scalar", s1(), ctx=j)

                def v3(j=j, st=st):
                    return lambda e: e.reciprocal(st[:, 2:3], st[:, 1:2])
                sp.op("vector", v3(), ctx=j)

                def v4(j=j, st=st):
                    return lambda e: e.tensor_scalar(pr2[j], e82[j], st[:, 2:3], None, ALU.mult)
                sp.op("vector", v4(), ctx=j)

                def gmax(i=i, j=j):
                    return lambda e: e.reduce_max(gates[:, i:i + 1], pr2[j], axis=AX.X)
                sp.op("vector", gmax(), ctx=j)

                def gmask(i=i, j=j):
                    return lambda e: e.tensor_scalar(g8[:, i, :], pr2[j], gates[:, i:i + 1], None, ALU.is_ge)
                sp.op("vector", gmask(), ctx=j)
                emit_ln(sp, xt2[j], xc2[j], sq2[j], stat, D, c0=16 * j + 8, ctx=j)

                def v5(j=j):
                    return lambda e: e.tensor_scalar(sq2[j], xc2[j], stat[:, 16 * j + 11:16 * j + 12], None, ALU.mult)
                sp.op("vector", v5(), ctx=j)

                def v6(j=j):
                    return lambda e: e.tensor_mul(sq2[j], sq2[j], lg_b)
                sp.op("vector", v6(), ctx=j)

                def v7(j=j):
                    return lambda e: e.tensor_add(sq2[j], sq2[j], lnb_b)
                sp.op("vector", v7(), ctx=j)
                for c in range(8):
                    def tr1(c=c, j=j, psa_j=psa_j):
                        return lambda e: e.transpose(psa_j, sq2[j][:, 128 * c:128 * (c + 1)], idn_sb)
                    sp.op("tensor", tr1(), ctx=j)

                    def cp1(c=c, j=j, psa_j=psa_j):
                        return lambda e: e.tensor_copy(xnT2[j][:, c, :], psa_j)
                    sp.op("vector", cp1(), ctx=j)

                def stx(i=i, j=j):
                    return lambda e: e.dma_start(xl_r[:, :, 128 * i:128 * (i + 1)], xnT2[j][:])
                sp.op(qs, stx(), inc=16, ctx=j)
            sp.ctx_close()
            e_prolog = list(sp.now().items())
            e_xl_store = ("sync", sp.cnt["sync"])
            e_hlast = ("tensor", 0)
            e_ylast = ("tensor", 0)
            e_rel_last = ("scalar", 0)
            e_b2mm = ("tensor", sp.cnt["tensor"])
            e_evac = {qt: ("vector", 0) for qt in range(4)}
            e_stg = [("vector", 0), ("vector", 0)]
            e_b2cp = ("vector", sp.cnt["vector"])
            e_acdma = ("gpsimd", sp.cnt["gpsimd"])
            ge = {}
            e_xlb_war = [("tensor", 0), ("tensor", 0)]
            e_xlb_load = [("sync", 0), ("sync", 0)]
            for ex in range(8):
                pieces = []
                for c in range(8):
                    for hf in range(4):
                        pieces.append(("w1", c, hf))
                for c in range(32):
                    pieces.append(("w2", c, 0))
                w1c = 0; w2c = 0
                for pi, (kind, c, hf) in enumerate(pieces):
                    par = pi % 2
                    stg = stage if par == 0 else stage2
                    if kind == "w1":
                        def dd(ex=ex, c=c, hf=hf, stg=stg):
                            return lambda e: e.dma_start(stg, w1r[ex, :, c, 1024 * hf:1024 * (hf + 1)])
                        war = e_hlast
                    else:
                        def dd(ex=ex, c=c, stg=stg):
                            return lambda e: e.dma_start(stg, w2r[ex, :, c, :])
                        war = e_ylast
                    e_d = sp.op("sync", dd(), inc=16, deps=[e_stg[par]])
                    if kind == "w1":
                        def cc(c=c, hf=hf, stg=stg):
                            return lambda e: e.tensor_copy(w1_bf[:, c, 1024 * hf:1024 * (hf + 1)], stg)
                    else:
                        def cc(c=c, stg=stg):
                            return lambda e: e.tensor_copy(w2_bf[:, c, :], stg)
                    e_c = sp.op("vector", cc(), deps=[e_d, war])
                    e_stg[par] = e_c
                    if kind == "w1":
                        w1c = e_c[1]
                    else:
                        w2c = e_c[1]
                e_w1cast = ("vector", w1c)
                e_w2cast = ("vector", w2c)
                e_b1 = sp.op("sync", (lambda ex=ex: (lambda e: e.dma_start(b1f, b1fmb[ex])))(), inc=16, deps=[e_rel_last])
                e_b2r = sp.op("sync", (lambda ex=ex: (lambda e: e.dma_start(brow[0:1, :], b2b9[ex:ex + 1, :])))(), inc=16, deps=[e_b2mm, e_stg[0]])
                e_b2mm = sp.op("tensor", lambda e: e.matmul(psA[:, 1536:2048], ones1, brow[0:1, 0:512], start=True, stop=True), deps=[e_b2r, e_evac[2], e_evac[3], e_b2cp])
                e_b2mm2 = sp.op("tensor", lambda e: e.matmul(psB[:, 1536:2048], ones1, brow[0:1, 512:1024], start=True, stop=True), deps=[e_b2r, e_evac[1], e_b2cp])
                e_c1 = sp.op("vector", lambda e: e.tensor_copy(b2bb[:, 0:512], psA[:, 1536:2048]), deps=[e_b2mm, e_evac[0]])
                e_b2cp = sp.op("vector", lambda e: e.tensor_copy(b2bb[:, 512:1024], psB[:, 1536:2048]), deps=[e_b2mm2])
                e_stg[0] = e_b2mm2
                for qb in range(4):
                    par = qb % 2

                    def ldx(ex=ex, qb=qb, par=par):
                        return lambda e: e.dma_start(xlb[par][:], xl_r[:, :, 512 * qb:512 * (qb + 1)])
                    e_xlb_load[par] = sp.op("sync", ldx(), inc=16, deps=[e_xlb_war[par]] + e_prolog)
                    e_y_prev_qb = e_ylast
                    for fc in range(32):
                        bank = psA[:, 512 * (fc % 2):512 * (fc % 2 + 1)]
                        if fc >= 2:
                            bank_war = [ge.get(fc - 2, ("scalar", 0))]
                        else:
                            bank_war = [ge.get(fc + 30, ("scalar", 0)), e_evac[2], e_evac[3]]
                        dl = [e_w1cast, e_xlb_load[par], e_b2cp] + bank_war
                        for k in range(8):
                            def mh(fc=fc, k=k, qb=qb, bank=bank, par=par):
                                return lambda e: e.matmul(bank, w1_bf[:, k, 128 * fc:128 * (fc + 1)], xlb[par][:, k, :], start=(k == 0), stop=(k == 7))
                            e_hlast = sp.op("tensor", mh(), deps=dl)

                        def rel(fc=fc, bank=bank):
                            return lambda e: e.activation(h_bf[:, fc, :], bank, AF.Relu, bias=b1f[:, fc:fc + 1])
                        ge[fc] = sp.op("scalar", rel(), deps=[e_hlast, e_y_prev_qb, e_b1])
                    e_xlb_war[par] = e_hlast
                    e_rel_last = ge[31]
                    for qt in range(4):
                        ps = psB if qt < 2 else psA
                        off = 1024 * (qt % 2)
                        dl = [e_w2cast, e_evac[qt], ge[31], e_b2cp]
                        for fc in range(32):
                            dl2 = dl + [ge[fc]]
                            for s in range(2):
                                def my(fc=fc, s=s, qt=qt, ps=ps, off=off):
                                    return lambda e: e.matmul(ps[:, off + 512 * s:off + 512 * (s + 1)], h_bf[:, fc, 128 * qt:128 * (qt + 1)], w2_bf[:, fc, 512 * s:512 * (s + 1)], start=(fc == 0), stop=(fc == 31))
                                e_ylast = sp.op("tensor", my(), deps=dl2)

                        def ev(ps=ps, off=off):
                            return lambda e: e.tensor_add(ysb, ps[:, off:off + 1024], b2bb)
                        e_ev = sp.op("vector", ev(), deps=[e_ylast, e_acdma])
                        qg = 4 * qb + qt

                        def mm_(ex=ex, qg=qg):
                            return lambda e: e.tensor_scalar(ysb, ysb, g8[:, qg, ex:ex + 1], None, ALU.mult)
                        e_mm = sp.op("vector", mm_(), deps=[])
                        e_evac[qt] = e_mm

                        def dacc(qg=qg):
                            return lambda e: e.dma_start(accd[128 * qg:128 * (qg + 1), :], ysb, accum_op=ALU.add)
                        e_acdma = sp.op("gpsimd", dacc(), inc=16, deps=[e_mm])
            for i in range(NT):
                r0 = 128 * i
                sp.dma(xt, accd[r0:r0 + 128, :])

                def fg(i=i):
                    return lambda e: e.tensor_scalar(xt, xt, gates[:, i:i + 1], None, ALU.mult)
                sp.op("vector", fg(i))
                sp.dma(xc, x3_s[r0:r0 + 128, :])
                sp.op("vector", lambda e: e.tensor_add(xt, xt, xc))
                sp.dma(ox4[r0:r0 + 128, :], xt)

        ph9()

        sp.emit(block)
    return nc


_NC_CACHE = {}


def _f32(a):
    return np.ascontiguousarray(np.asarray(a, dtype=np.float32))


def host_maps(inp):
    B = 2
    x = _f32(inp["L0_attn_x"]).reshape(B * T, D)
    idn = np.eye(128, dtype=np.float32)
    triu = np.triu(np.ones((128, 128), np.float32))
    i16 = np.eye(16, dtype=np.float32)

    qw = _f32(inp["L0_ffn_q_w"]); qb = _f32(inp["L0_ffn_q_b"])
    cols = np.concatenate([np.arange(64 * h, 64 * h + 32) for h in range(16)])
    qwe = np.ascontiguousarray(qw[:, cols]); qbe = np.ascontiguousarray(qb[cols])
    owf = np.ascontiguousarray(_f32(inp["L0_ffn_o_w"])[:512])

    b1a = _f32(inp["L0_ffn_b1"])
    b1fma = np.ascontiguousarray(b1a.reshape(8, 16, 128).transpose(0, 2, 1))
    b1b = _f32(inp["L1_ffn_b1"])
    b1fmb = np.ascontiguousarray(b1b.reshape(8, 32, 128).transpose(0, 2, 1))

    win = _f32(inp["L1_attn_w_in"]); bin_ = _f32(inp["L1_attn_b_in"])
    wina = np.zeros((D, 2816), np.float32); wina[:, :2730] = win[:, :2730]
    winb = np.zeros((D, 2816), np.float32); winb[:, :2730] = win[:, 2730:]
    binaf = np.zeros(2816, np.float32); binaf[:2730] = bin_[:2730]
    binbf = np.zeros(2816, np.float32); binbf[:2730] = bin_[2730:]
    binaf = np.ascontiguousarray(binaf.reshape(22, 128).T)
    binbf = np.ascontiguousarray(binbf.reshape(22, 128).T)
    woutp = np.zeros((2816, 1024), np.float32)
    woutp[:2730] = _f32(inp["L1_attn_w_out"])

    gamma = (1.0 - np.exp2(-5.0 - np.arange(16))).astype(np.float64)
    tt = np.arange(128, dtype=np.float64)
    rgq = np.stack([(gamma[h] ** (tt + 1)) for h in range(16)], 1).astype(np.float32)
    rgk2 = np.stack([(gamma[h] ** (-(tt + 1))) for h in range(16)], 1).astype(np.float32)
    rgkh = np.stack([(gamma[h] ** (127 - tt)) for h in range(16)], 1).astype(np.float32)
    rgs = (gamma ** 128).astype(np.float32)[None, :]
    rgcc = np.zeros((64, 256), np.float32)
    for h in range(16):
        for c in range(16):
            rgcc[:, 16 * h + c] = gamma[h] ** (128 * c)

    shared = dict(
        wq=_f32(inp["L0_attn_Wq"]), wk=_f32(inp["L0_attn_Wk"]),
        wv=_f32(inp["L0_attn_Wv"]), wg=_f32(inp["L0_attn_Wg"]),
        gkw1=_f32(inp["L0_attn_gk_w1"]), gkw2=_f32(inp["L0_attn_gk_w2"]),
        gkb=_f32(inp["L0_attn_gk_b"]), rmsw=_f32(inp["L0_attn_rms_w"]),
        wo=_f32(inp["L0_attn_Wo"]),
        qwe=qwe, qbe=qbe,
        kwf=_f32(inp["L0_ffn_k_w"]), kbf=_f32(inp["L0_ffn_k_b"]),
        vwf=_f32(inp["L0_ffn_v_w"]), vbf=_f32(inp["L0_ffn_v_b"]),
        owe=owf, obv=_f32(inp["L0_ffn_o_b"]),
        rw=_f32(inp["L0_ffn_router_w"]), rb=_f32(inp["L0_ffn_router_b"]),
        w1a=_f32(inp["L0_ffn_w1"]), b1fma=b1fma,
        w2a=_f32(inp["L0_ffn_w2"]), b2a=_f32(inp["L0_ffn_b2"]),
        ln1w=_f32(inp["L1_attn_ln1_w"]), ln1b=_f32(inp["L1_attn_ln1_b"]),
        wq2=_f32(inp["L1_attn_wq"]), wk2=_f32(inp["L1_attn_wk"]),
        wv2=_f32(inp["L1_attn_wv"]), wg2=_f32(inp["L1_attn_wg"]),
        wo2=_f32(inp["L1_attn_wo"]),
        ln2w=_f32(inp["L1_attn_ln2_w"]), ln2b=_f32(inp["L1_attn_ln2_b"]),
        wina=wina, winb=winb, binaf=binaf, binbf=binbf,
        wout=woutp, bout=_f32(inp["L1_attn_b_out"]),
        wr9=_f32(inp["L1_ffn_Wr"]), br9=_f32(inp["L1_ffn_br"]),
        lng=_f32(inp["L1_ffn_ln_g"]), lnb9=_f32(inp["L1_ffn_ln_b"]),
        w1b=_f32(inp["L1_ffn_W1"]), b1fmb=b1fmb,
        w2b=_f32(inp["L1_ffn_W2"]), b2b9=_f32(inp["L1_ffn_b2"]),
        idn=idn, triu=triu, i16=i16,
        rgq=rgq, rgk2=rgk2, rgkh=rgkh, rgs=rgs, rgcc=rgcc,
    )
    maps = []
    for c in range(NCORE):
        r = c % 4
        msel = np.zeros((4, 4), np.float64)
        sel = np.zeros(4, np.float32)
        for j in range(4):
            if j < r:
                sel[j] = 1.0
                for l in range(j + 1, r):
                    msel[j, l] = 1.0
        gmsel = np.broadcast_to(msel.reshape(1, 16).astype(np.float32), (128, 16)).copy()
        gsel = np.broadcast_to(sel[None, :], (128, 4)).copy()
        rccoef = np.zeros((64, 64), np.float32)
        for h in range(16):
            for j in range(4):
                if j < r:
                    rccoef[:, 4 * h + j] = gamma[h] ** (2048.0 * (r - 1 - j))
        m = dict(shared)
        m["x"] = x[NTOK * c:NTOK * (c + 1)]
        m["gmsel"] = gmsel
        m["gsel"] = gsel
        m["rccoef"] = rccoef
        maps.append(m)
    return maps


def kernel(**inp):
    if "nc" not in _NC_CACHE:
        _NC_CACHE["nc"] = build_fused(debug=False)
    nc = _NC_CACHE["nc"]
    maps = host_maps(inp)
    res = run_bass_kernel_spmd(nc, maps, core_ids=list(range(NCORE)))
    x4 = np.concatenate([r["ox4"] for r in res.results], 0)
    return x4.reshape(2, T, D).astype(np.float32)


# version 12 (final)
# speedup vs baseline: 1.0264x; 1.0264x over previous
import numpy as np
import ml_dtypes
import concourse.bass as bass
import concourse.mybir as mybir
from concourse.bass_utils import run_bass_kernel_spmd

F32 = mybir.dt.float32
BF16 = mybir.dt.bfloat16
AF = mybir.ActivationFunctionType
ALU = mybir.AluOpType
AX = mybir.AxisListType

NCORE = 8
NTOK = 2048
NT = 16
D = 1024
T = 8192
RS32 = 1.0 / np.sqrt(32.0)
RG = [[0, 1, 2, 3], [4, 5, 6, 7]]


class SP:

    ENGS = ["sync", "scalar", "vector", "tensor", "gpsimd"]

    def __init__(self, nc, sems):
        self.nc = nc
        self.sems = sems
        self.plan = []
        self.cnt = {e: 0 for e in self.ENGS}
        self.ctxc = None

    def now(self):
        return dict(self.cnt)

    def ctx_open(self, n):
        self.ctxc = [dict(self.cnt) for _ in range(n)]

    def ctx_close(self):
        self.ctxc = None

    def op(self, eng, fn, inc=1, deps=None, ctx=None):
        if ctx is not None:
            base = list(self.ctxc[ctx].items())
            deps = base if deps is None else list(deps) + base
        if deps is None:
            deps = [(F, self.cnt[F]) for F in self.ENGS
                    if self.cnt[F] > 0 and not (F == eng == "tensor")]
        else:
            dd = {}
            for (F, c) in deps:
                if c > 0 and not (F == eng == "tensor"):
                    dd[F] = max(dd.get(F, 0), c)
            deps = list(dd.items())
        self.plan.append((eng, fn, deps, inc))
        self.cnt[eng] += inc
        if ctx is not None:
            self.ctxc[ctx][eng] = self.cnt[eng]
        return (eng, self.cnt[eng])

    def dma(self, out, in_, eng="sync", deps=None, ctx=None, **kw):
        return self.op(eng, lambda e: e.dma_start(out, in_, **kw), inc=16, deps=deps, ctx=ctx)

    def emit(self, block):
        for eng_name in self.ENGS:
            items = [(f, d, i) for (e, f, d, i) in self.plan if e == eng_name]
            if not items:
                continue

            def make(items, eng_name):
                def body(eng):
                    g = {}
                    for fn, deps, inc in items:
                        for (F, c) in deps:
                            if g.get(F, 0) < c:
                                eng.wait_ge(self.sems[F], c)
                                g[F] = c
                        fn(eng).then_inc(self.sems[eng_name], inc)
                return body

            getattr(block, eng_name)(make(items, eng_name))
        fin_counts = dict(self.cnt)

        def fin(eng):
            for F in self.ENGS:
                if fin_counts[F] > 0 and F != "scalar":
                    eng.wait_ge(self.sems[F], fin_counts[F])

        block.scalar(fin)


class Arena:

    def __init__(self, nc, n_bf16):
        self.h = nc.alloc_sbuf_tensor("arena", [128, n_bf16], BF16)
        self.n = n_bf16
        self.off = 0

    def reset(self):
        self.off = 0

    def _carve(self, units):
        o = self.off
        assert o + units <= self.n, f"arena overflow: {o}+{units} > {self.n}"
        self.off = o + units
        return o

    @staticmethod
    def _shape(ap, shape):
        if len(shape) == 3:
            ap = ap.rearrange("p (a b) -> p a b", b=shape[2])
        elif len(shape) == 4:
            ap = ap.rearrange("p (a b c) -> p a b c", b=shape[2], c=shape[3])
        if shape[0] != 128:
            ap = ap[0:shape[0]]
        return ap

    def bf16(self, shape):
        free = int(np.prod(shape[1:]))
        o = self._carve((free + 15) // 16 * 16)
        return self._shape(self.h.ap()[:, o:o + free], shape)

    def f32(self, shape):
        free = int(np.prod(shape[1:]))
        o = self._carve((2 * free + 15) // 16 * 16)
        return self._shape(self.h.ap()[:, o:o + 2 * free].bitcast(F32), shape)


def emit_ln(sp, x_ap, xc, sq, stat, Dd, eps=1e-5, c0=0, ctx=None):
    sp.op("vector", lambda e: e.reduce_sum(stat[:, c0:c0 + 1], x_ap, axis=AX.X), ctx=ctx)
    sp.op("vector", lambda e: e.tensor_scalar_mul(stat[:, c0 + 1:c0 + 2], stat[:, c0:c0 + 1], 1.0 / Dd), ctx=ctx)
    sp.op("vector", lambda e: e.tensor_scalar(xc, x_ap, stat[:, c0 + 1:c0 + 2], None, ALU.subtract), ctx=ctx)
    sp.op("scalar", lambda e: e.activation(sq, xc, AF.Square, accum_out=stat[:, c0 + 2:c0 + 3]), ctx=ctx)
    sp.op("vector", lambda e: e.tensor_scalar(stat[:, c0 + 3:c0 + 4], stat[:, c0 + 2:c0 + 3], 1.0 / Dd, eps, ALU.mult, ALU.add), ctx=ctx)
    sp.op("scalar", lambda e: e.activation(stat[:, c0 + 3:c0 + 4], stat[:, c0 + 3:c0 + 4], AF.Sqrt), ctx=ctx)
    sp.op("vector", lambda e: e.reciprocal(stat[:, c0 + 3:c0 + 4], stat[:, c0 + 3:c0 + 4]), ctx=ctx)


def emit_transposes(sp, dst_bf, src_f32, idn, ps, nchunk, ctx=None):
    for c in range(nchunk):
        def tr(c=c):
            return lambda e: e.transpose(ps[:, 0:128], src_f32[:, 128 * c:128 * (c + 1)], idn)
        sp.op("tensor", tr(c), ctx=ctx)

        def cp(c=c):
            return lambda e: e.tensor_copy(dst_bf[:, c, :], ps[:, 0:128])
        sp.op("vector", cp(c), ctx=ctx)


def emit_bcast(sp, dst, src_row, ones1, ps, N):
    nslab = (N + 511) // 512
    for s in range(nslab):
        lo, hi = 512 * s, min(512 * (s + 1), N)

        def mm(lo=lo, hi=hi):
            return lambda e: e.matmul(ps[:, lo:hi], ones1[:], src_row[0:1, lo:hi], start=True, stop=True)
        sp.op("tensor", mm(lo, hi))
    sp.op("vector", lambda e: e.tensor_copy(dst, ps[:, 0:N]))


def emit_load_w(sp, dst_bf, w_dram, din, dout, stage, stage_cols):
    nch = din // 128
    wr = w_dram.rearrange("(c p) n -> p c n", p=128)
    per = max(1, stage_cols // dout)
    c = 0
    while c < nch:
        cc = min(per, nch - c)

        def d(c=c, cc=cc):
            return lambda e: e.dma_start(stage[:, 0:cc * dout], wr[:, c:c + cc, :])
        sp.op("sync", d(c, cc), inc=16)

        def cp(c=c, cc=cc):
            return lambda e: e.tensor_copy(dst_bf[:, c:c + cc, :], stage[:, 0:cc * dout].rearrange("p (c n) -> p c n", n=dout))
        sp.op("vector", cp(c, cc))
        c += cc


def emit_proj_tm(sp, ps_view, lhsT_bf, rhs_w_bf, nk, dout, ctx=None, coff=0):
    nslab = (dout + 511) // 512
    for s in range(nslab):
        lo = 512 * s
        hi = min(lo + 512, dout)
        for k in range(nk):
            def mm(k=k, lo=lo, hi=hi):
                return lambda e: e.matmul(
                    ps_view[:, lo:hi], lhsT_bf[:, k, :], rhs_w_bf[:, k, coff + lo:coff + hi],
                    start=(k == 0), stop=(k == nk - 1))
            sp.op("tensor", mm(k, lo, hi), ctx=ctx)


def build_fused(debug=False):
    nc = bass.Bass("TRN2", target_bir_lowering=False)

    def inp(name, shape):
        return nc.dram_tensor(name, shape, F32, kind="ExternalInput").ap()

    x = inp("x", [NTOK, D])
    wq = inp("wq", [D, 1024]); wk = inp("wk", [D, 1024])
    wv = inp("wv", [D, 2048]); wg = inp("wg", [D, 2048])
    gkw1 = inp("gkw1", [D, 16]); gkw2 = inp("gkw2", [16, 1024]); gkb = inp("gkb", [D])
    rmsw = inp("rmsw", [512])
    wo = inp("wo", [2048, 1024])
    qwe = inp("qwe", [D, 512]); qbe = inp("qbe", [512])
    kwf = inp("kwf", [D, 512]); kbf = inp("kbf", [512])
    vwf = inp("vwf", [D, 512]); vbf = inp("vbf", [512])
    owe = inp("owe", [512, 1024]); obv = inp("obv", [1024])
    rw = inp("rw", [D, 8]); rb = inp("rb", [8])
    w1a = inp("w1a", [8, 1024, 2048]); b1fma = inp("b1fma", [8, 128, 16])
    w2a = inp("w2a", [8, 2048, 1024]); b2a = inp("b2a", [8, 1024])
    ln1w = inp("ln1w", [D]); ln1b = inp("ln1b", [D])
    wq2 = inp("wq2", [D, D]); wk2 = inp("wk2", [D, D])
    wv2 = inp("wv2", [D, D]); wg2 = inp("wg2", [D, D])
    wo2 = inp("wo2", [D, D])
    ln2w = inp("ln2w", [D]); ln2b = inp("ln2b", [D])
    wina = inp("wina", [D, 2816]); winb = inp("winb", [D, 2816])
    binaf = inp("binaf", [128, 22]); binbf = inp("binbf", [128, 22])
    wout = inp("wout", [2816, 1024]); bout = inp("bout", [1024])
    wr9 = inp("wr9", [D, 8]); br9 = inp("br9", [8])
    lng = inp("lng", [D]); lnb9 = inp("lnb9", [D])
    w1b = inp("w1b", [8, 1024, 4096]); b1fmb = inp("b1fmb", [8, 128, 32])
    w2b = inp("w2b", [8, 4096, 1024]); b2b9 = inp("b2b9", [8, 1024])
    idn = inp("idn", [128, 128])
    triu = inp("triu", [128, 128])
    i16 = inp("i16", [16, 16])
    rgq = inp("rgq", [128, 16])
    rgk2 = inp("rgk2", [128, 16])
    rgkh = inp("rgkh", [128, 16])
    rgs = inp("rgs", [1, 16])
    rgcc = inp("rgcc", [64, 256])
    rccoef = inp("rccoef", [64, 64])
    gmsel = inp("gmsel", [128, 16])
    gsel = inp("gsel", [128, 4])

    ox4 = nc.dram_tensor("ox4", [NTOK, D], F32, kind="ExternalOutput").ap()

    kind_s = "ExternalOutput" if debug else "Internal"

    def scr(name, shape, dtype=F32):
        return nc.dram_tensor(name, shape, dtype, kind=kind_s).ap()

    q_s = scr("q_s", [NTOK, 1024])
    k_s = scr("k_s", [NTOK, 1024])
    v_s = scr("v_s", [NTOK, 2048])
    g_s = scr("g_s", [NTOK, 1024])
    gate_s = scr("gate_s", [NTOK, 2048])
    oraw_s = scr("oraw_s", [NTOK, 2048])
    qgT_s = scr("qgT_s", [1024, NTOK], BF16)
    on_s = scr("on_s", [NTOK, 2048])
    x1_s = scr("x1_s", [NTOK, D])
    xn1_s = scr("xn1_s", [NTOK, D])
    Qe_s = scr("Qe_s", [NTOK, 512])
    mla_s = scr("mla_s", [NTOK, 1024])
    comb_s = scr("comb_s", [NTOK, 8])
    x2_s = scr("x2_s", [NTOK, D])
    xn2_s = scr("xn2_s", [NTOK, D])
    q2_s = scr("q2_s", [NTOK, D])
    k2_s = scr("k2_s", [NTOK, D])
    v2_s = scr("v2_s", [NTOK, D])
    gate2_s = scr("gate2_s", [NTOK, D])
    o2raw_s = scr("o2raw_s", [NTOK, D])
    q2f_s = scr("q2f_s", [1024, NTOK], BF16)
    ret_s = scr("ret_s", [NTOK, D])
    x3_s = scr("x3_s", [NTOK, D])
    xn3_s = scr("xn3_s", [NTOK, D])
    accd = nc.dram_tensor("accd", [NTOK, D], F32, kind="Internal").ap()
    xl_s = nc.dram_tensor("xl_s", [1024, NTOK], BF16, kind="Internal").ap()

    cc_gs_in = nc.dram_tensor("cc_gs_in", [1026, 512], F32, kind="Internal").ap()
    cc_gs_out = nc.dram_tensor("cc_gs_out", [4104, 512], F32, kind="Internal").ap()
    cc_kv_in = nc.dram_tensor("cc_kv_in", [NTOK, 1024], BF16, kind="Internal").ap()
    cc_kv_out = nc.dram_tensor("cc_kv_out", [T, 1024], BF16, kind="Internal").ap()
    cc_rs_in = nc.dram_tensor("cc_rs_in", [1024, 64], F32, kind="Internal").ap()
    cc_rs_out = nc.dram_tensor("cc_rs_out", [4096, 64], F32, kind="Internal").ap()

    A = Arena(nc, 105088)
    idn_sb = nc.alloc_sbuf_tensor("idn_p", [128, 128], F32).ap()
    triu_sb = nc.alloc_sbuf_tensor("triu_p", [128, 128], F32).ap()
    ones1 = nc.alloc_sbuf_tensor("ones1_p", [1, 128], F32).ap()
    ones16 = nc.alloc_sbuf_tensor("ones16_p", [16, 128], F32).ap()
    i16_sb = nc.alloc_sbuf_tensor("i16_p", [16, 16], F32).ap()
    statp = nc.alloc_sbuf_tensor("stat_p", [128, 64], F32).ap()

    with nc.psum_tensor("psA", [128, 2048], F32) as psA, \
         nc.psum_tensor("psB", [128, 2048], F32) as psB, \
         nc.semaphore("S0") as S0, nc.semaphore("S1") as S1, \
         nc.semaphore("S2") as S2, nc.semaphore("S3") as S3, \
         nc.semaphore("S4") as S4, nc.Block() as block:
        sems = {"sync": S0, "scalar": S1, "vector": S2, "tensor": S3, "gpsimd": S4}
        sp = SP(nc, sems)
        stat = statp
        sp.op("vector", lambda e: e.memset(ones1, 1.0))
        sp.op("vector", lambda e: e.memset(ones16, 1.0))
        sp.dma(idn_sb, idn)
        sp.dma(triu_sb, triu)
        sp.dma(i16_sb, i16)

        def ph1():
            A.reset()
            wq_bf = A.bf16([128, 8, 1024]); wk_bf = A.bf16([128, 8, 1024])
            wv_bf = A.bf16([128, 8, 2048]); wg_bf = A.bf16([128, 8, 2048])
            gkw1_bf = A.bf16([128, 8, 16]); gkw2_bf = A.bf16([16, 1024])
            stage = A.f32([128, 8192])
            gkb_row = A.f32([1, 1024]); gkb_b = A.f32([128, 1024])
            xt = A.f32([128, 1024]); xc = A.f32([128, 1024]); sq = A.f32([128, 1024])
            xn32 = A.f32([128, 1024]); xnT = A.bf16([128, 8, 128])
            t1_bf = A.bf16([16, 128])
            gl = A.f32([128, 1024]); gw = A.f32([128, 1024]); ob = A.f32([128, 2048])

            emit_load_w(sp, wq_bf, wq, D, 1024, stage, 8192)
            emit_load_w(sp, wk_bf, wk, D, 1024, stage, 8192)
            emit_load_w(sp, wv_bf, wv, D, 2048, stage, 8192)
            emit_load_w(sp, wg_bf, wg, D, 2048, stage, 8192)
            emit_load_w(sp, gkw1_bf, gkw1, D, 16, stage, 8192)
            sp.dma(stage[0:16, 0:1024], gkw2)
            sp.op("vector", lambda e: e.tensor_copy(gkw2_bf, stage[0:16, 0:1024]))
            sp.dma(gkb_row[0:1, :], gkb[None, :])
            emit_bcast(sp, gkb_b, gkb_row, ones1, psA, 1024)

            for i in range(NT):
                r0 = 128 * i
                sp.dma(xt, x[r0:r0 + 128, :])
                emit_ln(sp, xt, xc, sq, stat, D)
                sp.op("vector", lambda e: e.tensor_scalar(xn32, xc, stat[:, 3:4], None, ALU.mult))
                emit_transposes(sp, xnT, xn32, idn_sb, psA, 8)
                emit_proj_tm(sp, psB, xnT, wq_bf, 8, 1024)
                sp.op("scalar", lambda e: e.activation(ob[:, 0:1024], psB[:, 0:1024], AF.Copy, scale=1.0 / 16.0))
                sp.dma(q_s[r0:r0 + 128, :], ob[:, 0:1024])
                emit_proj_tm(sp, psB, xnT, wk_bf, 8, 1024)
                sp.op("scalar", lambda e: e.copy(ob[:, 0:1024], psB[:, 0:1024]))
                sp.dma(k_s[r0:r0 + 128, :], ob[:, 0:1024])
                emit_proj_tm(sp, psB, xnT, wv_bf, 8, 2048)
                sp.op("scalar", lambda e: e.copy(ob, psB[:]))
                sp.dma(v_s[r0:r0 + 128, :], ob)
                emit_proj_tm(sp, psB, xnT, wg_bf, 8, 2048)
                sp.op("scalar", lambda e: e.activation(ob, psB[:], AF.Silu))
                sp.dma(gate_s[r0:r0 + 128, :], ob)
                for k in range(8):
                    def mm(k=k):
                        return lambda e: e.matmul(psA[0:16, 0:128], gkw1_bf[:, k, :], xnT[:, k, :], start=(k == 0), stop=(k == 7))
                    sp.op("tensor", mm(k))
                sp.op("vector", lambda e: e.tensor_copy(t1_bf, psA[0:16, 0:128]))
                for s in range(2):
                    def mm2(s=s):
                        return lambda e: e.matmul(psA[:, 512 * s:512 * (s + 1)], t1_bf, gkw2_bf[:, 512 * s:512 * (s + 1)], start=True, stop=True)
                    sp.op("tensor", mm2(s))
                sp.op("vector", lambda e: e.tensor_add(gl, psA[:, 0:1024], gkb_b))
                sp.op("scalar", lambda e: e.activation(gw, gl, AF.Abs))
                sp.op("scalar", lambda e: e.activation(gw, gw, AF.Exp, scale=-1.0))
                sp.op("scalar", lambda e: e.activation(gw, gw, AF.Ln, bias=1.0))
                sp.op("vector", lambda e: e.tensor_scalar_min(xc, gl, 0.0))
                sp.op("vector", lambda e: e.tensor_sub(gw, xc, gw))
                sp.op("scalar", lambda e: e.activation(gw, gw, AF.Copy, scale=1.0 / 16.0))
                sp.dma(g_s[r0:r0 + 128, :], gw)

        ph1()

        def ph2():
            A.reset()
            qtm32 = [[A.f32([128, 256]) for _ in range(2)] for _ in range(4)]
            ktm32 = [[A.f32([128, 256]) for _ in range(2)] for _ in range(4)]
            vtm32 = [[A.f32([128, 512]) for _ in range(2)] for _ in range(4)]
            gtm32 = [[A.f32([128, 256]) for _ in range(2)] for _ in range(4)]
            ePfm = [A.f32([128, 256]) for _ in range(4)]
            eMPfm = [A.f32([128, 256]) for _ in range(4)]
            eMPtm = [A.f32([128, 256]) for _ in range(4)]
            qt_bf = [A.bf16([128, 2, 128]) for _ in range(4)]
            kt_bf = [A.bf16([128, 2, 128]) for _ in range(4)]
            ktm_bf = [A.bf16([128, 256]) for _ in range(4)]
            v_bf = [A.bf16([128, 512]) for _ in range(4)]
            at_bf = [A.bf16([128, 128]) for _ in range(4)]
            Ss = [[A.f32([128, 512]) for _ in range(2)] for _ in range(4)]
            Sbs = [[A.bf16([128, 512]) for _ in range(2)] for _ in range(4)]
            D_sb = [A.f32([128, 2]) for _ in range(4)]
            Prun = [A.f32([128, 2]) for _ in range(4)]
            pcum = A.f32([128, 4, 32])
            osb = [A.f32([128, 512]) for _ in range(4)]
            sp.ctx_open(4)
            for h in range(4):
                def z0(h=h):
                    return lambda e: e.memset(Ss[h][0], 0.0)
                sp.op("vector", z0(), ctx=h)

                def z1(h=h):
                    return lambda e: e.memset(Ss[h][1], 0.0)
                sp.op("vector", z1(), ctx=h)

                def zp(h=h):
                    return lambda e: e.memset(Prun[h], 0.0)
                sp.op("vector", zp(), ctx=h)
            def ld2(h, t):
                qm = "sync"
                r0_ = 128 * t
                b = t % 2

                def dm(src, dst, lo, w, r0=r0_):
                    return lambda e: e.dma_start(dst, src[r0:r0 + 128, lo:lo + w])
                sp.op(qm, dm(g_s, gtm32[h][b], 256 * h, 256), inc=16, ctx=h)
                sp.op(qm, dm(k_s, ktm32[h][b], 256 * h, 256), inc=16, ctx=h)
                sp.op(qm, dm(v_s, vtm32[h][b], 512 * h, 512), inc=16, ctx=h)
                sp.op(qm, dm(q_s, qtm32[h][b], 256 * h, 256), inc=16, ctx=h)

            for c in range(NT):
                r0 = 128 * c
                for h in range(4):
                    qs = "sync"
                    psa_h = psA[:, 512 * h:512 * h + 512]
                    psb_h = psB[:, 512 * h:512 * h + 512]
                    if c == 0:
                        ld2(h, 0)
                    if c < NT - 1:
                        ld2(h, c + 1)
                    b = c % 2
                    def pc(h=h, c=c):
                        return lambda e: e.tensor_copy(pcum[:, h, 2 * c:2 * c + 2], Prun[h])
                    sp.op("vector", pc(), ctx=h)
                    for c2 in range(2):
                        def mmp(c2=c2, h=h, psa_h=psa_h, b=b):
                            return lambda e: e.matmul(psa_h[:, 128 * c2:128 * (c2 + 1)], gtm32[h][b][:, 128 * c2:128 * (c2 + 1)], triu_sb, start=True, stop=True)
                        sp.op("tensor", mmp(), ctx=h)

                    def ep1(h=h, psa_h=psa_h):
                        return lambda e: e.activation(ePfm[h], psa_h[:, 0:256], AF.Exp)
                    sp.op("scalar", ep1(), ctx=h)

                    def ep2(h=h, psa_h=psa_h):
                        return lambda e: e.activation(eMPfm[h], psa_h[:, 0:256], AF.Exp, scale=-1.0)
                    sp.op("scalar", ep2(), ctx=h)

                    def ep3(h=h, psa_h=psa_h):
                        return lambda e: e.activation(D_sb[h], psa_h[:, 127:256:128], AF.Exp)
                    sp.op("scalar", ep3(), ctx=h)

                    def pra(h=h, psa_h=psa_h):
                        return lambda e: e.tensor_add(Prun[h], Prun[h], psa_h[:, 127:256:128])
                    sp.op("vector", pra(), ctx=h)
                    def mmt(h=h, psa_h=psa_h, b=b):
                        return lambda e: e.matmul(psa_h[:, 256:512], triu_sb, gtm32[h][b], start=True, stop=True)
                    sp.op("tensor", mmt(), ctx=h)

                    def ep4(h=h, psa_h=psa_h):
                        return lambda e: e.activation(eMPtm[h], psa_h[:, 256:512], AF.Exp, scale=-1.0)
                    sp.op("scalar", ep4(), ctx=h)
                    for c2 in range(2):
                        def trq(c2=c2, h=h, psb_h=psb_h, b=b):
                            return lambda e: e.transpose(psb_h[:, 128 * c2:128 * (c2 + 1)], qtm32[h][b][:, 128 * c2:128 * (c2 + 1)], idn_sb)
                        sp.op("tensor", trq(), ctx=h)

                        def mq(c2=c2, h=h, psb_h=psb_h):
                            return lambda e: e.tensor_tensor(qt_bf[h][:, c2, :], psb_h[:, 128 * c2:128 * (c2 + 1)], ePfm[h][:, 128 * c2:128 * (c2 + 1)], ALU.mult)
                        sp.op("vector", mq(), ctx=h)

                        def trk(c2=c2, h=h, psb_h=psb_h, b=b):
                            return lambda e: e.transpose(psb_h[:, 256 + 128 * c2:256 + 128 * (c2 + 1)], ktm32[h][b][:, 128 * c2:128 * (c2 + 1)], idn_sb)
                        sp.op("tensor", trk(), ctx=h)

                        def mk(c2=c2, h=h, psb_h=psb_h):
                            return lambda e: e.tensor_tensor(kt_bf[h][:, c2, :], psb_h[:, 256 + 128 * c2:256 + 128 * (c2 + 1)], eMPfm[h][:, 128 * c2:128 * (c2 + 1)], ALU.mult)
                        sp.op("vector", mk(), ctx=h)

                    def ktb(h=h, b=b):
                        return lambda e: e.tensor_mul(ktm_bf[h], ktm32[h][b], eMPtm[h])
                    sp.op("vector", ktb(), ctx=h)

                    def vcp2(h=h, b=b):
                        return lambda e: e.tensor_copy(v_bf[h], vtm32[h][b])
                    sp.op("vector", vcp2(), ctx=h)

                    def sb0(h=h):
                        return lambda e: e.tensor_copy(Sbs[h][0], Ss[h][0])
                    sp.op("vector", sb0(), ctx=h)

                    def sb1(h=h):
                        return lambda e: e.tensor_copy(Sbs[h][1], Ss[h][1])
                    sp.op("vector", sb1(), ctx=h)
                    for c2 in range(2):
                        def sq_(h=h, c2=c2, r0=r0):
                            return lambda e: e.dma_start(qgT_s[256 * h + 128 * c2:256 * h + 128 * (c2 + 1), r0:r0 + 128], qt_bf[h][:, c2, :])
                        sp.op(qs, sq_(), inc=16, ctx=h)
                    for c2 in range(2):
                        def mma(c2=c2, h=h, psb_h=psb_h):
                            return lambda e: e.matmul(psb_h[:, 0:128], kt_bf[h][:, c2, :], qt_bf[h][:, c2, :], start=(c2 == 0), stop=(c2 == 1))
                        sp.op("tensor", mma(), ctx=h)

                    def atc(h=h, psb_h=psb_h):
                        return lambda e: e.tensor_tensor(at_bf[h], psb_h[:, 0:128], triu_sb, ALU.mult)
                    sp.op("vector", atc(), ctx=h)
                    def mo1g(h=h, psa_h=psa_h):
                        return lambda e: e.matmul(psa_h[:, 0:512], at_bf[h], v_bf[h], start=True, stop=False)
                    sp.op("tensor", mo1g(), ctx=h)
                    for c2 in range(2):
                        def mmo(c2=c2, h=h, psa_h=psa_h):
                            return lambda e: e.matmul(psa_h[:, 0:512], qt_bf[h][:, c2, :], Sbs[h][c2], start=False, stop=(c2 == 1))
                        sp.op("tensor", mmo(), ctx=h)
                    def ocp2(h=h, psa_h=psa_h):
                        return lambda e: e.tensor_copy(osb[h], psa_h[:, 0:512])
                    sp.op("vector", ocp2(), ctx=h)

                    def do_(h=h, r0=r0):
                        return lambda e: e.dma_start(oraw_s[r0:r0 + 128, 512 * h:512 * (h + 1)], osb[h])
                    sp.op(qs, do_(), inc=16, ctx=h)
                    for c2 in range(2):
                        def mmd(c2=c2, h=h, psb_h=psb_h):
                            return lambda e: e.matmul(psb_h[:, 0:512], ktm_bf[h][:, 128 * c2:128 * (c2 + 1)], v_bf[h], start=True, stop=True)
                        sp.op("tensor", mmd(), ctx=h)

                        def upd2(c2=c2, h=h, psb_h=psb_h):
                            return lambda e: e.tensor_add(Ss[h][c2], Ss[h][c2], psb_h[:, 0:512])
                        sp.op("vector", upd2(), ctx=h)

                        def upd(c2=c2, h=h):
                            return lambda e: e.tensor_scalar(Ss[h][c2], Ss[h][c2], D_sb[h][:, c2:c2 + 1], None, ALU.mult)
                        sp.op("vector", upd(), ctx=h)
            for h in range(4):
                qm = "sync"
                for c2 in range(2):
                    def ds_(h=h, c2=c2):
                        return lambda e: e.dma_start(cc_gs_in[256 * h + 128 * c2:256 * h + 128 * (c2 + 1), :], Ss[h][c2])
                    sp.op(qm, ds_(), inc=16, ctx=h)

                    def dp_(h=h, c2=c2):
                        return lambda e: e.dma_start(cc_gs_in[1024 + h // 2, (h % 2) * 256 + 128 * c2:(h % 2) * 256 + 128 * (c2 + 1)], Prun[h][:, c2:c2 + 1])
                    sp.op(qm, dp_(), inc=16, ctx=h)
            sp.ctx_close()

            sp.op("gpsimd", lambda e: e.collective_compute(
                "AllGather", ALU.bypass, replica_groups=RG,
                ins=[cc_gs_in.opt()],
                outs=[cc_gs_out.opt()]), inc=1)

            Pl = [A.f32([128, 4, 2]) for _ in range(4)]
            coef = [A.f32([128, 4, 2]) for _ in range(4)]
            Sins = [[A.f32([128, 512]) for _ in range(2)] for _ in range(4)]
            Sib2 = [[A.bf16([128, 512]) for _ in range(2)] for _ in range(4)]
            eP2 = [A.f32([128, 2]) for _ in range(4)]
            qld = [[A.bf16([128, 2, 128]) for _ in range(2)] for _ in range(4)]
            orawt = [[A.f32([128, 512]) for _ in range(2)] for _ in range(4)]
            osb2 = [A.f32([128, 512]) for _ in range(4)]
            gat = [A.f32([128, 512]) for _ in range(4)]
            sq5 = [A.f32([128, 512]) for _ in range(4)]
            msel_sb = A.f32([128, 16])
            sel_sb = A.f32([128, 4])
            rms_row = A.f32([1, 512]); rms_b = A.f32([128, 512])
            sp.dma(msel_sb, gmsel)
            sp.dma(sel_sb, gsel)
            sp.dma(rms_row[0:1, :], rmsw[None, :])
            emit_bcast(sp, rms_b, rms_row, ones1, psA, 512)
            qgT_r = qgT_s.rearrange("(a p) t -> p a t", p=128)
            sp.ctx_open(4)
            for h in range(4):
                qm = "sync"
                for l in range(4):
                    for c2 in range(2):
                        def dpl(l=l, h=h, c2=c2):
                            return lambda e: e.dma_start(Pl[h][:, l, c2:c2 + 1], cc_gs_out[1026 * l + 1024 + h // 2, (h % 2) * 256 + 128 * c2:(h % 2) * 256 + 128 * (c2 + 1)])
                        sp.op(qm, dpl(), inc=16, ctx=h)
                for j in range(4):
                    def z_(j=j, h=h):
                        return lambda e: e.memset(coef[h][:, j, :], 0.0)
                    sp.op("vector", z_(), ctx=h)
                    for l in range(4):
                        def ac(j=j, l=l, h=h):
                            return lambda e: e.scalar_tensor_tensor(coef[h][:, j, :], Pl[h][:, l, :], msel_sb[:, 4 * j + l:4 * j + l + 1], coef[h][:, j, :], ALU.mult, ALU.add)
                        sp.op("vector", ac(), ctx=h)

                    def ex_(j=j, h=h):
                        return lambda e: e.activation(coef[h][:, j, :], coef[h][:, j, :], AF.Exp)
                    sp.op("scalar", ex_(), ctx=h)

                    def se_(j=j, h=h):
                        return lambda e: e.tensor_scalar(coef[h][:, j, :], coef[h][:, j, :], sel_sb[:, j:j + 1], None, ALU.mult)
                    sp.op("vector", se_(), ctx=h)
                def zi0(h=h):
                    return lambda e: e.memset(Sins[h][0], 0.0)
                sp.op("vector", zi0(), ctx=h)

                def zi1(h=h):
                    return lambda e: e.memset(Sins[h][1], 0.0)
                sp.op("vector", zi1(), ctx=h)
                for j in range(4):
                    for c2 in range(2):
                        def dsj(j=j, h=h, c2=c2):
                            return lambda e: e.dma_start(orawt[h][0], cc_gs_out[1026 * j + 256 * h + 128 * c2:1026 * j + 256 * h + 128 * (c2 + 1), :])
                        sp.op(qm, dsj(), inc=16, ctx=h)

                        def acs(j=j, c2=c2, h=h):
                            return lambda e: e.scalar_tensor_tensor(Sins[h][c2], orawt[h][0], coef[h][:, j, c2:c2 + 1], Sins[h][c2], ALU.mult, ALU.add)
                        sp.op("vector", acs(), ctx=h)
            def ld2c(h, t):
                qm = "sync"
                r0_ = 128 * t
                b = t % 2

                def dq(h=h, r0=r0_, b=b):
                    return lambda e: e.dma_start(qld[h][b], qgT_r[:, 2 * h:2 * h + 2, r0:r0 + 128])
                sp.op(qm, dq(), inc=16, ctx=h)

                def dor(h=h, r0=r0_, b=b):
                    return lambda e: e.dma_start(orawt[h][b], oraw_s[r0:r0 + 128, 512 * h:512 * (h + 1)])
                sp.op(qm, dor(), inc=16, ctx=h)

            for c in range(NT):
                r0 = 128 * c
                for h in range(4):
                    qs = "sync"
                    psa_h = psA[:, 512 * h:512 * h + 512]
                    st = stat[:, 8 * h:8 * h + 8]
                    if c == 0:
                        ld2c(h, 0)
                    if c < NT - 1:
                        ld2c(h, c + 1)
                    b = c % 2

                    def epc(h=h, c=c):
                        return lambda e: e.activation(eP2[h], pcum[:, h, 2 * c:2 * c + 2], AF.Exp)
                    sp.op("scalar", epc(), ctx=h)
                    for c2 in range(2):
                        def sc_(c2=c2, h=h):
                            return lambda e: e.tensor_scalar(Sib2[h][c2], Sins[h][c2], eP2[h][:, c2:c2 + 1], None, ALU.mult)
                        sp.op("vector", sc_(), ctx=h)
                    for c2 in range(2):
                        def mmc(c2=c2, h=h, psa_h=psa_h, b=b):
                            return lambda e: e.matmul(psa_h[:, 0:512], qld[h][b][:, c2, :], Sib2[h][c2], start=(c2 == 0), stop=(c2 == 1))
                        sp.op("tensor", mmc(), ctx=h)

                    def oad(h=h, psa_h=psa_h, b=b):
                        return lambda e: e.tensor_add(osb2[h], orawt[h][b], psa_h[:, 0:512])
                    sp.op("vector", oad(), ctx=h)
                    def r1(h=h, st=st):
                        return lambda e: e.activation(sq5[h], osb2[h], AF.Square, accum_out=st[:, 0:1])
                    sp.op("scalar", r1(), ctx=h)

                    def r2(h=h, st=st):
                        return lambda e: e.tensor_scalar(st[:, 1:2], st[:, 0:1], 1.0 / 512.0, 1e-5, ALU.mult, ALU.add)
                    sp.op("vector", r2(), ctx=h)

                    def r3(h=h, st=st):
                        return lambda e: e.activation(st[:, 1:2], st[:, 1:2], AF.Sqrt)
                    sp.op("scalar", r3(), ctx=h)

                    def r4(h=h, st=st):
                        return lambda e: e.reciprocal(st[:, 2:3], st[:, 1:2])
                    sp.op("vector", r4(), ctx=h)

                    def r5(h=h, st=st):
                        return lambda e: e.scalar_tensor_tensor(gat[h], osb2[h], st[:, 2:3], rms_b, ALU.mult, ALU.mult)
                    sp.op("vector", r5(), ctx=h)

                    def don(h=h, r0=r0):
                        return lambda e: e.dma_start(on_s[r0:r0 + 128, 512 * h:512 * (h + 1)], gat[h])
                    sp.op(qs, don(), inc=16, ctx=h)
            sp.ctx_close()

        ph2()

        def ph3():
            A.reset()
            wo_bf = A.bf16([128, 16, 1024])
            qw_bf = A.bf16([128, 8, 512]); kw_bf = A.bf16([128, 8, 512]); vw_bf = A.bf16([128, 8, 512])
            stage = A.f32([128, 8192])
            qb_b = A.f32([128, 512]); kb_b = A.f32([128, 512]); vb_b = A.f32([128, 512])
            brow = A.f32([1, 512])
            on_t = [A.f32([128, 2048]) for _ in range(2)]
            ga_t = [A.f32([128, 2048]) for _ in range(2)]
            og32 = [A.f32([128, 2048]) for _ in range(2)]
            ogT = [A.bf16([128, 16, 128]) for _ in range(2)]
            xt = [A.f32([128, 1024]) for _ in range(2)]
            x1t = [A.f32([128, 1024]) for _ in range(2)]
            xc = [A.f32([128, 1024]) for _ in range(2)]
            sq = [A.f32([128, 1024]) for _ in range(2)]
            xnT = [A.bf16([128, 8, 128]) for _ in range(2)]
            ob = [A.f32([128, 1024]) for _ in range(2)]
            kvb = [A.bf16([128, 1024]) for _ in range(2)]
            emit_load_w(sp, wo_bf, wo, 2048, 1024, stage, 8192)
            emit_load_w(sp, qw_bf, qwe, D, 512, stage, 8192)
            emit_load_w(sp, kw_bf, kwf, D, 512, stage, 8192)
            emit_load_w(sp, vw_bf, vwf, D, 512, stage, 8192)
            for row, dst in ((qbe, qb_b), (kbf, kb_b), (vbf, vb_b)):
                def d(row=row):
                    return lambda e: e.dma_start(brow[0:1, :], row[None, :])
                sp.op("sync", d(row), inc=16)
                emit_bcast(sp, dst, brow, ones1, psA, 512)
            sp.ctx_open(2)
            for i in range(NT):
                r0 = 128 * i
                j = i % 2
                qm = "sync"
                qst = "sync"
                psb_j = psB[:, 1024 * j:1024 * j + 1024]
                psa_j = psA[:, 512 * j:512 * j + 128]

                def d1(j=j, r0=r0):
                    return lambda e: e.dma_start(on_t[j], on_s[r0:r0 + 128, :])
                sp.op(qm, d1(), inc=16, ctx=j)

                def d2(j=j, r0=r0):
                    return lambda e: e.dma_start(ga_t[j], gate_s[r0:r0 + 128, :])
                sp.op(qm, d2(), inc=16, ctx=j)

                def d3(j=j, r0=r0):
                    return lambda e: e.dma_start(xt[j], x[r0:r0 + 128, :])
                sp.op(qm, d3(), inc=16, ctx=j)

                def m1(j=j):
                    return lambda e: e.tensor_mul(og32[j], on_t[j], ga_t[j])
                sp.op("vector", m1(), ctx=j)
                emit_transposes(sp, ogT[j], og32[j], idn_sb, psa_j, 16, ctx=j)
                emit_proj_tm(sp, psb_j, ogT[j], wo_bf, 16, 1024, ctx=j)

                def m2(j=j, psb_j=psb_j):
                    return lambda e: e.tensor_add(x1t[j], psb_j[:, 0:1024], xt[j])
                sp.op("vector", m2(), ctx=j)

                def d4(j=j, r0=r0):
                    return lambda e: e.dma_start(x1_s[r0:r0 + 128, :], x1t[j])
                sp.op(qst, d4(), inc=16, ctx=j)
                emit_ln(sp, x1t[j], xc[j], sq[j], stat, D, c0=16 * j, ctx=j)

                def m3(j=j):
                    return lambda e: e.tensor_scalar(ob[j], xc[j], stat[:, 16 * j + 3:16 * j + 4], None, ALU.mult)
                sp.op("vector", m3(), ctx=j)

                def d5(j=j, r0=r0):
                    return lambda e: e.dma_start(xn1_s[r0:r0 + 128, :], ob[j])
                sp.op(qst, d5(), inc=16, ctx=j)
                emit_transposes(sp, xnT[j], ob[j], idn_sb, psa_j, 8, ctx=j)
                emit_proj_tm(sp, psb_j, xnT[j], qw_bf, 8, 512, ctx=j)

                def m4(j=j, psb_j=psb_j):
                    return lambda e: e.tensor_add(ob[j][:, 0:512], psb_j[:, 0:512], qb_b)
                sp.op("vector", m4(), ctx=j)

                def d6(j=j, r0=r0):
                    return lambda e: e.dma_start(Qe_s[r0:r0 + 128, :], ob[j][:, 0:512])
                sp.op(qst, d6(), inc=16, ctx=j)
                emit_proj_tm(sp, psb_j, xnT[j], kw_bf, 8, 512, ctx=j)

                def m5(j=j, psb_j=psb_j):
                    return lambda e: e.tensor_add(ob[j][:, 0:512], psb_j[:, 0:512], kb_b)
                sp.op("vector", m5(), ctx=j)

                def m6(j=j):
                    return lambda e: e.tensor_copy(kvb[j][:, 0:512], ob[j][:, 0:512])
                sp.op("vector", m6(), ctx=j)
                emit_proj_tm(sp, psb_j, xnT[j], vw_bf, 8, 512, ctx=j)

                def m7(j=j, psb_j=psb_j):
                    return lambda e: e.tensor_add(ob[j][:, 0:512], psb_j[:, 0:512], vb_b)
                sp.op("vector", m7(), ctx=j)

                def m8(j=j):
                    return lambda e: e.tensor_copy(kvb[j][:, 512:1024], ob[j][:, 0:512])
                sp.op("vector", m8(), ctx=j)

                def d7(j=j, r0=r0):
                    return lambda e: e.dma_start(cc_kv_in[r0:r0 + 128, :], kvb[j])
                sp.op(qst, d7(), inc=16, ctx=j)
            sp.ctx_close()

            sp.op("gpsimd", lambda e: e.collective_compute(
                "AllGather", ALU.bypass, replica_groups=RG,
                ins=[cc_kv_in.opt()], outs=[cc_kv_out.opt()]), inc=1)

        ph3()

        def ph4():
            A.reset()
            Kfm = A.bf16([128, 4, T])
            Vp = A.bf16([128, 16, 64, 34])
            kv = A.bf16([128, 1024])
            kvf = A.f32([128, 512])
            Qfm = A.bf16([128, 4, 512])
            qstage = A.f32([128, 512])
            kn = A.f32([128, 16]); knt = A.f32([128, 16]); knT = A.f32([16, 128])
            knmax = A.f32([16, 1]); qnm = A.f32([128, 16]); qmax = A.f32([16, 1])
            Mv = A.f32([16, 1]); mdiag = A.f32([16, 16])
            mb = A.f32([128, 16]); mbt = A.f32([128, 8]); mbg = A.f32([128, 4])
            sq4 = A.f32([128, 512])
            ptil2 = [A.bf16([128, 2048]), A.bf16([128, 2048])]
            ozsb = A.f32([33, 2048])
            tz = A.f32([128, 33])
            aot = A.f32([128, 4, 512])
            aofm = A.bf16([128, 4, 512])
            ow_bf = A.bf16([128, 4, 1024])
            ob_b = A.f32([128, 1024]); rb_b = A.f32([128, 8])
            rw_bf = A.bf16([128, 8, 8])
            mla_sb = A.f32([128, 1024])
            mlaT = A.bf16([128, 8, 128])
            pr = A.f32([128, 8]); e8 = A.f32([128, 8]); w8 = A.f32([128, 8])
            brow = A.f32([1, 1024])
            stage4 = A.f32([128, 2048])
            emit_load_w(sp, ow_bf, owe, 512, 1024, stage4, 2048)
            emit_load_w(sp, rw_bf, rw, 1024, 8, stage4, 2048)
            sp.dma(brow[0:1, :], obv[None, :])
            emit_bcast(sp, ob_b, brow, ones1, psA, 1024)
            sp.dma(brow[0:1, 0:8], rb[None, :])
            emit_bcast(sp, rb_b, brow, ones1, psA, 8)
            sp.op("vector", lambda e: e.memset(kn, 0.0))
            sp.op("vector", lambda e: e.memset(Vp[:, :, :, 32:33], 1.0))
            for j in range(64):
                r0 = 128 * j
                sp.dma(kv, cc_kv_out[r0:r0 + 128, :])
                sp.op("vector", lambda e: e.tensor_copy(kvf, kv[:, 0:512]))
                sp.op("scalar", lambda e: e.activation(sq4, kvf, AF.Square))
                sp.op("vector", lambda e: e.reduce_sum(knt, sq4.rearrange("p (h k) -> p h k", k=32), axis=AX.X))
                sp.op("vector", lambda e: e.tensor_max(kn, kn, knt))
                for c in range(4):
                    def tr(c=c):
                        return lambda e: e.transpose(psA[:, 0:128], kvf[:, 128 * c:128 * (c + 1)], idn_sb)
                    sp.op("tensor", tr(c))

                    def cp(c=c, j=j):
                        return lambda e: e.tensor_copy(Kfm[:, c, 128 * j:128 * (j + 1)], psA[:, 0:128])
                    sp.op("vector", cp(c, j))

                def cpv(j=j):
                    return lambda e: e.tensor_copy(Vp[:, :, j, 0:32], kv[:, 512:1024].rearrange("p (h k) -> p h k", k=32))
                sp.op("vector", cpv(j))
            sp.op("tensor", lambda e: e.transpose(psA[0:16, 0:128], kn, idn_sb))
            sp.op("vector", lambda e: e.tensor_copy(knT, psA[0:16, 0:128]))
            sp.op("vector", lambda e: e.reduce_max(knmax, knT, axis=AX.X))
            sp.op("scalar", lambda e: e.activation(knmax, knmax, AF.Sqrt))

            for qb in range(4):
                q0 = 512 * qb
                for qt in range(4):
                    r0 = q0 + 128 * qt
                    sp.dma(qstage, Qe_s[r0:r0 + 128, :])
                    sp.op("scalar", lambda e: e.activation(sq4, qstage, AF.Square))
                    sp.op("vector", lambda e: e.reduce_sum(knt, sq4.rearrange("p (h k) -> p h k", k=32), axis=AX.X))
                    if qt == 0:
                        sp.op("vector", lambda e: e.tensor_copy(qnm, knt))
                    else:
                        sp.op("vector", lambda e: e.tensor_max(qnm, qnm, knt))
                    for c in range(4):
                        def tr(c=c):
                            return lambda e: e.transpose(psA[:, 0:128], qstage[:, 128 * c:128 * (c + 1)], idn_sb)
                        sp.op("tensor", tr(c))

                        def cp(c=c, qt=qt):
                            return lambda e: e.tensor_copy(Qfm[:, c, 128 * qt:128 * (qt + 1)], psA[:, 0:128])
                        sp.op("vector", cp(c, qt))
                sp.op("tensor", lambda e: e.transpose(psA[0:16, 0:128], qnm, idn_sb))
                sp.op("vector", lambda e: e.tensor_copy(knT, psA[0:16, 0:128]))
                sp.op("vector", lambda e: e.reduce_max(qmax, knT, axis=AX.X))
                sp.op("scalar", lambda e: e.activation(qmax, qmax, AF.Sqrt))
                sp.op("vector", lambda e: e.tensor_mul(Mv, qmax, knmax))
                sp.op("vector", lambda e: e.tensor_scalar(mdiag, i16_sb, Mv[:, 0:1], None, ALU.mult))
                sp.op("tensor", lambda e: e.matmul(psA[:, 0:16], ones16, mdiag, start=True, stop=True))
                sp.op("scalar", lambda e: e.activation(mb, psA[:, 0:16], AF.Copy, scale=-RS32))
                sp.op("vector", lambda e: e.tensor_tensor(mbt, mb[:, 0:16:2], mb[:, 1:16:2], ALU.min))
                sp.op("vector", lambda e: e.tensor_tensor(mbg, mbt[:, 0:8:2], mbt[:, 1:8:2], ALU.min))

                bq = list(sp.now().items())
                for g in range(4):
                    e_ex_prev = [None, None]
                    e_mmv_pp = [[None, None], [None, None]]
                    e_ex_j = {}

                    def emit_mms(j, pair):
                        dl = bq + ([e_ex_prev[pair]] if e_ex_prev[pair] else [])
                        ev = None
                        for hh in (2 * pair, 2 * pair + 1):
                            def mms(g=g, j=j, hh=hh):
                                return lambda e: e.matmul(
                                    psA[:, 512 * hh:512 * (hh + 1)],
                                    Kfm[32 * hh:32 * (hh + 1), g, 128 * j:128 * (j + 1)],
                                    Qfm[32 * hh:32 * (hh + 1), g, :],
                                    start=True, stop=True, tile_position=(32 * hh, 0))
                            ev = sp.op("tensor", mms(g, j, hh), deps=dl)
                        return ev

                    def emit_ex(j, pair, e_mms):
                        par = j % 2

                        def ex(g=g, par=par, pair=pair):
                            return lambda e: e.activation(
                                ptil2[par][:, 1024 * pair:1024 * (pair + 1)],
                                psA[:, 1024 * pair:1024 * (pair + 1)],
                                AF.Exp, bias=mbg[:, g:g + 1], scale=RS32)
                        dl = bq + [e_mms] + ([e_mmv_pp[par][pair]] if e_mmv_pp[par][pair] else [])
                        e_ex_prev[pair] = sp.op("scalar", ex(), deps=dl)
                        e_ex_j[(j, pair)] = e_ex_prev[pair]

                    def emit_mmv(j, pair):
                        par = j % 2
                        dl = bq + [e_ex_j.pop((j, pair))]
                        ev = None
                        for hh in (2 * pair, 2 * pair + 1):
                            h = 4 * g + hh

                            def mmv(h=h, hh=hh, j=j, par=par):
                                return lambda e: e.matmul(
                                    psB[0:33, 512 * hh:512 * hh + 512],
                                    Vp[:, h, j, 0:33],
                                    ptil2[par][:, 512 * hh:512 * (hh + 1)],
                                    start=(j == 0), stop=(j == 63), skip_group_check=True)
                            ev = sp.op("tensor", mmv(h, hh, j, par), deps=dl)
                        e_mmv_pp[par][pair] = ev

                    for j in range(64):
                        for pair in range(2):
                            e_mms = emit_mms(j, pair)
                            emit_ex(j, pair, e_mms)
                        if j > 0:
                            for pair in range(2):
                                emit_mmv(j - 1, pair)
                    for pair in range(2):
                        emit_mmv(63, pair)
                    sp.op("vector", lambda e: e.tensor_copy(ozsb, psB[0:33, :]))
                    for hh in range(4):
                        h = 4 * g + hh
                        for qt in range(4):
                            def trz(hh=hh, qt=qt):
                                return lambda e: e.transpose(psA[:, 0:33], ozsb[0:33, 512 * hh + 128 * qt:512 * hh + 128 * (qt + 1)], idn_sb[0:33, 0:33])
                            sp.op("tensor", trz(hh, qt))
                            sp.op("vector", lambda e: e.tensor_copy(tz, psA[:, 0:33]))
                            sp.op("vector", lambda e: e.tensor_scalar_add(stat[:, 0:1], tz[:, 32:33], 1e-38))
                            sp.op("vector", lambda e: e.reciprocal(stat[:, 1:2], stat[:, 0:1]))

                            def dv(h=h, qt=qt):
                                return lambda e: e.tensor_scalar(aot[:, qt, 32 * h:32 * (h + 1)], tz[:, 0:32], stat[:, 1:2], None, ALU.mult)
                            sp.op("vector", dv(h, qt))
                    bq = list(sp.now().items())
                    e_ex_prev = None
                    e_mmv_pp = [None, None]
                for qt in range(4):
                    for c in range(4):
                        def tra(qt=qt, c=c):
                            return lambda e: e.transpose(psA[:, 0:128], aot[:, qt, 128 * c:128 * (c + 1)], idn_sb)
                        sp.op("tensor", tra(qt, c))

                        def cpa(qt=qt, c=c):
                            return lambda e: e.tensor_copy(aofm[:, c, 128 * qt:128 * (qt + 1)], psA[:, 0:128])
                        sp.op("vector", cpa(qt, c))
                for qt in range(4):
                    r0 = q0 + 128 * qt
                    for c in range(4):
                        for s in range(2):
                            def mmm(c=c, s=s, qt=qt):
                                return lambda e: e.matmul(psB[:, 512 * s:512 * (s + 1)], aofm[:, c, 128 * qt:128 * (qt + 1)], ow_bf[:, c, 512 * s:512 * (s + 1)], start=(c == 0), stop=(c == 3))
                            sp.op("tensor", mmm(c, s, qt))
                    sp.op("vector", lambda e: e.tensor_add(mla_sb, psB[:, 0:1024], ob_b))

                    def dmla(r0=r0):
                        return lambda e: e.dma_start(mla_s[r0:r0 + 128, :], mla_sb)
                    sp.op("sync", dmla(r0), inc=16)
                    emit_transposes(sp, mlaT, mla_sb, idn_sb, psA, 8)
                    for c in range(8):
                        def mmr(c=c):
                            return lambda e: e.matmul(psB[:, 1024:1032], mlaT[:, c, :], rw_bf[:, c, :], start=(c == 0), stop=(c == 7))
                        sp.op("tensor", mmr(c))
                    sp.op("vector", lambda e: e.tensor_add(pr, psB[:, 1024:1032], rb_b))
                    sp.op("vector", lambda e: e.reduce_max(stat[:, 0:1], pr, axis=AX.X, negate=True))
                    sp.op("scalar", lambda e: e.activation(e8, pr, AF.Exp, bias=stat[:, 0:1], accum_out=stat[:, 1:2]))
                    sp.op("vector", lambda e: e.reciprocal(stat[:, 2:3], stat[:, 1:2]))
                    sp.op("vector", lambda e: e.tensor_scalar(pr, e8, stat[:, 2:3], None, ALU.mult))
                    sp.op("vector", lambda e: e.reduce_max(stat[:, 3:4], pr, axis=AX.X))
                    sp.op("vector", lambda e: e.tensor_scalar(e8, pr, stat[:, 3:4], None, ALU.is_ge))
                    sp.op("vector", lambda e: e.scalar_tensor_tensor(w8, e8, -1e30, pr, ALU.mult, ALU.add))
                    sp.op("vector", lambda e: e.reduce_max(stat[:, 4:5], w8, axis=AX.X))
                    sp.op("vector", lambda e: e.tensor_scalar(e8, pr, stat[:, 4:5], None, ALU.is_ge))
                    sp.op("vector", lambda e: e.tensor_add(stat[:, 5:6], stat[:, 3:4], stat[:, 4:5]))
                    sp.op("vector", lambda e: e.tensor_scalar_add(stat[:, 5:6], stat[:, 5:6], 1e-8))
                    sp.op("vector", lambda e: e.reciprocal(stat[:, 6:7], stat[:, 5:6]))
                    sp.op("vector", lambda e: e.tensor_mul(e8, e8, pr))
                    sp.op("vector", lambda e: e.tensor_scalar(e8, e8, stat[:, 6:7], None, ALU.mult))

                    def dcomb(r0=r0):
                        return lambda e: e.dma_start(comb_s[r0:r0 + 128, :], e8)
                    sp.op("sync", dcomb(r0), inc=16)

        ph4()

        def ph5():
            A.reset()
            mlaT = A.bf16([128, 8, 2048])
            comb_sb = A.f32([128, 16, 8])
            w1_bf = A.bf16([128, 8, 2048])
            w2_bf = A.bf16([128, 16, 1024])
            h1_bf = A.bf16([128, 16, 512])
            acc = A.f32([128, 16, 1024])
            stage = A.f32([128, 2048])
            b1f = A.f32([128, 16]); b2b = A.f32([128, 1024])
            brow = A.f32([1, 1024])
            mt = A.f32([128, 1024]); ysb = A.f32([128, 1024]); xc = A.f32([128, 1024])
            w1r = w1a.rearrange("e (c p) n -> e p c n", p=128)
            w2r = w2a.rearrange("e (c p) n -> e p c n", p=128)
            sp.dma(comb_sb, comb_s.rearrange("(n p) e -> p n e", p=128))
            sp.op("vector", lambda e: e.memset(acc, 0.0))
            mt2 = [mt, xc]
            sp.ctx_open(2)
            for i in range(NT):
                r0 = 128 * i
                j = i % 2
                qm = "sync"

                def dmt(i=i, j=j, r0=r0):
                    return lambda e: e.dma_start(mt2[j], mla_s[r0:r0 + 128, :])
                sp.op(qm, dmt(), inc=16, ctx=j)
                for c in range(8):
                    def tr(c=c, j=j):
                        return lambda e: e.transpose(psA[:, 512 * j:512 * j + 128], mt2[j][:, 128 * c:128 * (c + 1)], idn_sb)
                    sp.op("tensor", tr(c, j), ctx=j)

                    def cp(c=c, i=i, j=j):
                        return lambda e: e.tensor_copy(mlaT[:, c, 128 * i:128 * (i + 1)], psA[:, 512 * j:512 * j + 128])
                    sp.op("vector", cp(c, i, j), ctx=j)
            sp.ctx_close()
            e_mlaT = ("vector", sp.cnt["vector"])
            e_hlast = ("tensor", 0)
            e_ylast = ("tensor", 0)
            e_gel_last = ("scalar", 0)
            e_b2mm = ("tensor", 0)
            e_evac = {qt: ("vector", 0) for qt in range(4)}
            e_stg = [("vector", 0), ("vector", 0)]
            e_b2cp = ("vector", 0)
            ge = {}
            for ex in range(8):
                caster = []
                pieces = []
                for c in range(8):
                    for hf in range(2):
                        pieces.append(("w1", c, hf))
                for c in range(16):
                    pieces.append(("w2", c, 0))
                for pi, (kind, c, hf) in enumerate(pieces):
                    par = pi % 2
                    stg = stage[:, 1024 * par:1024 * (par + 1)]
                    if kind == "w1":
                        def dd(ex=ex, c=c, hf=hf, stg=stg):
                            return lambda e: e.dma_start(stg, w1r[ex, :, c, 1024 * hf:1024 * (hf + 1)])
                        war = e_hlast
                    else:
                        def dd(ex=ex, c=c, stg=stg):
                            return lambda e: e.dma_start(stg, w2r[ex, :, c, :])
                        war = e_ylast
                    e_d = sp.op("sync", dd(), inc=16, deps=[e_stg[par]])
                    if kind == "w1":
                        def cc(c=c, hf=hf, stg=stg):
                            return lambda e: e.tensor_copy(w1_bf[:, c, 1024 * hf:1024 * (hf + 1)], stg)
                    else:
                        def cc(c=c, stg=stg):
                            return lambda e: e.tensor_copy(w2_bf[:, c, :], stg)
                    e_c = sp.op("vector", cc(), deps=[e_d, war])
                    e_stg[par] = e_c
                    caster.append((kind, e_c))
                e_w1cast = max(c for (k, c) in [(k, e[1]) for (k, e) in caster if k == "w1"])
                e_w1cast = ("vector", e_w1cast)
                e_w2cast = ("vector", max(e[1] for (k, e) in caster if k == "w2"))
                e_b1 = sp.op("sync", (lambda ex=ex: (lambda e: e.dma_start(b1f, b1fma[ex])))(), inc=16, deps=[e_gel_last])
                e_b2r = sp.op("sync", (lambda ex=ex: (lambda e: e.dma_start(brow[0:1, :], b2a[ex:ex + 1, :])))(), inc=16, deps=[e_b2mm])
                e_b2mm = sp.op("tensor", lambda e: e.matmul(psA[:, 1536:2048], ones1, brow[0:1, 0:512], start=True, stop=True), deps=[e_b2r, e_evac[2], e_evac[3], e_b2cp])
                e_b2mm2 = sp.op("tensor", lambda e: e.matmul(psB[:, 1536:2048], ones1, brow[0:1, 512:1024], start=True, stop=True), deps=[e_b2r, e_evac[1], e_b2cp])
                e_c1 = sp.op("vector", lambda e: e.tensor_copy(b2b[:, 0:512], psA[:, 1536:2048]), deps=[e_b2mm, e_evac[0]])
                e_b2cp = sp.op("vector", lambda e: e.tensor_copy(b2b[:, 512:1024], psB[:, 1536:2048]), deps=[e_b2mm2])
                for qb in range(4):
                    e_y_prev_qb = e_ylast
                    for fc in range(16):
                        bank = psA[:, 512 * (fc % 2):512 * (fc % 2 + 1)]
                        if fc >= 2:
                            bank_war = [ge.get(fc - 2, ("scalar", 0))]
                        else:
                            bank_war = [ge.get(fc + 14, ("scalar", 0)), e_evac[2], e_evac[3]]
                        dl = [e_w1cast, e_mlaT, e_b2cp] + bank_war
                        for k in range(8):
                            def mh(fc=fc, k=k, qb=qb, bank=bank):
                                return lambda e: e.matmul(bank, w1_bf[:, k, 128 * fc:128 * (fc + 1)], mlaT[:, k, 512 * qb:512 * (qb + 1)], start=(k == 0), stop=(k == 7))
                            e_hlast = sp.op("tensor", mh(), deps=dl)

                        def gel(fc=fc, bank=bank):
                            return lambda e: e.activation(h1_bf[:, fc, :], bank, AF.Gelu, bias=b1f[:, fc:fc + 1])
                        ge[fc] = sp.op("scalar", gel(), deps=[e_hlast, e_y_prev_qb, e_b1])
                    e_gel_last = ge[15]
                    for qt in range(4):
                        ps = psB if qt < 2 else psA
                        off = 1024 * (qt % 2)
                        dl = [e_w2cast, e_evac[qt], ge[15], e_b2cp]
                        for fc in range(16):
                            dl2 = dl + [ge[fc]]
                            for s in range(2):
                                def my(fc=fc, s=s, qt=qt, ps=ps, off=off, qb=qb):
                                    return lambda e: e.matmul(ps[:, off + 512 * s:off + 512 * (s + 1)], h1_bf[:, fc, 128 * qt:128 * (qt + 1)], w2_bf[:, fc, 512 * s:512 * (s + 1)], start=(fc == 0), stop=(fc == 15))
                                e_ylast = sp.op("tensor", my(), deps=dl2)

                        def ev(ps=ps, off=off):
                            return lambda e: e.tensor_add(ysb, ps[:, off:off + 1024], b2b)
                        e_ev = sp.op("vector", ev(), deps=[e_ylast])
                        qg = 4 * qb + qt

                        def accum(ex=ex, qg=qg):
                            return lambda e: e.scalar_tensor_tensor(acc[:, qg, :], ysb, comb_sb[:, qg, ex:ex + 1], acc[:, qg, :], ALU.mult, ALU.add)
                        e_evac[qt] = sp.op("vector", accum(), deps=[])
            for i in range(NT):
                r0 = 128 * i
                sp.dma(mt, x1_s[r0:r0 + 128, :])
                sp.dma(stage[:, 0:1024], xn1_s[r0:r0 + 128, :])

                def fin(i=i):
                    return lambda e: e.scalar_tensor_tensor(ysb, acc[:, i, :], 0.3, mt, ALU.mult, ALU.add)
                sp.op("vector", fin(i))
                sp.op("vector", lambda e: e.tensor_add(ysb, ysb, stage[:, 0:1024]))
                sp.dma(x2_s[r0:r0 + 128, :], ysb)
                emit_ln(sp, ysb, xc, stage[:, 1024:2048], stat, D)
                sp.op("vector", lambda e: e.tensor_scalar(mt, xc, stat[:, 3:4], None, ALU.mult))
                sp.dma(xn2_s[r0:r0 + 128, :], mt)

        ph5()

        def ph6():
            A.reset()
            wqb = A.bf16([128, 8, 1024]); wkb = A.bf16([128, 8, 1024])
            wvb = A.bf16([128, 8, 1024]); wgb = A.bf16([128, 8, 1024])
            stage = A.f32([128, 8192])
            lw_b = A.f32([128, 1024]); lb_b = A.f32([128, 1024])
            brow = A.f32([1, 1024])
            xt = [A.f32([128, 1024]) for _ in range(2)]
            xc = [A.f32([128, 1024]) for _ in range(2)]
            sq = [A.f32([128, 1024]) for _ in range(2)]
            u = [A.f32([128, 1024]) for _ in range(2)]
            uT = [A.bf16([128, 8, 128]) for _ in range(2)]
            ob = [A.f32([128, 1024]) for _ in range(2)]
            emit_load_w(sp, wqb, wq2, D, 1024, stage, 8192)
            emit_load_w(sp, wkb, wk2, D, 1024, stage, 8192)
            emit_load_w(sp, wvb, wv2, D, 1024, stage, 8192)
            emit_load_w(sp, wgb, wg2, D, 1024, stage, 8192)
            for row, dst in ((ln1w, lw_b), (ln1b, lb_b)):
                def d(row=row):
                    return lambda e: e.dma_start(brow[0:1, :], row[None, :])
                sp.op("sync", d(row), inc=16)
                emit_bcast(sp, dst, brow, ones1, psA, 1024)
            sp.ctx_open(2)
            for i in range(NT):
                r0 = 128 * i
                j = i % 2
                qm = "sync"
                qst = "sync"

                def dxt6(j=j, r0=r0):
                    return lambda e: e.dma_start(xt[j], xn2_s[r0:r0 + 128, :])
                sp.op(qm, dxt6(), inc=16, ctx=j)
                emit_ln(sp, xt[j], xc[j], sq[j], stat, D, c0=16 * j, ctx=j)

                def u1(j=j):
                    return lambda e: e.tensor_scalar(u[j], xc[j], stat[:, 16 * j + 3:16 * j + 4], None, ALU.mult)
                sp.op("vector", u1(), ctx=j)

                def u2(j=j):
                    return lambda e: e.tensor_mul(u[j], u[j], lw_b)
                sp.op("vector", u2(), ctx=j)

                def u3(j=j):
                    return lambda e: e.tensor_add(u[j], u[j], lb_b)
                sp.op("vector", u3(), ctx=j)
                emit_transposes(sp, uT[j], u[j], idn_sb, psA[:, 512 * j:512 * j + 128], 8, ctx=j)
                for wbf, dst, act, scale in ((wqb, q2_s, AF.Copy, 0.125), (wkb, k2_s, AF.Copy, 1.0), (wvb, v2_s, AF.Copy, 1.0), (wgb, gate2_s, AF.Silu, 1.0)):
                    emit_proj_tm(sp, psB[:, 1024 * j:1024 * j + 1024], uT[j], wbf, 8, 1024, ctx=j)

                    def ev(act=act, scale=scale, j=j):
                        return lambda e: e.activation(ob[j], psB[:, 1024 * j:1024 * j + 1024], act, scale=scale)
                    sp.op("scalar", ev(), ctx=j)

                    def dd(dst=dst, r0=r0, j=j):
                        return lambda e: e.dma_start(dst[r0:r0 + 128, :], ob[j])
                    sp.op(qst, dd(), inc=16, ctx=j)
            sp.ctx_close()

        ph6()

        def ph7a():
            A.reset()
            NC7 = 8
            S16 = A.f32([64, 16, 64])
            gq_sb = A.f32([128, 16]); gk2_sb = A.f32([128, 16]); gkh_sb = A.f32([128, 16])
            gs_b = A.f32([64, 16])
            rg_row = A.f32([1, 16])
            qtm = [[A.f32([128, 64]) for _ in range(2)] for _ in range(NC7)]
            ktm7 = [[A.f32([128, 64]) for _ in range(2)] for _ in range(NC7)]
            vtm7 = [[A.f32([128, 64]) for _ in range(2)] for _ in range(NC7)]
            qstm = [A.f32([128, 64]) for _ in range(NC7)]
            k2tm = [A.f32([128, 64]) for _ in range(NC7)]
            khat = [A.bf16([128, 64]) for _ in range(NC7)]
            v_bf7 = [A.bf16([128, 64]) for _ in range(NC7)]
            qf_bf = [A.bf16([64, 128]) for _ in range(NC7)]
            kf_bf = [A.bf16([64, 128]) for _ in range(NC7)]
            at7 = [A.bf16([128, 128]) for _ in range(NC7)]
            Sb7 = [A.bf16([64, 64]) for _ in range(NC7)]
            osb7 = [A.f32([128, 64]) for _ in range(NC7)]
            sp.dma(gq_sb, rgq)
            sp.dma(gk2_sb, rgk2)
            sp.dma(gkh_sb, rgkh)
            sp.dma(rg_row[0:1, :], rgs)
            sp.op("tensor", lambda e: e.matmul(psA[0:64, 0:16], ones1[0:1, 0:64], rg_row[0:1, :], start=True, stop=True))
            sp.op("vector", lambda e: e.tensor_copy(gs_b, psA[0:64, 0:16]))
            sp.op("vector", lambda e: e.memset(S16, 0.0))
            def ld7(j, t):
                c_, hi_ = t // 2, t % 2
                h_ = 2 * j + hi_
                r0_ = 128 * c_
                qm = "sync"
                b = t % 2

                def dq7(h=h_, r0=r0_, j=j, b=b):
                    return lambda e: e.dma_start(qtm[j][b], q2_s[r0:r0 + 128, 64 * h:64 * (h + 1)])
                sp.op(qm, dq7(), inc=16, ctx=j)

                def dk7(h=h_, r0=r0_, j=j, b=b):
                    return lambda e: e.dma_start(ktm7[j][b], k2_s[r0:r0 + 128, 64 * h:64 * (h + 1)])
                sp.op(qm, dk7(), inc=16, ctx=j)

                def dv7(h=h_, r0=r0_, j=j, b=b):
                    return lambda e: e.dma_start(vtm7[j][b], v2_s[r0:r0 + 128, 64 * h:64 * (h + 1)])
                sp.op(qm, dv7(), inc=16, ctx=j)

            sp.ctx_open(NC7)
            for c in range(NT):
                r0 = 128 * c
                for j in range(NC7):
                    qs = "sync"
                    psm = (psA if j < 4 else psB)[:, 512 * (j % 4):512 * (j % 4) + 512]
                    for h in (2 * j, 2 * j + 1):
                        t = 2 * c + (h - 2 * j)
                        if t == 0:
                            ld7(j, 0)
                        if t < 2 * NT - 1:
                            ld7(j, t + 1)
                        b = t % 2

                        def sq_(h=h, j=j, b=b):
                            return lambda e: e.tensor_scalar(qstm[j], qtm[j][b], gq_sb[:, h:h + 1], None, ALU.mult)
                        sp.op("vector", sq_(), ctx=j)

                        def sk2(h=h, j=j, b=b):
                            return lambda e: e.tensor_scalar(k2tm[j], ktm7[j][b], gk2_sb[:, h:h + 1], None, ALU.mult)
                        sp.op("vector", sk2(), ctx=j)

                        def skh(h=h, j=j, b=b):
                            return lambda e: e.tensor_scalar(khat[j], ktm7[j][b], gkh_sb[:, h:h + 1], None, ALU.mult)
                        sp.op("vector", skh(), ctx=j)

                        def vcp(j=j, b=b):
                            return lambda e: e.tensor_copy(v_bf7[j], vtm7[j][b])
                        sp.op("vector", vcp(), ctx=j)

                        def sbc(h=h, j=j):
                            return lambda e: e.tensor_copy(Sb7[j], S16[:, h, :])
                        sp.op("vector", sbc(), ctx=j)

                        def trq7(j=j, psm=psm):
                            return lambda e: e.transpose(psm[0:64, 0:128], qstm[j], idn_sb)
                        sp.op("tensor", trq7(), ctx=j)

                        def cpq7(j=j, psm=psm):
                            return lambda e: e.tensor_copy(qf_bf[j], psm[0:64, 0:128])
                        sp.op("vector", cpq7(), ctx=j)

                        def trk7(j=j, psm=psm):
                            return lambda e: e.transpose(psm[0:64, 128:256], k2tm[j], idn_sb)
                        sp.op("tensor", trk7(), ctx=j)

                        def cpk7(j=j, psm=psm):
                            return lambda e: e.tensor_copy(kf_bf[j], psm[0:64, 128:256])
                        sp.op("vector", cpk7(), ctx=j)

                        def mkq(j=j, psm=psm):
                            return lambda e: e.matmul(psm[:, 256:384], kf_bf[j], qf_bf[j], start=True, stop=True)
                        sp.op("tensor", mkq(), ctx=j)

                        def mat7(j=j, psm=psm):
                            return lambda e: e.tensor_tensor(at7[j], psm[:, 256:384], triu_sb, ALU.mult)
                        sp.op("vector", mat7(), ctx=j)

                        def mo1(j=j, psm=psm):
                            return lambda e: e.matmul(psm[:, 384:448], at7[j], v_bf7[j], start=True, stop=False)
                        sp.op("tensor", mo1(), ctx=j)

                        def mo2(j=j, psm=psm):
                            return lambda e: e.matmul(psm[:, 384:448], qf_bf[j], Sb7[j], start=False, stop=True)
                        sp.op("tensor", mo2(), ctx=j)

                        def dqs(h=h, r0=r0, j=j):
                            return lambda e: e.dma_start(q2f_s[64 * h:64 * (h + 1), r0:r0 + 128], qf_bf[j])
                        sp.op(qs, dqs(), inc=16, ctx=j)

                        def ocp(j=j, psm=psm):
                            return lambda e: e.tensor_copy(osb7[j], psm[:, 384:448])
                        sp.op("vector", ocp(), ctx=j)

                        def mkh(j=j, psm=psm):
                            return lambda e: e.matmul(psm[0:64, 448:512], khat[j], v_bf7[j], start=True, stop=True)
                        sp.op("tensor", mkh(), ctx=j)

                        def upd(h=h, j=j, psm=psm):
                            return lambda e: e.scalar_tensor_tensor(S16[:, h, :], S16[:, h, :], gs_b[:, h:h + 1], psm[0:64, 448:512], ALU.mult, ALU.add)
                        sp.op("vector", upd(), ctx=j)

                        def dos(h=h, r0=r0, j=j):
                            return lambda e: e.dma_start(o2raw_s[r0:r0 + 128, 64 * h:64 * (h + 1)], osb7[j])
                        sp.op(qs, dos(), inc=16, ctx=j)
            for j in range(NC7):
                qs = "sync"
                for h in (2 * j, 2 * j + 1):
                    def dst_(h=h):
                        return lambda e: e.dma_start(cc_rs_in[64 * h:64 * (h + 1), :], S16[:, h, :])
                    sp.op(qs, dst_(), inc=16, ctx=j)
            sp.ctx_close()

            sp.op("gpsimd", lambda e: e.collective_compute(
                "AllGather", ALU.bypass, replica_groups=RG,
                ins=[cc_rs_in.opt()], outs=[cc_rs_out.opt()]), inc=1)

        ph7a()

        def ph7c():
            NC7 = 8
            ccoef_sb = A.f32([64, 64])
            gcc_sb = A.f32([64, 256])
            Sin7 = [A.f32([64, 2, 64]) for _ in range(NC7)]
            Sic = [A.bf16([64, 64]) for _ in range(NC7)]
            qld7 = [[A.bf16([64, 128]) for _ in range(2)] for _ in range(NC7)]
            ort = [[A.f32([128, 64]) for _ in range(2)] for _ in range(NC7)]
            osb7c = [A.f32([128, 64]) for _ in range(NC7)]
            sq7 = [A.f32([128, 64]) for _ in range(NC7)]
            sp.dma(ccoef_sb, rccoef)
            sp.dma(gcc_sb, rgcc)
            def ldc(j, t):
                c_, hi_ = t // 2, t % 2
                h_ = 2 * j + hi_
                r0_ = 128 * c_
                qm = "sync"
                b = t % 2

                def dq2(h=h_, r0=r0_, j=j, b=b):
                    return lambda e: e.dma_start(qld7[j][b], q2f_s[64 * h:64 * (h + 1), r0:r0 + 128])
                sp.op(qm, dq2(), inc=16, ctx=j)

                def dor2(h=h_, r0=r0_, j=j, b=b):
                    return lambda e: e.dma_start(ort[j][b], o2raw_s[r0:r0 + 128, 64 * h:64 * (h + 1)])
                sp.op(qm, dor2(), inc=16, ctx=j)

            sp.ctx_open(NC7)
            for j in range(NC7):
                qm = "sync"
                for hi in range(2):
                    h = 2 * j + hi

                    def zs(j=j, hi=hi):
                        return lambda e: e.memset(Sin7[j][:, hi, :], 0.0)
                    sp.op("vector", zs(), ctx=j)
                    for l in range(4):
                        def dsj(l=l, h=h, j=j):
                            return lambda e: e.dma_start(ort[j][0][0:64, :], cc_rs_out[1024 * l + 64 * h:1024 * l + 64 * (h + 1), :])
                        sp.op(qm, dsj(), inc=16, ctx=j)

                        def acs(l=l, h=h, j=j, hi=hi):
                            return lambda e: e.scalar_tensor_tensor(Sin7[j][:, hi, :], ort[j][0][0:64, :], ccoef_sb[:, 4 * h + l:4 * h + l + 1], Sin7[j][:, hi, :], ALU.mult, ALU.add)
                        sp.op("vector", acs(), ctx=j)
            for c in range(NT):
                r0 = 128 * c
                for j in range(NC7):
                    qs = "sync"
                    psm = (psA if j < 4 else psB)[:, 512 * (j % 4):512 * (j % 4) + 512]
                    st = stat[:, 8 * j:8 * j + 8]
                    for hi in range(2):
                        h = 2 * j + hi
                        t = 2 * c + hi
                        if t == 0:
                            ldc(j, 0)
                        if t < 2 * NT - 1:
                            ldc(j, t + 1)
                        b = t % 2

                        def sc7(h=h, c=c, j=j, hi=hi):
                            return lambda e: e.tensor_scalar(Sic[j], Sin7[j][:, hi, :], gcc_sb[:, 16 * h + c:16 * h + c + 1], None, ALU.mult)
                        sp.op("vector", sc7(), ctx=j)

                        def mmc7(j=j, psm=psm, b=b):
                            return lambda e: e.matmul(psm[:, 0:64], qld7[j][b], Sic[j], start=True, stop=True)
                        sp.op("tensor", mmc7(), ctx=j)

                        def add7(j=j, psm=psm, b=b):
                            return lambda e: e.tensor_add(osb7c[j], ort[j][b], psm[:, 0:64])
                        sp.op("vector", add7(), ctx=j)

                        def gn1(j=j, st=st):
                            return lambda e: e.reduce_sum(st[:, 0:1], osb7c[j], axis=AX.X)
                        sp.op("vector", gn1(), ctx=j)

                        def gn2(j=j, st=st):
                            return lambda e: e.tensor_scalar_mul(st[:, 1:2], st[:, 0:1], 1.0 / 64.0)
                        sp.op("vector", gn2(), ctx=j)

                        def gn3(j=j, st=st):
                            return lambda e: e.tensor_scalar(osb7c[j], osb7c[j], st[:, 1:2], None, ALU.subtract)
                        sp.op("vector", gn3(), ctx=j)

                        def gn4(j=j, st=st):
                            return lambda e: e.activation(sq7[j], osb7c[j], AF.Square, accum_out=st[:, 2:3])
                        sp.op("scalar", gn4(), ctx=j)

                        def gn5(j=j, st=st):
                            return lambda e: e.tensor_scalar(st[:, 3:4], st[:, 2:3], 1.0 / 64.0, 1e-6, ALU.mult, ALU.add)
                        sp.op("vector", gn5(), ctx=j)

                        def gn6(j=j, st=st):
                            return lambda e: e.activation(st[:, 3:4], st[:, 3:4], AF.Sqrt)
                        sp.op("scalar", gn6(), ctx=j)

                        def gn7(j=j, st=st):
                            return lambda e: e.reciprocal(st[:, 3:4], st[:, 3:4])
                        sp.op("vector", gn7(), ctx=j)

                        def gn8(j=j, st=st):
                            return lambda e: e.tensor_scalar(osb7c[j], osb7c[j], st[:, 3:4], None, ALU.mult)
                        sp.op("vector", gn8(), ctx=j)

                        def drt(h=h, r0=r0, j=j):
                            return lambda e: e.dma_start(ret_s[r0:r0 + 128, 64 * h:64 * (h + 1)], osb7c[j])
                        sp.op(qs, drt(), inc=16, ctx=j)
            sp.ctx_close()

        ph7c()

        def ph8():
            A.reset()
            wo2_bf = A.bf16([128, 8, 1024])
            wout_bf = A.bf16([128, 22, 1024])
            wcb_a = A.bf16([128, 8, 128]); wcb_b = A.bf16([128, 8, 128])
            stage = A.f32([128, 4096])
            binaf_sb = A.f32([128, 22]); binbf_sb = A.f32([128, 22])
            lw_b = A.f32([128, 1024]); lb_b = A.f32([128, 1024]); bo_b = A.f32([128, 1024])
            brow = A.f32([1, 1024])
            rt = A.f32([128, 1024]); g2t = A.f32([128, 1024]); mo32 = A.f32([128, 1024])
            moT = A.bf16([128, 8, 128])
            t4 = A.f32([128, 4, 1024])
            u = A.f32([128, 1024]); uT = A.bf16([128, 8, 512])
            att = A.f32([128, 512]); sbt = A.f32([128, 512])
            swfm = A.bf16([128, 22, 512])
            xc = A.f32([128, 1024]); sq = A.f32([128, 1024])
            ysb = A.f32([128, 1024]); xt = A.f32([128, 1024])
            winar = wina.rearrange("(c p) n -> p c n", p=128)
            winbr = winb.rearrange("(c p) n -> p c n", p=128)
            emit_load_w(sp, wo2_bf, wo2, D, 1024, stage, 4096)
            emit_load_w(sp, wout_bf, wout, 2816, 1024, stage, 4096)
            sp.dma(binaf_sb, binaf)
            sp.dma(binbf_sb, binbf)
            for row, dst in ((ln2w, lw_b), (ln2b, lb_b), (bout, bo_b)):
                def d(row=row):
                    return lambda e: e.dma_start(brow[0:1, :], row[None, :])
                sp.op("sync", d(row), inc=16)
                emit_bcast(sp, dst, brow, ones1, psA, 1024)
            for qb in range(4):
                q0 = 512 * qb
                for qt in range(4):
                    r0 = q0 + 128 * qt
                    sp.dma(rt, ret_s[r0:r0 + 128, :])
                    sp.dma(g2t, gate2_s[r0:r0 + 128, :])
                    sp.op("vector", lambda e: e.tensor_mul(mo32, rt, g2t))
                    emit_transposes(sp, moT, mo32, idn_sb, psA, 8)
                    emit_proj_tm(sp, psB, moT, wo2_bf, 8, 1024)
                    sp.dma(xt, xn2_s[r0:r0 + 128, :])

                    def ti(qt=qt):
                        return lambda e: e.tensor_add(t4[:, qt, :], psB[:, 0:1024], xt)
                    sp.op("vector", ti(qt))
                    emit_ln(sp, t4[:, qt, :], xc, sq, stat, D)
                    sp.op("vector", lambda e: e.tensor_scalar(u, xc, stat[:, 3:4], None, ALU.mult))
                    sp.op("vector", lambda e: e.tensor_mul(u, u, lw_b))
                    sp.op("vector", lambda e: e.tensor_add(u, u, lb_b))
                    for c in range(8):
                        def tr(c=c):
                            return lambda e: e.transpose(psA[:, 0:128], u[:, 128 * c:128 * (c + 1)], idn_sb)
                        sp.op("tensor", tr(c))

                        def cp(c=c, qt=qt):
                            return lambda e: e.tensor_copy(uT[:, c, 128 * qt:128 * (qt + 1)], psA[:, 0:128])
                        sp.op("vector", cp(c, qt))
                for fc in range(22):
                    def da(fc=fc):
                        return lambda e: e.dma_start(stage[:, 0:1024], winar[:, :, 128 * fc:128 * (fc + 1)])
                    sp.op("sync", da(fc), inc=16)
                    sp.op("vector", lambda e: e.tensor_copy(wcb_a.rearrange("p c n -> p (c n)"), stage[:, 0:1024]))

                    def db(fc=fc):
                        return lambda e: e.dma_start(stage[:, 0:1024], winbr[:, :, 128 * fc:128 * (fc + 1)])
                    sp.op("sync", db(fc), inc=16)
                    sp.op("vector", lambda e: e.tensor_copy(wcb_b.rearrange("p c n -> p (c n)"), stage[:, 0:1024]))
                    for k in range(8):
                        def ma(k=k):
                            return lambda e: e.matmul(psA[:, 0:512], wcb_a[:, k, :], uT[:, k, :], start=(k == 0), stop=(k == 7))
                        sp.op("tensor", ma(k))

                        def mb_(k=k):
                            return lambda e: e.matmul(psA[:, 512:1024], wcb_b[:, k, :], uT[:, k, :], start=(k == 0), stop=(k == 7))
                        sp.op("tensor", mb_(k))

                    def eva(fc=fc):
                        return lambda e: e.activation(att, psA[:, 0:512], AF.Identity, bias=binaf_sb[:, fc:fc + 1])
                    sp.op("scalar", eva(fc))

                    def evb(fc=fc):
                        return lambda e: e.activation(sbt, psA[:, 512:1024], AF.Silu, bias=binbf_sb[:, fc:fc + 1])
                    sp.op("scalar", evb(fc))

                    def mulab(fc=fc):
                        return lambda e: e.tensor_mul(swfm[:, fc, :], att, sbt)
                    sp.op("vector", mulab(fc))
                for qt in range(4):
                    ps = psA if qt < 2 else psB
                    off = 1024 * (qt % 2)
                    for fc in range(22):
                        for s in range(2):
                            def my(fc=fc, s=s, qt=qt, ps=ps, off=off):
                                return lambda e: e.matmul(ps[:, off + 512 * s:off + 512 * (s + 1)], swfm[:, fc, 128 * qt:128 * (qt + 1)], wout_bf[:, fc, 512 * s:512 * (s + 1)], start=(fc == 0), stop=(fc == 21))
                            sp.op("tensor", my(fc, s, qt, ps, off))
                    r0 = q0 + 128 * qt

                    def ev1(ps=ps, off=off):
                        return lambda e: e.tensor_add(ysb, ps[:, off:off + 1024], bo_b)
                    sp.op("vector", ev1(ps, off))

                    def ev2(qt=qt):
                        return lambda e: e.tensor_add(ysb, ysb, t4[:, qt, :])
                    sp.op("vector", ev2(qt))
                    sp.dma(xt, x2_s[r0:r0 + 128, :])
                    sp.op("vector", lambda e: e.tensor_add(ysb, ysb, xt))
                    sp.dma(x3_s[r0:r0 + 128, :], ysb)
                    emit_ln(sp, ysb, xc, sq, stat, D)
                    sp.op("vector", lambda e: e.tensor_scalar(xt, xc, stat[:, 3:4], None, ALU.mult))
                    sp.dma(xn3_s[r0:r0 + 128, :], xt)

        ph8()

        def ph9():
            A.reset()
            xlb = [A.bf16([128, 8, 512]), A.bf16([128, 8, 512])]
            w1_bf = A.bf16([128, 8, 4096])
            w2_bf = A.bf16([128, 32, 1024])
            h_bf = A.bf16([128, 32, 512])
            stage = A.f32([128, 1024])
            stage2 = A.f32([128, 1024])
            wr_bf = A.bf16([128, 8, 8])
            br_b = A.f32([128, 8])
            lg_b = A.bf16([128, 1024]); lnb_b = A.bf16([128, 1024])
            b1f = A.f32([128, 32]); b2bb = A.bf16([128, 1024])
            g8 = A.f32([128, 16, 8]); gates = A.f32([128, 16])
            pr = A.f32([128, 8]); e8 = A.f32([128, 8])
            ysb = A.f32([128, 1024])
            hflat = h_bf.rearrange("p a b -> p (a b)")
            hv = hflat.bitcast(F32)
            xt = hv[:, 0:1024]; xc = hv[:, 1024:2048]; sq = hv[:, 2048:3072]
            u = sq
            xnT = hflat[:, 10240:11264].rearrange("p (c t) -> p c t", t=128)
            x0flat = xlb[0].rearrange("p a b -> p (a b)")
            x0v = x0flat.bitcast(F32)
            x1flat = xlb[1].rearrange("p a b -> p (a b)")
            x1v = x1flat.bitcast(F32)
            xt2 = [xt, x0v[:, 0:1024]]
            xc2 = [xc, x0v[:, 1024:2048]]
            sq2 = [sq, x1v[:, 0:1024]]
            xnT2 = [xnT, x1flat[:, 2048:3072].rearrange("p (c t) -> p c t", t=128)]
            pr2 = [pr, A.f32([128, 8])]
            e82 = [e8, A.f32([128, 8])]
            brow = stage
            zt = ysb
            xl_r = xl_s.rearrange("(c p) t -> p c t", p=128)
            w1r = w1b.rearrange("e (c p) n -> e p c n", p=128)
            w2r = w2b.rearrange("e (c p) n -> e p c n", p=128)
            emit_load_w(sp, wr_bf, wr9, D, 8, stage, 1024)
            sp.dma(brow[0:1, 0:8], br9[None, :])
            emit_bcast(sp, br_b, brow, ones1, psA, 8)
            for row, dst in ((lng, lg_b), (lnb9, lnb_b)):
                def d(row=row):
                    return lambda e: e.dma_start(brow[0:1, :], row[None, :])
                sp.op("sync", d(row), inc=16)
                emit_bcast(sp, dst, brow, ones1, psA, 1024)
            sp.op("vector", lambda e: e.memset(zt, 0.0))
            for i in range(NT):
                def dz(i=i):
                    return lambda e: e.dma_start(accd[128 * i:128 * (i + 1), :], zt)
                sp.op("sync", dz(i), inc=16)
            sp.ctx_open(2)
            for i in range(NT):
                r0 = 128 * i
                j = i % 2
                qm = "sync"
                qs = "sync"
                psa_j = psA[:, 512 * j:512 * j + 128]
                st = stat[:, 16 * j:16 * j + 8]

                def dxt(i=i, j=j, r0=r0):
                    return lambda e: e.dma_start(xt2[j], xn3_s[r0:r0 + 128, :])
                sp.op(qm, dxt(), inc=16, ctx=j)
                for c in range(8):
                    def tr0(c=c, j=j, psa_j=psa_j):
                        return lambda e: e.transpose(psa_j, xt2[j][:, 128 * c:128 * (c + 1)], idn_sb)
                    sp.op("tensor", tr0(), ctx=j)

                    def cp0(c=c, j=j, psa_j=psa_j):
                        return lambda e: e.tensor_copy(xnT2[j][:, c, :], psa_j)
                    sp.op("vector", cp0(), ctx=j)
                for c in range(8):
                    def mr(c=c, j=j):
                        return lambda e: e.matmul(psB[:, 512 * j:512 * j + 8], xnT2[j][:, c, :], wr_bf[:, c, :], start=(c == 0), stop=(c == 7))
                    sp.op("tensor", mr(), ctx=j)

                def v1(j=j):
                    return lambda e: e.tensor_add(pr2[j], psB[:, 512 * j:512 * j + 8], br_b)
                sp.op("vector", v1(), ctx=j)

                def v2(j=j, st=st):
                    return lambda e: e.reduce_max(st[:, 0:1], pr2[j], axis=AX.X, negate=True)
                sp.op("vector", v2(), ctx=j)

                def s1(j=j, st=st):
                    return lambda e: e.activation(e82[j], pr2[j], AF.Exp, bias=st[:, 0:1], accum_out=st[:, 1:2])
                sp.op("scalar", s1(), ctx=j)

                def v3(j=j, st=st):
                    return lambda e: e.reciprocal(st[:, 2:3], st[:, 1:2])
                sp.op("vector", v3(), ctx=j)

                def v4(j=j, st=st):
                    return lambda e: e.tensor_scalar(pr2[j], e82[j], st[:, 2:3], None, ALU.mult)
                sp.op("vector", v4(), ctx=j)

                def gmax(i=i, j=j):
                    return lambda e: e.reduce_max(gates[:, i:i + 1], pr2[j], axis=AX.X)
                sp.op("vector", gmax(), ctx=j)

                def gmask(i=i, j=j):
                    return lambda e: e.tensor_scalar(g8[:, i, :], pr2[j], gates[:, i:i + 1], None, ALU.is_ge)
                sp.op("vector", gmask(), ctx=j)
                emit_ln(sp, xt2[j], xc2[j], sq2[j], stat, D, c0=16 * j + 8, ctx=j)

                def v5(j=j):
                    return lambda e: e.tensor_scalar(sq2[j], xc2[j], stat[:, 16 * j + 11:16 * j + 12], None, ALU.mult)
                sp.op("vector", v5(), ctx=j)

                def v6(j=j):
                    return lambda e: e.tensor_mul(sq2[j], sq2[j], lg_b)
                sp.op("vector", v6(), ctx=j)

                def v7(j=j):
                    return lambda e: e.tensor_add(sq2[j], sq2[j], lnb_b)
                sp.op("vector", v7(), ctx=j)
                for c in range(8):
                    def tr1(c=c, j=j, psa_j=psa_j):
                        return lambda e: e.transpose(psa_j, sq2[j][:, 128 * c:128 * (c + 1)], idn_sb)
                    sp.op("tensor", tr1(), ctx=j)

                    def cp1(c=c, j=j, psa_j=psa_j):
                        return lambda e: e.tensor_copy(xnT2[j][:, c, :], psa_j)
                    sp.op("vector", cp1(), ctx=j)

                def stx(i=i, j=j):
                    return lambda e: e.dma_start(xl_r[:, :, 128 * i:128 * (i + 1)], xnT2[j][:])
                sp.op(qs, stx(), inc=16, ctx=j)
            sp.ctx_close()
            e_prolog = list(sp.now().items())
            e_xl_store = ("sync", sp.cnt["sync"])
            e_hlast = ("tensor", 0)
            e_ylast = ("tensor", 0)
            e_rel_last = ("scalar", 0)
            e_b2mm = ("tensor", sp.cnt["tensor"])
            e_evac = {qt: ("vector", 0) for qt in range(4)}
            e_stg = [("vector", 0), ("vector", 0)]
            e_b2cp = ("vector", sp.cnt["vector"])
            e_acdma = ("gpsimd", sp.cnt["gpsimd"])
            ge = {}
            e_xlb_war = [("tensor", 0), ("tensor", 0)]
            e_xlb_load = [("sync", 0), ("sync", 0)]
            for ex in range(8):
                pieces = []
                for c in range(8):
                    for hf in range(4):
                        pieces.append(("w1", c, hf))
                for c in range(32):
                    pieces.append(("w2", c, 0))
                w1c = 0; w2c = 0
                for pi, (kind, c, hf) in enumerate(pieces):
                    par = pi % 2
                    stg = stage if par == 0 else stage2
                    if kind == "w1":
                        def dd(ex=ex, c=c, hf=hf, stg=stg):
                            return lambda e: e.dma_start(stg, w1r[ex, :, c, 1024 * hf:1024 * (hf + 1)])
                        war = e_hlast
                    else:
                        def dd(ex=ex, c=c, stg=stg):
                            return lambda e: e.dma_start(stg, w2r[ex, :, c, :])
                        war = e_ylast
                    e_d = sp.op("sync", dd(), inc=16, deps=[e_stg[par]])
                    if kind == "w1":
                        def cc(c=c, hf=hf, stg=stg):
                            return lambda e: e.tensor_copy(w1_bf[:, c, 1024 * hf:1024 * (hf + 1)], stg)
                    else:
                        def cc(c=c, stg=stg):
                            return lambda e: e.tensor_copy(w2_bf[:, c, :], stg)
                    e_c = sp.op("vector", cc(), deps=[e_d, war])
                    e_stg[par] = e_c
                    if kind == "w1":
                        w1c = e_c[1]
                    else:
                        w2c = e_c[1]
                e_w1cast = ("vector", w1c)
                e_w2cast = ("vector", w2c)
                e_b1 = sp.op("sync", (lambda ex=ex: (lambda e: e.dma_start(b1f, b1fmb[ex])))(), inc=16, deps=[e_rel_last])
                e_b2r = sp.op("sync", (lambda ex=ex: (lambda e: e.dma_start(brow[0:1, :], b2b9[ex:ex + 1, :])))(), inc=16, deps=[e_b2mm, e_stg[0]])
                e_b2mm = sp.op("tensor", lambda e: e.matmul(psA[:, 1536:2048], ones1, brow[0:1, 0:512], start=True, stop=True), deps=[e_b2r, e_evac[2], e_evac[3], e_b2cp])
                e_b2mm2 = sp.op("tensor", lambda e: e.matmul(psB[:, 1536:2048], ones1, brow[0:1, 512:1024], start=True, stop=True), deps=[e_b2r, e_evac[1], e_b2cp])
                e_c1 = sp.op("vector", lambda e: e.tensor_copy(b2bb[:, 0:512], psA[:, 1536:2048]), deps=[e_b2mm, e_evac[0]])
                e_b2cp = sp.op("vector", lambda e: e.tensor_copy(b2bb[:, 512:1024], psB[:, 1536:2048]), deps=[e_b2mm2])
                e_stg[0] = e_b2mm2
                for qb in range(4):
                    par = qb % 2

                    def ldx(ex=ex, qb=qb, par=par):
                        return lambda e: e.dma_start(xlb[par][:], xl_r[:, :, 512 * qb:512 * (qb + 1)])
                    e_xlb_load[par] = sp.op("sync", ldx(), inc=16, deps=[e_xlb_war[par]] + e_prolog)
                    e_y_prev_qb = e_ylast
                    for fc in range(32):
                        bank = psA[:, 512 * (fc % 2):512 * (fc % 2 + 1)]
                        if fc >= 2:
                            bank_war = [ge.get(fc - 2, ("scalar", 0))]
                        else:
                            bank_war = [ge.get(fc + 30, ("scalar", 0)), e_evac[2], e_evac[3]]
                        dl = [e_w1cast, e_xlb_load[par], e_b2cp] + bank_war
                        for k in range(8):
                            def mh(fc=fc, k=k, qb=qb, bank=bank, par=par):
                                return lambda e: e.matmul(bank, w1_bf[:, k, 128 * fc:128 * (fc + 1)], xlb[par][:, k, :], start=(k == 0), stop=(k == 7))
                            e_hlast = sp.op("tensor", mh(), deps=dl)

                        def rel(fc=fc, bank=bank):
                            return lambda e: e.activation(h_bf[:, fc, :], bank, AF.Relu, bias=b1f[:, fc:fc + 1])
                        ge[fc] = sp.op("scalar", rel(), deps=[e_hlast, e_y_prev_qb, e_b1])
                    e_xlb_war[par] = e_hlast
                    e_rel_last = ge[31]
                    for qt in range(4):
                        ps = psB if qt < 2 else psA
                        off = 1024 * (qt % 2)
                        dl = [e_w2cast, e_evac[qt], ge[31], e_b2cp]
                        for fc in range(32):
                            dl2 = dl + [ge[fc]]
                            for s in range(2):
                                def my(fc=fc, s=s, qt=qt, ps=ps, off=off):
                                    return lambda e: e.matmul(ps[:, off + 512 * s:off + 512 * (s + 1)], h_bf[:, fc, 128 * qt:128 * (qt + 1)], w2_bf[:, fc, 512 * s:512 * (s + 1)], start=(fc == 0), stop=(fc == 31))
                                e_ylast = sp.op("tensor", my(), deps=dl2)

                        def ev(ps=ps, off=off):
                            return lambda e: e.tensor_add(ysb, ps[:, off:off + 1024], b2bb)
                        e_ev = sp.op("vector", ev(), deps=[e_ylast, e_acdma])
                        qg = 4 * qb + qt

                        def mm_(ex=ex, qg=qg):
                            return lambda e: e.tensor_scalar(ysb, ysb, g8[:, qg, ex:ex + 1], None, ALU.mult)
                        e_mm = sp.op("vector", mm_(), deps=[])
                        e_evac[qt] = e_mm

                        def dacc(qg=qg):
                            return lambda e: e.dma_start(accd[128 * qg:128 * (qg + 1), :], ysb, accum_op=ALU.add)
                        e_acdma = sp.op("gpsimd", dacc(), inc=16, deps=[e_mm])
            for i in range(NT):
                r0 = 128 * i
                sp.dma(xt, accd[r0:r0 + 128, :])

                def fg(i=i):
                    return lambda e: e.tensor_scalar(xt, xt, gates[:, i:i + 1], None, ALU.mult)
                sp.op("vector", fg(i))
                sp.dma(xc, x3_s[r0:r0 + 128, :])
                sp.op("vector", lambda e: e.tensor_add(xt, xt, xc))
                sp.dma(ox4[r0:r0 + 128, :], xt)

        ph9()

        sp.emit(block)
    return nc


_NC_CACHE = {}


def _f32(a):
    return np.ascontiguousarray(np.asarray(a, dtype=np.float32))


def host_maps(inp):
    B = 2
    x = _f32(inp["L0_attn_x"]).reshape(B * T, D)
    idn = np.eye(128, dtype=np.float32)
    triu = np.triu(np.ones((128, 128), np.float32))
    i16 = np.eye(16, dtype=np.float32)

    qw = _f32(inp["L0_ffn_q_w"]); qb = _f32(inp["L0_ffn_q_b"])
    cols = np.concatenate([np.arange(64 * h, 64 * h + 32) for h in range(16)])
    qwe = np.ascontiguousarray(qw[:, cols]); qbe = np.ascontiguousarray(qb[cols])
    owf = np.ascontiguousarray(_f32(inp["L0_ffn_o_w"])[:512])

    b1a = _f32(inp["L0_ffn_b1"])
    b1fma = np.ascontiguousarray(b1a.reshape(8, 16, 128).transpose(0, 2, 1))
    b1b = _f32(inp["L1_ffn_b1"])
    b1fmb = np.ascontiguousarray(b1b.reshape(8, 32, 128).transpose(0, 2, 1))

    win = _f32(inp["L1_attn_w_in"]); bin_ = _f32(inp["L1_attn_b_in"])
    wina = np.zeros((D, 2816), np.float32); wina[:, :2730] = win[:, :2730]
    winb = np.zeros((D, 2816), np.float32); winb[:, :2730] = win[:, 2730:]
    binaf = np.zeros(2816, np.float32); binaf[:2730] = bin_[:2730]
    binbf = np.zeros(2816, np.float32); binbf[:2730] = bin_[2730:]
    binaf = np.ascontiguousarray(binaf.reshape(22, 128).T)
    binbf = np.ascontiguousarray(binbf.reshape(22, 128).T)
    woutp = np.zeros((2816, 1024), np.float32)
    woutp[:2730] = _f32(inp["L1_attn_w_out"])

    gamma = (1.0 - np.exp2(-5.0 - np.arange(16))).astype(np.float64)
    tt = np.arange(128, dtype=np.float64)
    rgq = np.stack([(gamma[h] ** (tt + 1)) for h in range(16)], 1).astype(np.float32)
    rgk2 = np.stack([(gamma[h] ** (-(tt + 1))) for h in range(16)], 1).astype(np.float32)
    rgkh = np.stack([(gamma[h] ** (127 - tt)) for h in range(16)], 1).astype(np.float32)
    rgs = (gamma ** 128).astype(np.float32)[None, :]
    rgcc = np.zeros((64, 256), np.float32)
    for h in range(16):
        for c in range(16):
            rgcc[:, 16 * h + c] = gamma[h] ** (128 * c)

    shared = dict(
        wq=_f32(inp["L0_attn_Wq"]), wk=_f32(inp["L0_attn_Wk"]),
        wv=_f32(inp["L0_attn_Wv"]), wg=_f32(inp["L0_attn_Wg"]),
        gkw1=_f32(inp["L0_attn_gk_w1"]), gkw2=_f32(inp["L0_attn_gk_w2"]),
        gkb=_f32(inp["L0_attn_gk_b"]), rmsw=_f32(inp["L0_attn_rms_w"]),
        wo=_f32(inp["L0_attn_Wo"]),
        qwe=qwe, qbe=qbe,
        kwf=_f32(inp["L0_ffn_k_w"]), kbf=_f32(inp["L0_ffn_k_b"]),
        vwf=_f32(inp["L0_ffn_v_w"]), vbf=_f32(inp["L0_ffn_v_b"]),
        owe=owf, obv=_f32(inp["L0_ffn_o_b"]),
        rw=_f32(inp["L0_ffn_router_w"]), rb=_f32(inp["L0_ffn_router_b"]),
        w1a=_f32(inp["L0_ffn_w1"]), b1fma=b1fma,
        w2a=_f32(inp["L0_ffn_w2"]), b2a=_f32(inp["L0_ffn_b2"]),
        ln1w=_f32(inp["L1_attn_ln1_w"]), ln1b=_f32(inp["L1_attn_ln1_b"]),
        wq2=_f32(inp["L1_attn_wq"]), wk2=_f32(inp["L1_attn_wk"]),
        wv2=_f32(inp["L1_attn_wv"]), wg2=_f32(inp["L1_attn_wg"]),
        wo2=_f32(inp["L1_attn_wo"]),
        ln2w=_f32(inp["L1_attn_ln2_w"]), ln2b=_f32(inp["L1_attn_ln2_b"]),
        wina=wina, winb=winb, binaf=binaf, binbf=binbf,
        wout=woutp, bout=_f32(inp["L1_attn_b_out"]),
        wr9=_f32(inp["L1_ffn_Wr"]), br9=_f32(inp["L1_ffn_br"]),
        lng=_f32(inp["L1_ffn_ln_g"]), lnb9=_f32(inp["L1_ffn_ln_b"]),
        w1b=_f32(inp["L1_ffn_W1"]), b1fmb=b1fmb,
        w2b=_f32(inp["L1_ffn_W2"]), b2b9=_f32(inp["L1_ffn_b2"]),
        idn=idn, triu=triu, i16=i16,
        rgq=rgq, rgk2=rgk2, rgkh=rgkh, rgs=rgs, rgcc=rgcc,
    )
    maps = []
    for c in range(NCORE):
        r = c % 4
        msel = np.zeros((4, 4), np.float64)
        sel = np.zeros(4, np.float32)
        for j in range(4):
            if j < r:
                sel[j] = 1.0
                for l in range(j + 1, r):
                    msel[j, l] = 1.0
        gmsel = np.broadcast_to(msel.reshape(1, 16).astype(np.float32), (128, 16)).copy()
        gsel = np.broadcast_to(sel[None, :], (128, 4)).copy()
        rccoef = np.zeros((64, 64), np.float32)
        for h in range(16):
            for j in range(4):
                if j < r:
                    rccoef[:, 4 * h + j] = gamma[h] ** (2048.0 * (r - 1 - j))
        m = dict(shared)
        m["x"] = x[NTOK * c:NTOK * (c + 1)]
        m["gmsel"] = gmsel
        m["gsel"] = gsel
        m["rccoef"] = rccoef
        maps.append(m)
    return maps


def kernel(**inp):
    if "nc" not in _NC_CACHE:
        _NC_CACHE["nc"] = build_fused(debug=False)
    nc = _NC_CACHE["nc"]
    maps = host_maps(inp)
    res = run_bass_kernel_spmd(nc, maps, core_ids=list(range(NCORE)))
    x4 = np.concatenate([r["ox4"] for r in res.results], 0)
    return x4.reshape(2, T, D).astype(np.float32)
